# version 07
# baseline (speedup 1.0000x reference)
import math
import numpy as np
import ml_dtypes
import bass_rust
import concourse.bass as bass
import concourse.bacc as bacc
import concourse.mybir as mybir
from concourse.bass_utils import run_bass_kernel_spmd
from concourse import library_config
from contextlib import ExitStack

F32 = mybir.dt.float32
BF16 = mybir.dt.bfloat16
I32 = mybir.dt.int32
FP8 = mybir.dt.float8e4
AF = mybir.ActivationFunctionType
ALU = mybir.AluOpType
AX = mybir.AxisListType
ds = bass.ds
BF = ml_dtypes.bfloat16

NCORE = 8
B, S, D = 4, 2048, 2048
TPC = 1024
NT = 8
NDC = D // 128
HD = 128
NEG = -1e30
N_SUB = 8

ALIBI_BASE0 = 1920
ALIBI_W = 4352


def mkap(ap, offset, pattern):
    return bass_rust.AP(tensor=ap.tensor, offset=int(offset),
                        ap=[[int(a), int(b)] for a, b in pattern])


class Res:
    __slots__ = ("w", "r")

    def __init__(self):
        self.w = None
        self.r = []

    def for_read(self):
        return [self.w] if self.w else []

    def for_write(self):
        out = list(self.r)
        if self.w:
            out.append(self.w)
        return out

    def did_read(self, tok):
        if tok:
            self.r.append(tok)

    def did_write(self, tok):
        if tok:
            self.w = tok
            self.r = []


class Sched:
    ENGINES = ("sync", "tensor", "vector", "scalar", "gpsimd")

    def __init__(self, nc, es):
        self.nc = nc
        self.q = {e: [] for e in self.ENGINES}
        self.psem = {e: [es.enter_context(nc.semaphore(f"p_{e}0")),
                         es.enter_context(nc.semaphore(f"p_{e}1"))] for e in self.ENGINES}
        self.pcnt = {e: 0 for e in self.ENGINES}
        self.bsem = {e: es.enter_context(nc.semaphore(f"b_{e}")) for e in self.ENGINES}
        self.bcnt = 0
        self.dma_cnt = {}
        self.dma_issued = {e: {} for e in self.ENGINES}
        self.last_wait = {e: {} for e in self.ENGINES}

    def _waits(self, eng, deps):
        lw = self.last_wait[eng]
        best = {}
        for tok in deps:
            if tok is None:
                continue
            semh, val = tok
            k = id(semh)
            if val > best.get(k, (None, -1))[1]:
                best[k] = (semh, val)
        out = []
        for k, (semh, val) in best.items():
            if lw.get(k, -1) >= val:
                continue
            lw[k] = val
            out.append((semh, val))
        return out

    def op(self, eng, emit, deps=(), inc=True):
        waits = self._waits(eng, deps)
        tok = None
        if inc:
            k = self.pcnt[eng]
            self.pcnt[eng] += 1
            tok = (self.psem[eng][k % 2], k // 2 + 1)
        self.q[eng].append(("op", waits, emit, tok))
        return tok

    def mark(self, eng):
        return self.op(eng, lambda E: E.nop(), inc=True)

    def dma(self, eng, out, in_, sem, deps=()):
        waits = self._waits(eng, deps)
        k = id(sem)
        self.dma_cnt[k] = self.dma_cnt.get(k, 0) + 16
        tok = (sem, self.dma_cnt[k])
        self.dma_issued[eng][k] = tok
        self.q[eng].append(("dma", waits, (out, in_), tok))
        return tok

    def raw(self, eng, emit, deps=()):
        waits = self._waits(eng, deps)
        self.q[eng].append(("raw", waits, emit, None))

    def barrier(self):
        self.bcnt += 1
        k = self.bcnt
        for e in self.ENGINES:
            waits = []
            n = self.pcnt[e]
            if (n + 1) // 2 > 0:
                waits.append((self.psem[e][0], (n + 1) // 2))
            if n // 2 > 0:
                waits.append((self.psem[e][1], n // 2))
            waits.extend(self.dma_issued[e].values())
            semb = self.bsem[e]

            def emit(E, waits=tuple(waits), semb=semb):
                for s, v in waits:
                    E.wait_ge(s, v)
                E.nop().then_inc(semb, 1)

            self.raw(e, emit)
        return [(self.bsem[e], k) for e in self.ENGINES]

    def finalize(self, block):
        def mk(eng_name):
            def emit_stream(E):
                for kind, waits, payload, tok in self.q[eng_name]:
                    for semh, val in waits:
                        E.wait_ge(semh, val)
                    if kind == "op":
                        ins = payload(E)
                    elif kind == "dma":
                        out, in_ = payload
                        if callable(out):
                            out = out()
                        if callable(in_):
                            in_ = in_()
                        ins = E.dma_start(out=out, in_=in_)
                    else:
                        payload(E)
                        continue
                    if tok is not None:
                        ins.then_inc(tok[0], 16 if kind == "dma" else 1)
            return emit_stream

        for name in self.ENGINES:
            getattr(block, name)(mk(name))


def rope_tables():
    inv_freq = (1.0 / (10000.0 ** (np.arange(0, HD, 2, dtype=np.float32)
                                   / np.float32(HD)))).astype(np.float32)
    t = np.arange(S, dtype=np.float32)
    freqs = (t[:, None] * inv_freq[None, :]).astype(np.float32)
    emb = np.concatenate([freqs, freqs], axis=-1)
    cos = np.cos(emb.astype(np.float64)).astype(np.float32)
    sin = np.sin(emb.astype(np.float64)).astype(np.float32)
    return cos, sin


def alibi_core_table(h):
    slopes = (2.0 ** (-8.0 * np.arange(1, 17) / 16)).astype(np.float32)
    r = np.arange(30)[:, None, None]
    p = np.arange(128)[None, :, None]
    f = np.arange(128)[None, None, :]
    delta = ((r - 14) * 128 + f - p - h * 128).astype(np.float32)
    tbl = slopes[:, None, None, None] * delta[None]
    tbl = np.where(delta[None] <= 0, tbl, np.float32(NEG)).astype(np.float32)
    return np.ascontiguousarray(tbl)


def lhsT_layout(w):
    din, dout = w.shape
    return np.ascontiguousarray(w.reshape(din, dout // 128, 128).transpose(1, 0, 2))


INPUT_DECLS = {}


def build_program(n_sub=N_SUB):
    INPUT_DECLS.clear()
    nc = bacc.Bacc("TRN2", target_bir_lowering=False, debug=False)

    def din(name, shape, dt=F32):
        INPUT_DECLS[name] = (tuple(shape), dt)
        return nc.dram_tensor(name, shape, dt, kind="ExternalInput").ap()

    x_in = din("x", [TPC, D])
    info_in = din("coreinfo", [1, 8], I32)
    ropeC_in = din("ropeC", [128, TPC])
    ropeSa_in = din("ropeSa", [128, TPC])
    ropeSb_in = din("ropeSb", [128, TPC])
    ident_in = din("ident", [128, 128])
    alibi_in = din("alibi", [16, 30, 128, 128])
    masks_in = din("masks", [1, 8])
    bsel_in = din("bsel", [4, 128], BF16)
    bsel1_in = din("bsel1", [1, 4], BF16)
    ones4_in = din("ones4", [1, 4], BF16)
    ones_in = din("onesrow", [1, 128], BF16)
    iota8_in = din("iota8", [128, 16])

    wi = {}

    def wdef(name, shape, dt=F32):
        wi[name] = din(name, shape, dt)

    wdef("l0a_q", [NDC, D, 128]); wdef("l0a_k", [4, D, 128])
    wdef("l0a_v", [D, 512]); wdef("l0a_o", [D, D])
    wdef("l0f_ing", [128, D]); wdef("l0f_inb", [128, D])
    wdef("l0f_png", [4, D]); wdef("l0f_pnb", [4, D])
    wdef("l0f_outg", [128, D]); wdef("l0f_outb", [128, D])
    wdef("l0f_agg", [D, D]); wdef("l0f_aggb_my", [4, D])
    wdef("l0f_d1", [D, 1024]); wdef("l0f_d1b", [4, 1024])
    wdef("l0f_d2", [1024, D]); wdef("l0f_d2b", [4, D])
    wdef("l0f_bc", [D, 256]); wdef("l0f_bcb", [4, 256])
    wdef("l0f_gate", [D, D]); wdef("l0f_gateb", [1, D])
    wdef("l0b_q", [NDC, D, 128]); wdef("l0b_k", [4, D, 128])
    wdef("l0b_v", [D, 512]); wdef("l0b_o", [D, D])
    wdef("l1a_q", [NDC, D, 128]); wdef("l1a_k", [NDC, D, 128])
    wdef("l1a_v", [D, D]); wdef("l1a_o", [D, D]); wdef("l1a_bo", [1, D])
    wdef("l1f_wg", [D, 8]); wdef("l1f_w", [8, D, D]); wdef("l1f_b", [8, D])
    wdef("l1b_qkv_q", [NDC, D, 128]); wdef("l1b_qkv_k", [NDC, D, 128])
    wdef("l1b_qkv_v", [D, D]); wdef("l1b_o", [D, D]); wdef("l1b_bo", [1, D])

    out_h = nc.dram_tensor("out", [TPC, D], F32, kind="ExternalOutput").ap()

    kt_hi_sh = nc.dram_tensor("kt_hi_sh", [2, 2048, 1024], BF16, kind="Internal",
                              addr_space="Shared").ap()
    kt_lo_sh = nc.dram_tensor("kt_lo_sh", [2, 2048, 1024], BF16, kind="Internal",
                              addr_space="Shared").ap()
    vp_sh = nc.dram_tensor("vp_sh", [2, 1024, 16, 256], BF16, kind="Internal",
                           addr_space="Shared").ap()
    q_spill = nc.dram_tensor("q_spill", [2, NDC, 128, TPC], BF16, kind="Internal").ap()
    kt_hi_st = nc.dram_tensor("kt_hi_st", [2048, 1024], BF16, kind="Internal").ap()
    kt_lo_st = nc.dram_tensor("kt_lo_st", [2048, 1024], BF16, kind="Internal").ap()
    vp_st = nc.dram_tensor("vp_st", [1024, 16, 256], BF16, kind="Internal").ap()
    g_spill = nc.dram_tensor("g_spill", [NT, 128, 2048], F32, kind="Internal").ap()
    qkv_f32 = nc.dram_tensor("qkv_f32", [3, 16, 128, 1024], F32, kind="Internal").ap()
    ccm_in = nc.dram_tensor("ccm_in", [4, 1], F32, kind="Internal").ap()
    ccm_out = nc.dram_tensor("ccm_out", [4, 1], F32, kind="Internal").ap()
    bsc = nc.dram_tensor("bsc", [4, 3], F32, kind="Internal").ap()
    cc_in = nc.dram_tensor("cc_in", [4, 2048], F32, kind="Internal").ap()
    cc_out = nc.dram_tensor("cc_out", [4, 2048], F32, kind="Internal").ap()

    es = ExitStack()
    with es:
        sbuf = lambda name, shape, dt=F32: es.enter_context(nc.sbuf_tensor(name, shape, dt))
        sem = lambda name: es.enter_context(nc.semaphore(name))

        xres = [sbuf(f"xres{i}", [128, D]) for i in range(NT)]
        featT_hi = sbuf("featT_hi", [128, NDC * TPC], BF16)
        featT_lo = sbuf("featT_lo", [128, NDC * TPC], BF16)
        ident = sbuf("ident_sb", [128, 128])
        info = sbuf("info_sb", [1, 8], I32)
        stats = sbuf("stats", [128, 64])
        w32 = [sbuf(f"w32_{i}", [128, 1024]) for i in range(2)]
        whi = [sbuf(f"whi_{i}", [128, 2048], BF16) for i in range(2)]
        wlo = [sbuf(f"wlo_{i}", [128, 2048], BF16) for i in range(2)]
        ev32 = sbuf("ev32", [128, 2048])
        ev32b = sbuf("ev32b", [128, 1024])
        evh = sbuf("evh", [128, 512], BF16)
        evl = sbuf("evl", [128, 512], BF16)
        kbuf_hi = sbuf("kbuf_hi", [128, 2048], BF16)
        kbuf_lo = sbuf("kbuf_lo", [128, 2048], BF16)
        vbuf = sbuf("vbuf", [128, 4096], BF16)
        xln = ev32
        p32 = ev32
        rope_c = kbuf_hi[:].bitcast(F32)
        rope_s = kbuf_lo[:].bitcast(F32)
        qbuf_hi = sbuf("qbuf_hi", [128, TPC], BF16)
        qbuf_lo = sbuf("qbuf_lo", [128, TPC], BF16)
        pT_hi = sbuf("pT_hi", [128, 2048], BF16)
        pT_lo = sbuf("pT_lo", [128, 2048], BF16)
        arow = sbuf("arow", [128, 16])
        scr2 = sbuf("scr2", [128, 2048])
        moe_c = sbuf("moe_c", [128, 64])
        vbuf32 = vbuf[:].bitcast(F32)
        kbuf_hi32 = kbuf_hi[:].bitcast(F32)
        kbuf_lo32 = kbuf_lo[:].bitcast(F32)
        pT_hi32 = pT_hi[:].bitcast(F32)
        pT_lo32 = pT_lo[:].bitcast(F32)

        psA = es.enter_context(nc.psum_tensor("psA", [128, 2048], F32))
        psB = [es.enter_context(nc.psum_tensor(f"psB{i}", [128, 512], F32)) for i in range(2)]
        psT = [es.enter_context(nc.psum_tensor(f"psT{i}", [128, 512], F32)) for i in range(2)]

        R = {}

        def res(name):
            if name not in R:
                R[name] = Res()
            return R[name]

        rxres = [res(f"xres{i}") for i in range(NT)]
        rfeatT = res("featT"); rstats = res("stats");         rw32 = [res(f"w32_{i}") for i in range(2)]
        rwsp = [res(f"wsp_{i}") for i in range(2)]
        rev = res("ev32"); revb = res("ev32b"); revh = res("evh")
        rkbuf = res("kbuf"); rvbuf = res("vbuf"); rqbuf = res("qbuf")
        rpT = res("pT"); rarow = res("arow")
        rpsA = [res(f"psA{i}") for i in range(4)]
        rpsB = [res(f"psB{i}") for i in range(2)]
        rpsT = [res(f"psT{i}") for i in range(2)]
        rkv = res("kv_sh"); rqsp = res("q_spill"); rgsp = res("g_spill")
        rmoec = res("moe_c")

        s_x = sem("s_x")
        s_w = [sem(f"s_w{i}") for i in range(2)]
        s_kvw = sem("s_kvw"); s_kvr = sem("s_kvr"); s_kvv = sem("s_kvv")
        s_kvb = sem("s_kvb")
        s_qs = sem("s_qs"); s_qr = sem("s_qr")
        s_out = sem("s_out"); s_rope = sem("s_rope"); s_misc = sem("s_misc")
        s_alibi = sem("s_alibi")
        s_rot = sem("s_rot")
        kv_ready = sem("kv_ready")
        kv_done = sem("kv_done")
        prep_sem = sem("prep_sem"); lsem = sem("lsem")
        ccsem = sem("ccsem")

        sched = Sched(nc, es)
        RT = {"kv_sig": 0, "prep": 0, "kv_rsig": 0}
        parv = {}

        onesrow = sbuf("ones_sb", [1, 128], BF16)
        bsel1 = sbuf("bsel1_sb", [1, 4], BF16)
        ones4 = sbuf("ones4_sb", [1, 4], BF16)
        bselr = sbuf("bselr_sb", [4, 128], BF16)
        masks_sb = sbuf("masks_sb", [1, 8])
        iota8_sb = sbuf("iota8_sb", [128, 16])
        scl = sbuf("scl_sb", [128, 8])
        t_info = sched.dma("sync", info[:], info_in[:, :], s_misc)
        t_ident = sched.dma("sync", ident[:], ident_in[:, :], s_misc)
        sched.dma("sync", onesrow[:], ones_in[:, :], s_misc)
        sched.dma("sync", bsel1[:], bsel1_in[:, :], s_misc)
        sched.dma("sync", ones4[:], ones4_in[:, :], s_misc)
        sched.dma("sync", bselr[:], bsel_in[:, :], s_misc)
        sched.dma("sync", iota8_sb[:], iota8_in[:, :], s_misc)
        t_masks = sched.dma("sync", masks_sb[:], masks_in[:, :], s_misc)
        t_consts = t_masks
        t_x = [sched.dma("sync", xres[i][:], x_in[i * 128:(i + 1) * 128, :], s_x)
               for i in range(NT)]
        for i in range(NT):
            rxres[i].did_write(t_x[i])

        colv = {}
        dyn = {}

        def load_parity(G):
            reg = G.alloc_register("par_reg")
            G.load(reg, info[0:1, 0:1])
            parv["v"] = G.snap(reg, min_val=0, max_val=1)
            reg2 = G.alloc_register("col_reg")
            G.load(reg2, info[0:1, 4:5])
            colv["v"] = G.snap(reg2, min_val=0, max_val=2048 - 256,
                               guaranteed_mod_val=256)
            dyn["kt_hi"] = kt_hi_sh[ds(parv["v"], 1)]
            dyn["kt_lo"] = kt_lo_sh[ds(parv["v"], 1)]
            dyn["vp"] = vp_sh[ds(parv["v"], 1)]

        sched.raw("gpsimd", load_parity, deps=[t_consts])

        def ln_stats(eps, srcs, rsrcs):
            deps = []
            for rs in rsrcs:
                deps += rs.for_read()
            tks = []
            for i in range(NT):
                t = sched.op("vector", lambda V, i=i: V.reduce_sum(
                    out=stats[:, i:i + 1], in_=srcs[i][:], axis=AX.X),
                    deps=deps + rstats.for_write())
                rstats.did_write(t)
                rsrcs[i].did_read(t)
                tks.append(t)
            t_negmu = sched.op("scalar", lambda Sc: Sc.activation(
                out=stats[:, 0:8], in_=stats[:, 0:8], func=AF.Copy, scale=-1.0 / D),
                deps=tks + rstats.for_write())
            rstats.did_write(t_negmu)
            vks = []
            for i in range(NT):
                t = sched.op("scalar", lambda Sc, i=i: Sc.activation(
                    out=ev32[:, 0:2048], in_=srcs[i][:], func=AF.Square,
                    bias=stats[:, i:i + 1], scale=1.0,
                    accum_out=stats[:, 16 + i:17 + i]),
                    deps=[t_negmu] + rev.for_write() + rstats.for_write())
                rstats.did_write(t)
                rsrcs[i].did_read(t)
                rev.did_write(t)
                vks.append(t)
            t_var = sched.op("vector", lambda V: V.tensor_scalar(
                out=stats[:, 24:32], in0=stats[:, 16:24], scalar1=1.0 / D,
                scalar2=float(eps), op0=ALU.mult, op1=ALU.add), deps=vks)
            rstats.did_write(t_var)
            t_s0 = sched.op("scalar", lambda Sc: Sc.activation(
                out=stats[:, 32:40], in_=stats[:, 24:32], func=AF.Sqrt), deps=[t_var])
            rstats.did_write(t_s0)
            t_r0 = sched.op("vector", lambda V: V.reciprocal(
                out=stats[:, 40:48], in_=stats[:, 32:40]), deps=[t_s0])
            rstats.did_write(t_r0)
            t_h = sched.op("vector", lambda V: V.tensor_tensor(
                out=stats[:, 48:56], in0=stats[:, 24:32], in1=stats[:, 40:48],
                op=ALU.mult), deps=[t_r0, t_var])
            t_s1 = sched.op("vector", lambda V: V.tensor_tensor(
                out=stats[:, 48:56], in0=stats[:, 48:56], in1=stats[:, 32:40],
                op=ALU.add), deps=[t_h, t_s0])
            t_s1b = sched.op("scalar", lambda Sc: Sc.activation(
                out=stats[:, 48:56], in_=stats[:, 48:56], func=AF.Copy, scale=0.5),
                deps=[t_s1])
            rstats.did_write(t_s1b)
            t_inv = sched.op("vector", lambda V: V.reciprocal(
                out=stats[:, 8:16], in_=stats[:, 48:56]), deps=[t_s1b])
            rstats.did_write(t_inv)
            return t_inv

        def ln_apply(i, srcs, rsrcs):
            t = sched.op("vector", lambda V, i=i: V.tensor_scalar(
                out=xln[:], in0=srcs[i][:], scalar1=stats[:, i:i + 1],
                scalar2=stats[:, 8 + i:9 + i], op0=ALU.add, op1=ALU.mult),
                deps=rsrcs[i].for_read() + rstats.for_read() + rev.for_write())
            rsrcs[i].did_read(t)
            rstats.did_read(t)
            rev.did_write(t)
            return t

        def build_featT(eps, srcs=None, rsrcs=None):
            srcs = srcs or xres
            rsrcs = rsrcs or rxres
            ln_stats(eps, srcs, rsrcs)
            last = None
            for i in range(NT):
                t_ln = ln_apply(i, srcs, rsrcs)
                for g in range(NDC // 4):
                    bank = psT[g % 2]
                    rbank = rpsT[g % 2]
                    mm = []
                    for jj in range(4):
                        j = g * 4 + jj
                        t = sched.op("tensor", lambda T, j=j, jj=jj, bank=bank: T.transpose(
                            bank[:, jj * 128:jj * 128 + 128],
                            xln[:, j * 128:(j + 1) * 128], ident[:]),
                            deps=[t_ln, t_consts] + rbank.for_write() + rfeatT.for_write())
                        rev.did_read(t)
                        mm.append(t)
                    rbank.did_write(mm[-1])
                    for jj in range(4):
                        j = g * 4 + jj
                        col = j * TPC + i * 128
                        t_hi = sched.op("vector", lambda V, bank=bank, jj=jj, col=col:
                                        V.tensor_copy(out=featT_hi[:, col:col + 128],
                                                      in_=bank[:, jj * 128:jj * 128 + 128]),
                                        deps=mm)
                        t_lo = sched.op("vector", lambda V, bank=bank, jj=jj, col=col:
                                        V.tensor_tensor(out=featT_lo[:, col:col + 128],
                                                        in0=bank[:, jj * 128:jj * 128 + 128],
                                                        in1=featT_hi[:, col:col + 128],
                                                        op=ALU.subtract), deps=[t_hi])
                        rbank.did_read(t_lo)
                        last = t_lo
            rfeatT.did_write(last)
            return last

        def split_piece(dst_hi_ap, dst_lo_ap, src_ap, deps):
            t_h = sched.op("scalar", lambda Sc: Sc.activation(
                out=dst_hi_ap, in_=src_ap, func=AF.Copy), deps=deps)
            t_l = sched.op("vector", lambda V: V.tensor_tensor(
                out=dst_lo_ap, in0=src_ap, in1=dst_hi_ap, op=ALU.subtract), deps=[t_h])
            return t_h, t_l

        def stream_lhsT_slab(w_ap, oc, buf, extra_deps=()):
            last = None
            for half in range(2):
                t_d = sched.dma("sync", w32[buf][:, 0:1024],
                                mkap(w_ap, oc * D * 128 + half * 8 * 128 * 128,
                                     [[128, 128], [128 * 128, 8], [1, 128]]),
                                s_w[buf],
                                deps=(list(extra_deps) if half == 0 else []) + rw32[buf].for_write())
                rw32[buf].did_write(t_d)
                t_h, t_l = split_piece(
                    whi[buf][:, half * 1024:half * 1024 + 1024],
                    wlo[buf][:, half * 1024:half * 1024 + 1024],
                    w32[buf][:, 0:1024],
                    deps=[t_d] + rwsp[buf].for_write())
                rw32[buf].did_read(t_l)
                rwsp[buf].did_write(t_l)
                last = t_l
            return last

        def stream_rhs_piece(w_ap, row0, col0, ncols, buf, extra_deps=(),
                             flat_base=None, dout=None):
            dout = dout if dout is not None else w_ap.shape[-1]
            base = flat_base if flat_base is not None else row0 * dout + col0
            t_d = sched.dma("sync", w32[buf][:, 0:ncols],
                            mkap(w_ap, base, [[dout, 128], [1, ncols]]),
                            s_w[buf],
                            deps=list(extra_deps) + rw32[buf].for_write())
            rw32[buf].did_write(t_d)
            t_h, t_l = split_piece(whi[buf][:, 0:ncols], wlo[buf][:, 0:ncols],
                                   w32[buf][:, 0:ncols],
                                   deps=[t_d] + rwsp[buf].for_write())
            rw32[buf].did_read(t_l)
            rwsp[buf].did_write(t_l)
            return t_l

        def mm3(bank_ap, rbank, lT_hi, lT_lo, r_hi, r_lo, start, stop, deps=(),
                mark=False):
            d0 = list(deps)
            if start:
                d0 += rbank.for_write()
            sched.op("tensor", lambda T: T.matmul(bank_ap, lT_hi, r_hi,
                                                  start=start, stop=False),
                     deps=d0, inc=False)
            sched.op("tensor", lambda T: T.matmul(bank_ap, lT_hi, r_lo,
                                                  start=False, stop=False), inc=False)
            t3 = sched.op("tensor", lambda T: T.matmul(bank_ap, lT_lo, r_hi,
                                                       start=False, stop=stop),
                          inc=stop or mark)
            if stop:
                rbank.did_write(t3)
            return t3

        def sub_attn(layer):
            nkv = layer["nkv"]
            last_feat = build_featT(1e-5)
            featdep = [last_feat]

            t_rc = t_rs = None
            if layer["rope"]:
                t_rc = sched.dma("sync", rope_c, ropeC_in[:, :], s_rope,
                                 deps=rkbuf.for_write())
                t_rs = sched.dma("sync", rope_s,
                                 (ropeSa_in if layer["rope"] == "a" else ropeSb_in)[:, :],
                                 s_rope, deps=rkbuf.for_write())
                rkbuf.did_write(t_rs)

            def proj_featmajor(w_ap, noc, writer):
                sp_next = stream_lhsT_slab(w_ap, 0, 0, extra_deps=featdep)
                for oc in range(noc):
                    buf = oc % 2
                    sp = sp_next
                    mm_last = []
                    for dc in range(NDC):
                        for tg in range(2):
                            t = mm3(psA[:, tg * 512:tg * 512 + 512], rpsA[tg],
                                    whi[buf][:, dc * 128:dc * 128 + 128],
                                    wlo[buf][:, dc * 128:dc * 128 + 128],
                                    featT_hi[:, dc * TPC + tg * 512: dc * TPC + tg * 512 + 512],
                                    featT_lo[:, dc * TPC + tg * 512: dc * TPC + tg * 512 + 512],
                                    start=(dc == 0), stop=(dc == NDC - 1),
                                    deps=([sp] + featdep) if dc == 0 else ())
                            if dc == NDC - 1:
                                mm_last.append(t)
                    rwsp[buf].did_read(mm_last[-1])
                    rfeatT.did_read(mm_last[-1])
                    if oc + 1 < noc:
                        sp_next = stream_lhsT_slab(w_ap, oc + 1, (oc + 1) % 2,
                                                   extra_deps=featdep)
                    t_ev = []
                    for tg in range(2):
                        t = sched.op("vector", lambda V, tg=tg: V.tensor_copy(
                            out=ev32[:, tg * 512:tg * 512 + 512],
                            in_=psA[:, tg * 512:tg * 512 + 512]),
                            deps=[mm_last[tg]] + rev.for_write())
                        rpsA[tg].did_read(t)
                        t_ev.append(t)
                    rev.did_write(t_ev[-1])
                    if layer["rope"]:
                        if layer["rope"] == "a":
                            t_r1 = sched.dma("gpsimd", ev32b[0:64, 0:1024],
                                             ev32[64:128, 0:1024], s_rot,
                                             deps=t_ev + revb.for_write())
                            t_r2 = sched.dma("gpsimd", ev32b[64:128, 0:1024],
                                             ev32[0:64, 0:1024], s_rot, deps=t_ev)
                        else:
                            t_r1 = sched.dma("gpsimd", ev32b[1:128, 0:1024],
                                             ev32[0:127, 0:1024], s_rot,
                                             deps=t_ev + revb.for_write())
                            t_r2 = sched.dma("gpsimd", ev32b[0:1, 0:1024],
                                             ev32[127:128, 0:1024], s_rot, deps=t_ev)
                        rev.did_read(t_r1); rev.did_read(t_r2)
                        revb.did_write(t_r2)
                        t_m1 = sched.op("vector", lambda V: V.tensor_tensor(
                            out=ev32[:, 0:1024], in0=ev32[:, 0:1024], in1=rope_c,
                            op=ALU.mult), deps=t_ev + [t_rs, t_r1, t_r2])
                        t_m2 = sched.op("vector", lambda V: V.tensor_tensor(
                            out=ev32b[:, 0:1024], in0=ev32b[:, 0:1024], in1=rope_s,
                            op=ALU.mult), deps=[t_r1, t_r2, t_rs])
                        rkbuf.did_read(t_m2)
                        revb.did_write(t_m2)
                        t_fin = sched.op("vector", lambda V: V.tensor_tensor(
                            out=ev32[:, 0:1024], in0=ev32[:, 0:1024], in1=ev32b[:, 0:1024],
                            op=ALU.add), deps=[t_m1, t_m2])
                        rev.did_write(t_fin)
                        revb.did_read(t_fin)
                        src_dep = [t_fin]
                    else:
                        src_dep = t_ev
                    writer(oc, src_dep)

            def q_writer(oc, src_dep):
                t_h, t_l = split_piece(qbuf_hi[:], qbuf_lo[:], ev32[:, 0:1024],
                                       deps=src_dep + rqbuf.for_write())
                rev.did_read(t_l)
                rqbuf.did_write(t_l)
                t1 = sched.dma("gpsimd", q_spill[0, oc, :, :], qbuf_hi[:], s_qs,
                               deps=[t_h])
                t2 = sched.dma("gpsimd", q_spill[1, oc, :, :], qbuf_lo[:], s_qs,
                               deps=[t_l])
                rqbuf.did_read(t1); rqbuf.did_read(t2)
                rqsp.did_write(t2)

            def k_writer(oc, src_dep):
                t_h, t_l = split_piece(qbuf_hi[:], qbuf_lo[:], ev32[:, 0:1024],
                                       deps=src_dep + rqbuf.for_write())
                rev.did_read(t_l)
                rqbuf.did_write(t_l)
                t1 = sched.dma("gpsimd", kt_hi_st[oc * 128:(oc + 1) * 128, :],
                               qbuf_hi[:], s_kvw, deps=[t_h])
                t2 = sched.dma("gpsimd", kt_lo_st[oc * 128:(oc + 1) * 128, :],
                               qbuf_lo[:], s_kvw, deps=[t_l])
                rqbuf.did_read(t1); rqbuf.did_read(t2)
                rkv.did_write(t2)

            t_prevread = (kv_done, RT["kv_rsig"]) if RT["kv_rsig"] else None
            rkv.did_write(t_prevread) if t_prevread else None

            proj_featmajor(layer["wq"], NDC, q_writer)
            proj_featmajor(layer["wk"], nkv, k_writer)

            banks = [psA[:, 0:512], psA[:, 512:1024], psA[:, 1024:1536],
                     psA[:, 1536:2048], psB[0][:], psB[1][:], psT[0][:], psT[1][:]]
            rbanks = rpsA + rpsB + rpsT
            kvd = nkv * 128
            for og in range(kvd // 512):
                for dc in range(NDC):
                    buf = dc % 2
                    sp = stream_rhs_piece(layer["wv"], dc * 128, og * 512, 512, buf)
                    tm = None
                    for i in range(NT):
                        tm = mm3(banks[i], rbanks[i],
                            featT_hi[:, dc * TPC + i * 128: dc * TPC + i * 128 + 128],
                            featT_lo[:, dc * TPC + i * 128: dc * TPC + i * 128 + 128],
                            whi[buf][:, 0:512], wlo[buf][:, 0:512],
                            start=(dc == 0), stop=(dc == NDC - 1),
                            deps=[sp] + (featdep if dc == 0 else []),
                            mark=(i == NT - 1))
                    rwsp[buf].did_read(tm)
                    rfeatT.did_read(tm)
                for i in range(NT):
                    t_h, t_l = split_piece(evh[:], evl[:], banks[i],
                                           deps=rbanks[i].for_read() + revh.for_write())
                    rbanks[i].did_read(t_l)
                    revh.did_write(t_l)
                    t1 = sched.dma("gpsimd",
                                   vp_st[i * 128:(i + 1) * 128, og * 4:og * 4 + 4, 0:128],
                                   evh[:], s_kvw, deps=[t_h])
                    t2 = sched.dma("gpsimd",
                                   vp_st[i * 128:(i + 1) * 128, og * 4:og * 4 + 4, 128:256],
                                   evl[:], s_kvw, deps=[t_l])
                    revh.did_read(t1); revh.did_read(t2)
                    rkv.did_write(t2)

            kvst_tok = (s_kvw, sched.dma_cnt[id(s_kvw)])
            tb1 = sched.dma("gpsimd", lambda: dyn["kt_hi"][0, 0:kvd, :],
                            kt_hi_st[0:kvd, :], s_kvb, deps=[kvst_tok] + rkv.for_write())
            tb2 = sched.dma("gpsimd", lambda: dyn["kt_lo"][0, 0:kvd, :],
                            kt_lo_st[0:kvd, :], s_kvb, deps=[kvst_tok])
            tb3 = sched.dma("gpsimd", lambda nh=kvd // 128: dyn["vp"][0, :, 0:nh, :],
                            vp_st[:, 0:kvd // 128, :], s_kvb, deps=[kvst_tok])
            rkv.did_write(tb3)
            kvw_tok = tb3
            RT["prep"] += 1
            pv = RT["prep"]

            def signal(G, pv=pv):
                ins = G.remote_sem_update_broadcast(
                    kv_ready, lsem, rdests=[(0, 1)] + [None] * 7,
                    sim_remote_sem_inc=2)
                ins.then_inc(prep_sem, 1)
                G.wait_ge(prep_sem, pv)
                G.trigger_dma(count=1)

            sched.raw("gpsimd", signal, deps=[kvw_tok])
            RT["kv_sig"] += 2
            t_peer = (kv_ready, RT["kv_sig"])

            if layer["rope"] == "a":
                scaling = float(np.float32(HD) ** np.float32(-0.5))
            else:
                scaling = float(np.float32(1.0) / np.float32(np.sqrt(np.float32(HD))))
            rep = 16 // nkv
            for kvh in range(nkv):
                kl = []
                for slot in range(2):
                    for src, dst in ((kt_hi_sh, kbuf_hi), (kt_lo_sh, kbuf_lo)):
                        t = sched.dma("gpsimd",
                                      dst[:, slot * 1024:slot * 1024 + 1024],
                                      src[slot, kvh * 128:(kvh + 1) * 128, :],
                                      s_kvr,
                                      deps=[t_peer, kvw_tok] + rkbuf.for_write())
                        kl.append(t)
                rkbuf.did_write(kl[-1])
                vl = []
                for ct in range(16):
                    slot = 0 if ct < 8 else 1
                    lt = ct % 8
                    t = sched.dma("gpsimd",
                                  vbuf[:, ct * 256:ct * 256 + 256],
                                  vp_sh[slot, lt * 128:(lt + 1) * 128,
                                        kvh, :],
                                  s_kvv,
                                  deps=[t_peer, kvw_tok] + rvbuf.for_write())
                    vl.append(t)
                rvbuf.did_write(vl[-1])
                t_kv = [kl[-1], vl[-1]]

                def emit_av(ctx):
                    slots_p, rc, colp, t_kv_p, pt_done_p = ctx
                    t_av = None
                    for j, ct in enumerate(slots_p):
                        t_av = mm3(psB[0][:, 0:128], rpsB[0],
                                   pT_hi[:, j * 128:j * 128 + 128],
                                   pT_lo[:, j * 128:j * 128 + 128],
                                   vbuf[:, ct * 256:ct * 256 + 128],
                                   vbuf[:, ct * 256 + 128:ct * 256 + 256],
                                   start=(j == 0), stop=(j == len(slots_p) - 1),
                                   deps=(pt_done_p + t_kv_p) if j == 0 else ())
                    rpT.did_read(t_av)
                    rvbuf.did_read(t_av)
                    t_o = sched.op("vector", lambda V, rc=rc: V.tensor_single_scalar(
                        out=ev32b[:, 0:128], in_=psB[0][:, 0:128],
                        scalar=arow[:, rc:rc + 1], op=ALU.mult),
                        deps=[t_av] + revb.for_write() + revb.for_read())
                    rpsB[0].did_read(t_o)
                    rarow.did_read(t_o)
                    revb.did_write(t_o)
                    bank = psT[0]; rbank = rpsT[0]
                    t_tr = sched.op("tensor", lambda T, bank=bank: T.transpose(
                        bank[:, 0:128], ev32b[:, 0:128], ident[:]),
                        deps=[t_o, t_consts] + rbank.for_write())
                    revb.did_read(t_tr)
                    rbank.did_write(t_tr)
                    t_hi = sched.op("vector", lambda V, colp=colp, bank=bank: V.tensor_copy(
                        out=featT_hi[:, colp:colp + 128], in_=bank[:, 0:128]),
                        deps=[t_tr] + rfeatT.for_write())
                    t_lo = sched.op("vector", lambda V, colp=colp, bank=bank:
                                    V.tensor_tensor(out=featT_lo[:, colp:colp + 128],
                                                    in0=bank[:, 0:128],
                                                    in1=featT_hi[:, colp:colp + 128],
                                                    op=ALU.subtract), deps=[t_hi])
                    rbank.did_read(t_lo)
                    rfeatT.did_write(t_lo)

                av_ctx = [None]
                for r in range(rep):
                    h = kvh * rep + r
                    tq1 = sched.dma("gpsimd", qbuf_hi[:], q_spill[0, h, :, :], s_qr,
                                    deps=rqsp.for_read() + rqbuf.for_write())
                    tq2 = sched.dma("gpsimd", qbuf_lo[:], q_spill[1, h, :, :], s_qr,
                                    deps=rqsp.for_read())
                    rqbuf.did_write(tq2)
                    t_q = [tq1, tq2]
                    for i in range(NT):
                        if layer.get("alibi"):
                            groups = [(0, False)] + ([(4, False)] if i >= 4 else []) \
                                + [(8, True)] + ([(12, True)] if i >= 4 else [])
                        else:
                            groups = [(4 * g, False) for g in range(4)]
                        ng = len(groups)
                        w = ng * 512
                        slots = [s0 + j for s0, _ in groups for j in range(4)]
                        mm_last = []
                        mx_toks = []
                        for b, (s0, _) in enumerate(groups):
                            t = mm3(psA[:, b * 512:b * 512 + 512], rpsA[b],
                                    qbuf_hi[:, i * 128:i * 128 + 128],
                                    qbuf_lo[:, i * 128:i * 128 + 128],
                                    kbuf_hi[:, s0 * 128:s0 * 128 + 512],
                                    kbuf_lo[:, s0 * 128:s0 * 128 + 512],
                                    start=True, stop=True,
                                    deps=t_q + t_kv)
                            mm_last.append(t)
                            if not layer.get("alibi"):
                                tmx = sched.op("vector", lambda V, b=b: V.reduce_max(
                                    out=arow[:, 4 + b:5 + b],
                                    in_=psA[:, b * 512:b * 512 + 512], axis=AX.X,
                                    negate=True), deps=[t] + rarow.for_write())
                                rpsA[b].did_read(tmx)
                                rarow.did_write(tmx)
                                mx_toks.append(tmx)
                        rqbuf.did_read(mm_last[-1])
                        rkbuf.did_read(mm_last[-1])
                        if layer.get("alibi"):
                            badd_toks = []
                            for b, (s0, peer) in enumerate(groups):
                                bk0 = 2 * s0 if not peer else 2 * (s0 - 8) + 1
                                r0 = bk0 - 2 * i + 14
                                t = sched.dma("scalar",
                                              scr2[:, b * 512:b * 512 + 512],
                                              mkap(alibi_in,
                                                   (kvh * 30 + r0) * 16384,
                                                   [[128, 128], [32768, 4], [1, 128]]),
                                              s_alibi,
                                              deps=rscr2.for_write() + rscr2.for_read())
                                rscr2.did_write(t)
                                t_badd = sched.op("vector", lambda V, b=b: V.tensor_tensor(
                                    out=ev32[:, b * 512:b * 512 + 512],
                                    in0=psA[:, b * 512:b * 512 + 512],
                                    in1=scr2[:, b * 512:b * 512 + 512],
                                    op=ALU.add),
                                    deps=[mm_last[b], t]
                                    + (rev.for_write() + rev.for_read() if b == 0
                                       else []))
                                rpsA[b].did_read(t_badd)
                                rscr2.did_read(t_badd)
                                rev.did_write(t_badd)
                                badd_toks.append(t_badd)
                                tmx = sched.op("vector", lambda V, b=b: V.reduce_max(
                                    out=arow[:, 4 + b:5 + b],
                                    in_=ev32[:, b * 512:b * 512 + 512], axis=AX.X,
                                    negate=True), deps=[t_badd] + rarow.for_write())
                                rev.did_read(tmx)
                                rarow.did_write(tmx)
                                mx_toks.append(tmx)
                            sc_src = ev32[:, 0:w]
                            sc_dep = badd_toks
                        else:
                            sc_src = psA[:, 0:w]
                            sc_dep = mm_last
                        if layer.get("alibi"):
                            t_mx = sched.op("vector", lambda V, ng=ng: V.tensor_reduce(
                                out=arow[:, 0:1], in_=arow[:, 4:4 + ng], op=ALU.min,
                                axis=AX.X), deps=mx_toks + rarow.for_write())
                            rarow.did_write(t_mx)
                        else:
                            t_mx = sched.op("vector", lambda V, ng=ng: V.tensor_reduce(
                                out=arow[:, 0:1], in_=arow[:, 4:4 + ng], op=ALU.min,
                                axis=AX.X), deps=mx_toks + rarow.for_write())
                            rarow.did_write(t_mx)
                        t_bias = sched.op("scalar", lambda Sc: Sc.activation(
                            out=arow[:, 1:2], in_=arow[:, 0:1], func=AF.Copy,
                            scale=scaling), deps=[t_mx])
                        rarow.did_write(t_bias)
                        if av_ctx[0] is not None:
                            emit_av(av_ctx[0])
                            av_ctx[0] = None
                        exp_toks = []
                        for g in range(ng):
                            esrc = (ev32[:, g * 512:g * 512 + 512] if layer.get("alibi")
                                    else psA[:, g * 512:g * 512 + 512])
                            t_e = sched.op("scalar", lambda Sc, g=g, esrc=esrc:
                                           Sc.activation(
                                out=p32[:, g * 512:g * 512 + 512], in_=esrc,
                                func=AF.Exp, bias=arow[:, 1:2], scale=scaling,
                                accum_out=arow[:, 8 + g:9 + g]),
                                deps=[t_bias] + sc_dep
                                + (rev.for_write() + rev.for_read()
                                   if not layer.get("alibi") else []))
                            rpsA[g].did_read(t_e)
                            rev.did_write(t_e)
                            rarow.did_write(t_e)
                            exp_toks.append(t_e)
                        t_exp = exp_toks[-1]
                        t_sum = sched.op("vector", lambda V, ng=ng: V.reduce_sum(
                            out=arow[:, 2:3], in_=arow[:, 8:8 + ng], axis=AX.X),
                            deps=exp_toks)
                        rarow.did_write(t_sum)
                        rc = 3 if (r * NT + i) % 2 == 0 else 12
                        t_rec = sched.op("vector", lambda V, rc=rc: V.reciprocal(
                            out=arow[:, rc:rc + 1], in_=arow[:, 2:3]), deps=[t_sum])
                        rarow.did_write(t_rec)
                        pt_done = []
                        for g in range(ng):
                            bank = psT[g % 2]; rbank = rpsT[g % 2]
                            mm = []
                            for jj in range(4):
                                ct = g * 4 + jj
                                t = sched.op("tensor", lambda T, ct=ct, jj=jj, bank=bank:
                                             T.transpose(bank[:, jj * 128:jj * 128 + 128],
                                                         p32[:, ct * 128:ct * 128 + 128],
                                                         ident[:]),
                                             deps=[exp_toks[g], t_consts] + rbank.for_write())
                                rev.did_read(t)
                                mm.append(t)
                            rbank.did_write(mm[-1])
                            t_hi = sched.op("vector", lambda V, g=g, bank=bank: V.tensor_copy(
                                out=pT_hi[:, g * 512:g * 512 + 512], in_=bank[:]),
                                deps=mm + rpT.for_write())
                            t_lo = sched.op("vector", lambda V, g=g, bank=bank: V.tensor_tensor(
                                out=pT_lo[:, g * 512:g * 512 + 512], in0=bank[:],
                                in1=pT_hi[:, g * 512:g * 512 + 512], op=ALU.subtract),
                                deps=[t_hi])
                            rbank.did_read(t_lo)
                            pt_done.append(t_lo)
                        rpT.did_write(pt_done[-1])
                        av_ctx[0] = (slots, rc, h * TPC + i * 128,
                                     list(t_q) + list(t_kv) + [t_rec], pt_done)
                if av_ctx[0] is not None:
                    emit_av(av_ctx[0])
                    av_ctx[0] = None

            rd_toks = [(s_kvr, sched.dma_cnt.get(id(s_kvr), 0)),
                       (s_kvv, sched.dma_cnt.get(id(s_kvv), 0))]
            RT["prep"] += 1
            pv2 = RT["prep"]

            def signal_rd(G, pv=pv2):
                ins = G.remote_sem_update_broadcast(
                    kv_done, lsem, rdests=[(0, 1)] + [None] * 7,
                    sim_remote_sem_inc=2)
                ins.then_inc(prep_sem, 1)
                G.wait_ge(prep_sem, pv)
                G.trigger_dma(count=1)

            sched.raw("gpsimd", signal_rd, deps=rd_toks)
            rkv.did_read(rd_toks[0])
            rkv.did_read(rd_toks[1])
            RT["kv_rsig"] += 2

            attn_dep = rfeatT.for_read()
            bo = layer.get("bo")
            if bo is not None:
                t_bo = sched.dma("gpsimd", ev32b[0:1, 0:1024], bo[0:1, 0:1024], s_qs,
                                 deps=revb.for_write() + revb.for_read())
                t_bo2 = sched.dma("gpsimd", ev32b[1:2, 0:1024]
                                  if False else ev32[0:1, 0:1024],
                                  bo[0:1, 1024:2048], s_qs,
                                  deps=rev.for_write() + rev.for_read())
                rev.did_write(t_bo2)
                revb.did_write(t_bo)
            for og in range(4):
                has_bias = bo is not None
                if has_bias:
                    bsrc = ev32b[0:1, og * 512:og * 512 + 512] if og < 2 else \
                        ev32[0:1, (og - 2) * 512:(og - 2) * 512 + 512]
                    bdep = [t_bo, t_bo2]
                    sh, sl = split_piece(evh[0:1, 0:512], evl[0:1, 0:512], bsrc,
                                         deps=bdep + revh.for_write())
                    revh.did_write(sl)
                    (revb if og < 2 else rev).did_read(sl)
                for dc in range(NDC):
                    buf = dc % 2
                    sp = stream_rhs_piece(layer["wo"], dc * 128, og * 512, 512, buf)
                    tm = None
                    for i in range(NT):
                        tm = mm3(banks[i], rbanks[i],
                            featT_hi[:, dc * TPC + i * 128: dc * TPC + i * 128 + 128],
                            featT_lo[:, dc * TPC + i * 128: dc * TPC + i * 128 + 128],
                            whi[buf][:, 0:512], wlo[buf][:, 0:512],
                            start=(dc == 0), stop=(dc == NDC - 1) and not has_bias,
                            deps=[sp] + (attn_dep if dc == 0 else []),
                            mark=(i == NT - 1))
                    rwsp[buf].did_read(tm)
                    rfeatT.did_read(tm)
                if has_bias:
                    for i in range(NT):
                        sched.op("tensor", lambda T, i=i: T.matmul(
                            banks[i], onesrow[0:1, :], evh[0:1, 0:512],
                            start=False, stop=False), deps=[sh, sl, t_consts], inc=False)
                        t_f = sched.op("tensor", lambda T, i=i: T.matmul(
                            banks[i], onesrow[0:1, :], evl[0:1, 0:512],
                            start=False, stop=True), inc=True)
                        rbanks[i].did_write(t_f)
                        revh.did_read(t_f)
                for i in range(NT):
                    t = sched.op("vector", lambda V, i=i, og=og, bank=banks[i]:
                                 V.tensor_tensor(out=xres[i][:, og * 512:og * 512 + 512],
                                                 in0=bank,
                                                 in1=xres[i][:, og * 512:og * 512 + 512],
                                                 op=ALU.add),
                                 deps=rbanks[i].for_read() + [rbanks[i].w] + rxres[i].for_write())
                    rbanks[i].did_read(t)
                    rxres[i].did_write(t)


        rscr2 = res("scr2"); rcc = res("cc")
        rqb2 = rqbuf

        def transpose4(src_ap_fn, nchunks, dst_hi, dst_lo, col0, deps):
            toks = []
            for g in range((nchunks + 3) // 4):
                bank = psT[g % 2]; rbank = rpsT[g % 2]
                mm = []
                for jj in range(min(4, nchunks - g * 4)):
                    j = g * 4 + jj
                    t = sched.op("tensor", lambda T, j=j, jj=jj, bank=bank: T.transpose(
                        bank[:, jj * 4:jj * 4 + 4], src_ap_fn(j), ident[0:4, 0:4]),
                        deps=list(deps) + rbank.for_write() + [t_consts])
                    mm.append(t)
                rbank.did_write(mm[-1])
                for jj in range(min(4, nchunks - g * 4)):
                    j = g * 4 + jj
                    c = col0 + j * 4
                    th = sched.op("vector", lambda V, bank=bank, jj=jj, c=c: V.tensor_copy(
                        out=dst_hi[:, c:c + 4], in_=bank[:, jj * 4:jj * 4 + 4]), deps=mm)
                    tl = sched.op("vector", lambda V, bank=bank, jj=jj, c=c: V.tensor_tensor(
                        out=dst_lo[:, c:c + 4], in0=bank[:, jj * 4:jj * 4 + 4],
                        in1=dst_hi[:, c:c + 4], op=ALU.subtract), deps=[th])
                    rbank.did_read(tl)
                    toks.append(tl)
            return toks[-1]

        def sub_ffn():
            last_feat = build_featT(1e-5)
            featdep = [last_feat]
            par = None

            t_d1b = sched.dma("gpsimd", ev32b[0:4, 0:1024], wi["l0f_d1b"][:, :], s_rot,
                              deps=revb.for_write())
            revb.did_write(t_d1b)

            t_mv1 = sched.dma("gpsimd", ev32[0:1, 0:2048], xres[7][127:128, :], s_rot,
                              deps=rxres[7].for_read() + rev.for_write())
            rxres[7].did_read(t_mv1); rev.did_write(t_mv1)
            t_mv2 = sched.dma("gpsimd", stats[0:1, 36:37], stats[127:128, 7:8], s_rot,
                              deps=rstats.for_read() + rstats.for_write())
            t_mv3 = sched.dma("gpsimd", stats[0:1, 38:39], stats[127:128, 15:16], s_rot)
            rstats.did_write(t_mv3)
            t_row = sched.op("vector", lambda V: V.tensor_scalar(
                out=ev32[0:1, 0:2048], in0=ev32[0:1, 0:2048],
                scalar1=stats[0:1, 36:37], scalar2=stats[0:1, 38:39],
                op0=ALU.add, op1=ALU.mult),
                deps=[t_mv1, t_mv2, t_mv3])
            rstats.did_read(t_row)
            rev.did_write(t_row)
            t1 = sched.op("vector", lambda V: V.reduce_sum(
                out=stats[0:1, 56:57], in_=ev32[0:1, 0:2048], axis=AX.X),
                deps=[t_row] + rstats.for_write())
            rstats.did_write(t1)
            t2 = sched.op("scalar", lambda Sc: Sc.activation(
                out=stats[0:1, 56:57], in_=stats[0:1, 56:57], func=AF.Copy,
                scale=-1.0 / D), deps=[t1])
            t3 = sched.op("scalar", lambda Sc: Sc.activation(
                out=vbuf32[0:1, 0:2048], in_=ev32[0:1, 0:2048], func=AF.Square,
                bias=stats[0:1, 56:57], scale=1.0, accum_out=stats[0:1, 57:58]),
                deps=[t2] + rvbuf.for_write())
            rvbuf.did_write(t3)
            t4 = sched.op("vector", lambda V: V.tensor_scalar(
                out=stats[0:1, 58:59], in0=stats[0:1, 57:58], scalar1=1.0 / D,
                scalar2=1e-6, op0=ALU.mult, op1=ALU.add), deps=[t3])
            t5 = sched.op("scalar", lambda Sc: Sc.activation(
                out=stats[0:1, 59:60], in_=stats[0:1, 58:59], func=AF.Sqrt), deps=[t4])
            t6 = sched.op("vector", lambda V: V.reciprocal(
                out=stats[0:1, 60:61], in_=stats[0:1, 59:60]), deps=[t5])
            t7 = sched.op("vector", lambda V: V.tensor_tensor(
                out=stats[0:1, 61:62], in0=stats[0:1, 58:59], in1=stats[0:1, 60:61],
                op=ALU.mult), deps=[t6, t4])
            t8 = sched.op("vector", lambda V: V.tensor_tensor(
                out=stats[0:1, 61:62], in0=stats[0:1, 61:62], in1=stats[0:1, 59:60],
                op=ALU.add), deps=[t7, t5])
            t9 = sched.op("scalar", lambda Sc: Sc.activation(
                out=stats[0:1, 61:62], in_=stats[0:1, 61:62], func=AF.Copy, scale=0.5),
                deps=[t8])
            t10 = sched.op("vector", lambda V: V.reciprocal(
                out=stats[0:1, 62:63], in_=stats[0:1, 61:62]), deps=[t9])
            t_ing = sched.dma("gpsimd", vbuf32[0:1, 0:1024],
                              wi["l0f_ing"][0:1, 0:1024], s_qs,
                              deps=rvbuf.for_write() + rvbuf.for_read())
            t_ing2 = sched.dma("gpsimd", kbuf_hi32[0:1, 0:1024],
                               wi["l0f_ing"][0:1, 1024:2048], s_qs,
                               deps=rkbuf.for_write())
            t_inb = sched.dma("gpsimd", vbuf32[0:1, 1024:2048],
                              wi["l0f_inb"][0:1, 0:1024], s_qs)
            t_inb2 = sched.dma("gpsimd", kbuf_lo32[0:1, 0:1024],
                               wi["l0f_inb"][0:1, 1024:2048], s_qs)
            rvbuf.did_write(t_inb2)
            rkbuf.did_write(t_inb2)
            t_xnl = sched.op("vector", lambda V: V.tensor_scalar(
                out=ev32[0:1, 0:2048], in0=ev32[0:1, 0:2048],
                scalar1=stats[0:1, 56:57], scalar2=stats[0:1, 62:63],
                op0=ALU.add, op1=ALU.mult), deps=[t10, t3])
            t_xa = sched.op("vector", lambda V: V.tensor_tensor(
                out=ev32[0:1, 0:1024], in0=ev32[0:1, 0:1024], in1=vbuf32[0:1, 0:1024],
                op=ALU.mult), deps=[t_xnl, t_inb2])
            t_xb = sched.op("vector", lambda V: V.tensor_tensor(
                out=ev32[0:1, 1024:2048], in0=ev32[0:1, 1024:2048],
                in1=kbuf_hi32[0:1, 0:1024], op=ALU.mult), deps=[t_xnl, t_inb2])
            t_xa = sched.op("vector", lambda V: V.tensor_tensor(
                out=ev32[0:1, 0:1024], in0=ev32[0:1, 0:1024], in1=vbuf32[0:1, 1024:2048],
                op=ALU.add), deps=[t_xa])
            t_xb = sched.op("vector", lambda V: V.tensor_tensor(
                out=ev32[0:1, 1024:2048], in0=ev32[0:1, 1024:2048],
                in1=kbuf_lo32[0:1, 0:1024], op=ALU.add), deps=[t_xb])
            t_xnl = sched.op("vector", lambda V: V.tensor_copy(
                out=vbuf32[0:1, 0:1], in_=ev32[0:1, 0:1]), deps=[t_xa, t_xb])
            rvbuf.did_read(t_xnl)
            rkbuf.did_read(t_xnl)
            t_xnl = sched.op("vector", lambda V: V.tensor_single_scalar(
                out=ev32[0:1, 0:2048], in_=ev32[0:1, 0:2048],
                scalar=masks_sb[0:1, 0:1], op=ALU.mult), deps=[t_xnl, t_consts])
            rev.did_write(t_xnl)

            xt_toks = []
            for g in range(4):
                bank = psT[g % 2]; rbank = rpsT[g % 2]
                mm = []
                for jj in range(4):
                    j = g * 4 + jj
                    t = sched.op("tensor", lambda T, j=j, jj=jj, bank=bank: T.transpose(
                        bank[:, jj:jj + 1], ev32[0:1, j * 128:(j + 1) * 128],
                        ident[0:1, 0:1]),
                        deps=[t_xnl, t_consts] + rbank.for_write())
                    mm.append(t)
                rbank.did_write(mm[-1])
                for jj in range(4):
                    j = g * 4 + jj
                    th = sched.op("vector", lambda V, bank=bank, jj=jj, j=j: V.tensor_copy(
                        out=qbuf_hi[:, j:j + 1], in_=bank[:, jj:jj + 1]),
                        deps=mm + (rqb2.for_write() if g == 0 and jj == 0 else []))
                    tl = sched.op("vector", lambda V, bank=bank, jj=jj, j=j: V.tensor_tensor(
                        out=qbuf_lo[:, j:j + 1], in0=bank[:, jj:jj + 1],
                        in1=qbuf_hi[:, j:j + 1], op=ALU.subtract), deps=[th])
                    rbank.did_read(tl)
                    xt_toks.append(tl)
            rqb2.did_write(xt_toks[-1])

            for og in range(4):
                for dc in range(NDC):
                    buf = dc % 2
                    sp = stream_rhs_piece(wi["l0f_agg"], dc * 128, og * 512, 512, buf)
                    t_mm = mm3(psA[0:1, og * 512:og * 512 + 512], rpsA[og],
                               qbuf_hi[:, dc:dc + 1], qbuf_lo[:, dc:dc + 1],
                               whi[buf][:, 0:512], wlo[buf][:, 0:512],
                               start=(dc == 0), stop=(dc == NDC - 1),
                               deps=[sp] + (xt_toks[-1:] if dc == 0 else []), mark=True)
                    rwsp[buf].did_read(t_mm)
                    rqb2.did_read(t_mm)
            ev_toks = []
            for og in range(4):
                t = sched.op("vector", lambda V, og=og: V.tensor_copy(
                    out=ev32[0:1, og * 512:og * 512 + 512],
                    in_=psA[0:1, og * 512:og * 512 + 512]),
                    deps=rpsA[og].for_read() + [rpsA[og].w] + rev.for_write())
                rpsA[og].did_read(t)
                ev_toks.append(t)
            rev.did_write(ev_toks[-1])
            th_toks = []
            for og in range(4):
                sh, sl = split_piece(evh[0:1, 0:512], evl[0:1, 0:512],
                                     ev32[0:1, og * 512:og * 512 + 512],
                                     deps=ev_toks + revh.for_write())
                revh.did_write(sl)
                m1 = sched.op("tensor", lambda T, og=og: T.matmul(
                    psB[0][0:4, 0:512], bsel1[0:1, 0:4], evh[0:1, 0:512],
                    start=True, stop=False), deps=[sh, sl, t_consts] + rpsB[0].for_write(),
                    inc=False)
                m2 = sched.op("tensor", lambda T, og=og: T.matmul(
                    psB[0][0:4, 0:512], bsel1[0:1, 0:4], evl[0:1, 0:512],
                    start=False, stop=True), inc=True)
                rpsB[0].did_write(m2)
                revh.did_read(m2)
                t_ab = sched.dma("gpsimd", vbuf32[0:4, 0:512],
                                 wi["l0f_aggb_my"][:, og * 512:og * 512 + 512], s_qs,
                                 deps=rvbuf.for_write())
                rvbuf.did_write(t_ab)
                t_th = sched.op("vector", lambda V, og=og: V.tensor_tensor(
                    out=scr2[0:4, og * 512:og * 512 + 512], in0=psB[0][0:4, 0:512],
                    in1=vbuf32[0:4, 0:512], op=ALU.add),
                    deps=[m2, t_ab] + rscr2.for_write())
                rpsB[0].did_read(t_th)
                rvbuf.did_read(t_th)
                th_toks.append(t_th)
            rscr2.did_write(th_toks[-1])

            def ar_thought(tag, mid=None):
                t_up = sched.dma("gpsimd", cc_in[:, :], scr2[0:4, 0:2048], s_kvw,
                                 deps=rscr2.for_read() + [rscr2.w] + rcc.for_write())
                rcc.did_write(t_up)
                ccs = {}
                def do_cc(G, t_up=t_up):
                    G.collective_compute(
                        "AllReduce", ALU.add, ins=[cc_in[:, :]], outs=[cc_out[:, :]],
                        replica_groups=[list(range(NCORE))],
                    ).then_inc(ccsem, 1)
                RT["cc"] = RT.get("cc", 0) + 1
                sched.raw("gpsimd", do_cc, deps=[t_up])
                if mid is not None:
                    mid()
                t_dn = sched.dma("gpsimd", scr2[0:4, 0:2048], cc_out[:, :], s_kvw,
                                 deps=[(ccsem, RT["cc"])] + rscr2.for_write())
                rscr2.did_write(t_dn)
                rcc.did_read(t_dn)
                return t_dn

            banks = [psA[:, 0:512], psA[:, 512:1024], psA[:, 1024:1536],
                     psA[:, 1536:2048], psB[0][:], psB[1][:], psT[0][:], psT[1][:]]
            rbanks = rpsA + rpsB + rpsT

            def gate_og_mm(og):
                t_gb = sched.dma("gpsimd", kbuf_lo32[0:1, 0:512],
                                 wi["l0f_gateb"][0:1, og * 512:og * 512 + 512],
                                 s_qs, deps=rkbuf.for_write() + rkbuf.for_read())
                rkbuf.did_write(t_gb)
                sh, sl = split_piece(evh[0:1, 0:512], evl[0:1, 0:512],
                                     kbuf_lo32[0:1, 0:512],
                                     deps=[t_gb] + revh.for_write())
                revh.did_write(sl)
                rkbuf.did_read(sl)
                tm = None
                for dc in range(NDC):
                    buf = dc % 2
                    sp = stream_rhs_piece(wi["l0f_gate"], dc * 128, og * 512, 512, buf)
                    for i in range(NT):
                        tm = mm3(banks[i], rbanks[i],
                                 featT_hi[:, dc * TPC + i * 128: dc * TPC + i * 128 + 128],
                                 featT_lo[:, dc * TPC + i * 128: dc * TPC + i * 128 + 128],
                                 whi[buf][:, 0:512], wlo[buf][:, 0:512],
                                 start=(dc == 0), stop=False,
                                 deps=[sp] + (featdep if dc == 0 else []),
                                 mark=(i == NT - 1))
                    rwsp[buf].did_read(tm)
                    rfeatT.did_read(tm)
                for i in range(NT):
                    sched.op("tensor", lambda T, i=i: T.matmul(
                        banks[i], onesrow[0:1, :], evh[0:1, 0:512],
                        start=False, stop=False), deps=[sh, sl, t_consts], inc=False)
                    t_f = sched.op("tensor", lambda T, i=i: T.matmul(
                        banks[i], onesrow[0:1, :], evl[0:1, 0:512],
                        start=False, stop=True), inc=True)
                    rbanks[i].did_write(t_f)
                    revh.did_read(t_f)

            def gate_og_evac(og):
                for i in range(NT):
                    t_s = sched.op("scalar", lambda Sc, i=i:
                                   Sc.activation(out=kbuf_hi32[:, 0:512],
                                                 in_=banks[i], func=AF.Sigmoid),
                                   deps=rbanks[i].for_read() + [rbanks[i].w]
                                   + rkbuf.for_write() + rkbuf.for_read())
                    rbanks[i].did_read(t_s)
                    rkbuf.did_write(t_s)
                    t_sp = sched.dma("gpsimd", g_spill[i, :, og * 512:og * 512 + 512],
                                     kbuf_hi32[:, 0:512], s_qs, deps=[t_s])
                    rkbuf.did_read(t_sp)
                    rgsp.did_write(t_sp)

            t_th = ar_thought("agg", mid=lambda: gate_og_mm(0))

            prev_stat = [t10]
            for step in range(4):
                sc = 40
                t1 = sched.op("vector", lambda V: V.reduce_sum(
                    out=stats[0:4, sc:sc + 1], in_=scr2[0:4, 0:2048], axis=AX.X),
                    deps=[t_th] + prev_stat)
                t2 = sched.op("scalar", lambda Sc: Sc.activation(
                    out=stats[0:4, sc:sc + 1], in_=stats[0:4, sc:sc + 1],
                    func=AF.Copy, scale=-1.0 / D), deps=[t1] + prev_stat)
                t3 = sched.op("scalar", lambda Sc: Sc.activation(
                    out=vbuf32[0:4, 0:2048], in_=scr2[0:4, 0:2048], func=AF.Square,
                    bias=stats[0:4, sc:sc + 1], scale=1.0,
                    accum_out=stats[0:4, sc + 1:sc + 2]),
                    deps=[t2] + rvbuf.for_write())
                rvbuf.did_write(t3)
                t4 = sched.op("vector", lambda V: V.tensor_scalar(
                    out=stats[0:4, sc + 2:sc + 3], in0=stats[0:4, sc + 1:sc + 2],
                    scalar1=1.0 / D, scalar2=1e-6, op0=ALU.mult, op1=ALU.add),
                    deps=[t3])
                t5 = sched.op("scalar", lambda Sc: Sc.activation(
                    out=stats[0:4, sc + 3:sc + 4], in_=stats[0:4, sc + 2:sc + 3],
                    func=AF.Sqrt), deps=[t4])
                t6 = sched.op("vector", lambda V: V.reciprocal(
                    out=stats[0:4, sc + 4:sc + 5], in_=stats[0:4, sc + 3:sc + 4]),
                    deps=[t5])
                t7 = sched.op("vector", lambda V: V.tensor_tensor(
                    out=stats[0:4, sc + 5:sc + 6], in0=stats[0:4, sc + 2:sc + 3],
                    in1=stats[0:4, sc + 4:sc + 5], op=ALU.mult), deps=[t6, t4])
                t8 = sched.op("vector", lambda V: V.tensor_tensor(
                    out=stats[0:4, sc + 5:sc + 6], in0=stats[0:4, sc + 5:sc + 6],
                    in1=stats[0:4, sc + 3:sc + 4], op=ALU.add), deps=[t7, t5])
                t9 = sched.op("scalar", lambda Sc: Sc.activation(
                    out=stats[0:4, sc + 5:sc + 6], in_=stats[0:4, sc + 5:sc + 6],
                    func=AF.Copy, scale=0.5), deps=[t8])
                t10 = sched.op("vector", lambda V: V.reciprocal(
                    out=stats[0:4, sc + 6:sc + 7], in_=stats[0:4, sc + 5:sc + 6]),
                    deps=[t9])
                t_tln = sched.op("vector", lambda V: V.tensor_scalar(
                    out=vbuf32[0:4, 0:2048], in0=scr2[0:4, 0:2048],
                    scalar1=stats[0:4, sc:sc + 1], scalar2=stats[0:4, sc + 6:sc + 7],
                    op0=ALU.add, op1=ALU.mult),
                    deps=[t10, t_th, t3] + rvbuf.for_write() + rvbuf.for_read())
                rvbuf.did_write(t_tln)
                t_g1 = sched.dma("gpsimd", ev32[0:4, 0:2048], wi["l0f_png"][:, :],
                                 s_qs, deps=rev.for_write())
                rev.did_write(t_g1)
                t_tln = sched.op("vector", lambda V: V.tensor_tensor(
                    out=vbuf32[0:4, 0:2048], in0=vbuf32[0:4, 0:2048],
                    in1=ev32[0:4, 0:2048], op=ALU.mult), deps=[t_tln, t_g1])
                rev.did_read(t_tln)
                t_g2 = sched.dma("gpsimd", ev32[0:4, 0:2048], wi["l0f_pnb"][:, :],
                                 s_qs, deps=[t_tln] + rev.for_write())
                rev.did_write(t_g2)
                t_tln = sched.op("vector", lambda V: V.tensor_tensor(
                    out=vbuf32[0:4, 0:2048], in0=vbuf32[0:4, 0:2048],
                    in1=ev32[0:4, 0:2048], op=ALU.add), deps=[t_tln, t_g2])
                rev.did_read(t_tln)
                rvbuf.did_write(t_tln)
                gate_og_evac(step)
                t_tT = transpose4(lambda j: vbuf32[0:4, j * 128:(j + 1) * 128], NDC,
                                  qbuf_hi, qbuf_lo, 16, [t_tln])
                rvbuf.did_read(t_tT)
                rqb2.did_write(t_tT)
                for ng in range(2):
                    for dc in range(NDC):
                        buf = dc % 2
                        sp = stream_rhs_piece(wi["l0f_d1"], dc * 128, ng * 512, 512, buf)
                        t_mm = mm3(psA[0:4, ng * 512:ng * 512 + 512], rpsA[ng],
                                   qbuf_hi[:, 16 + dc * 4:16 + dc * 4 + 4],
                                   qbuf_lo[:, 16 + dc * 4:16 + dc * 4 + 4],
                                   whi[buf][:, 0:512], wlo[buf][:, 0:512],
                                   start=(dc == 0), stop=(dc == NDC - 1),
                                   deps=[sp, t_tT], mark=True)
                        rwsp[buf].did_read(t_mm)
                        rqb2.did_read(t_mm)
                tg_toks = []
                for ng in range(2):
                    t_e = sched.op("vector", lambda V, ng=ng: V.tensor_tensor(
                        out=kbuf_hi32[0:4, ng * 512:ng * 512 + 512],
                        in0=psA[0:4, ng * 512:ng * 512 + 512],
                        in1=ev32b[0:4, ng * 512:ng * 512 + 512], op=ALU.add),
                        deps=rpsA[ng].for_read() + [rpsA[ng].w, t_d1b] + rkbuf.for_write())
                    rpsA[ng].did_read(t_e)
                    revb.did_read(t_e)
                    tg_toks.append(t_e)
                rkbuf.did_write(tg_toks[-1])
                t_gel = sched.op("scalar", lambda Sc: Sc.activation(
                    out=kbuf_hi32[0:4, 0:1024], in_=kbuf_hi32[0:4, 0:1024],
                    func=AF.Gelu), deps=tg_toks)
                rkbuf.did_write(t_gel)
                t_gT = transpose4(lambda j: kbuf_hi32[0:4, j * 128:(j + 1) * 128], 8,
                                  qbuf_hi, qbuf_lo, 80, [t_gel])
                rkbuf.did_read(t_gT)
                rqb2.did_write(t_gT)
                t_d2b = sched.dma("gpsimd", ev32[0:4, 0:2048], wi["l0f_d2b"][:, :],
                                  s_qs, deps=rev.for_write())
                rev.did_write(t_d2b)
                t_d2b = sched.op("vector", lambda V: V.tensor_single_scalar(
                    out=ev32[0:4, 0:2048], in_=ev32[0:4, 0:2048], scalar=0.125,
                    op=ALU.mult), deps=[t_d2b])
                rev.did_write(t_d2b)
                pc_toks = []
                for og in range(4):
                    for dc in range(8):
                        buf = dc % 2
                        sp = stream_rhs_piece(wi["l0f_d2"], dc * 128, og * 512, 512, buf)
                        t_mm = mm3(psB[og % 2][0:4, 0:512], rpsB[og % 2],
                                   qbuf_hi[:, 80 + dc * 4:80 + dc * 4 + 4],
                                   qbuf_lo[:, 80 + dc * 4:80 + dc * 4 + 4],
                                   whi[buf][:, 0:512], wlo[buf][:, 0:512],
                                   start=(dc == 0), stop=(dc == 7),
                                   deps=[sp, t_gT], mark=True)
                        rwsp[buf].did_read(t_mm)
                        rqb2.did_read(t_mm)
                    dsth = pT_hi32 if og < 2 else pT_lo32
                    c0 = (og % 2) * 512
                    t_p = sched.op("vector", lambda V, og=og, dsth=dsth, c0=c0:
                                   V.tensor_tensor(out=dsth[0:4, c0:c0 + 512],
                                                   in0=psB[og % 2][0:4, 0:512],
                                                   in1=ev32[0:4, og * 512:og * 512 + 512],
                                                   op=ALU.add),
                                   deps=rpsB[og % 2].for_read() + [rpsB[og % 2].w, t_d2b]
                                   + (rpT.for_write() if og == 0 else []))
                    rpsB[og % 2].did_read(t_p)
                    rev.did_read(t_p)
                    pc_toks.append(t_p)
                rpT.did_write(pc_toks[-1])
                t_up1 = sched.dma("gpsimd", cc_in[:, 0:1024], pT_hi32[0:4, 0:1024],
                                  s_kvw, deps=pc_toks + rcc.for_write())
                t_up2 = sched.dma("gpsimd", cc_in[:, 1024:2048], pT_lo32[0:4, 0:1024],
                                  s_kvw, deps=pc_toks)
                rcc.did_write(t_up2)
                rpT.did_read(t_up2)
                def do_cc2(G):
                    G.collective_compute(
                        "AllReduce", ALU.add, ins=[cc_in[:, :]], outs=[cc_out[:, :]],
                        replica_groups=[list(range(NCORE))],
                    ).then_inc(ccsem, 1)
                RT["cc"] = RT.get("cc", 0) + 1
                sched.raw("gpsimd", do_cc2, deps=[t_up1, t_up2])
                if step < 3:
                    gate_og_mm(step + 1)
                t_dn = sched.dma("gpsimd", vbuf32[0:4, 0:2048], cc_out[:, :], s_kvw,
                                 deps=[(ccsem, RT["cc"])] + rvbuf.for_write())
                rvbuf.did_write(t_dn)
                rcc.did_read(t_dn)
                t_th = sched.op("vector", lambda V: V.tensor_tensor(
                    out=scr2[0:4, 0:2048], in0=scr2[0:4, 0:2048],
                    in1=vbuf32[0:4, 0:2048], op=ALU.add),
                    deps=[t_dn, t_th] + rscr2.for_write() + rscr2.for_read())
                rscr2.did_write(t_th)
                rvbuf.did_read(t_th)
                prev_stat = [t9, t10, t_tln]

            t_thT = transpose4(lambda j: scr2[0:4, j * 128:(j + 1) * 128], NDC,
                               qbuf_hi, qbuf_lo, 16, [t_th])
            rscr2.did_read(t_thT)
            rqb2.did_write(t_thT)
            for dc in range(NDC):
                buf = dc % 2
                sp = stream_rhs_piece(wi["l0f_bc"], dc * 128, 0, 256, buf)
                t_mm = mm3(psB[0][0:4, 0:256], rpsB[0],
                           qbuf_hi[:, 16 + dc * 4:16 + dc * 4 + 4],
                           qbuf_lo[:, 16 + dc * 4:16 + dc * 4 + 4],
                           whi[buf][:, 0:256], wlo[buf][:, 0:256],
                           start=(dc == 0), stop=(dc == NDC - 1),
                           deps=[sp, t_thT], mark=True)
                rwsp[buf].did_read(t_mm)
                rqb2.did_read(t_mm)
            t_bcb = sched.dma("gpsimd", vbuf32[0:4, 0:256], wi["l0f_bcb"][:, :], s_qs,
                              deps=rvbuf.for_write())
            rvbuf.did_write(t_bcb)
            t_mybc = sched.op("vector", lambda V: V.tensor_tensor(
                out=vbuf32[0:4, 256:512], in0=psB[0][0:4, 0:256],
                in1=vbuf32[0:4, 0:256], op=ALU.add),
                deps=rpsB[0].for_read() + [rpsB[0].w, t_bcb])
            rpsB[0].did_read(t_mybc)
            rvbuf.did_write(t_mybc)
            t_z = sched.op("vector", lambda V: V.memset(ev32[0:4, 0:2048], 0.0),
                           deps=rev.for_write())
            rev.did_write(t_z)
            t_z = sched.dma("gpsimd", cc_in[:, :], ev32[0:4, 0:2048], s_kvw,
                            deps=[t_z] + rcc.for_write())
            t_w = sched.dma("gpsimd",
                            lambda: cc_in[:, ds(colv["v"], 256)],
                            vbuf32[0:4, 256:512], s_kvw, deps=[t_z, t_mybc])
            rcc.did_write(t_w)
            rvbuf.did_read(t_w)
            def do_cc3(G):
                G.collective_compute(
                    "AllReduce", ALU.add, ins=[cc_in[:, :]], outs=[cc_out[:, :]],
                    replica_groups=[list(range(NCORE))],
                ).then_inc(ccsem, 1)
            RT["cc"] = RT.get("cc", 0) + 1
            sched.raw("gpsimd", do_cc3, deps=[t_w])
            t_bc = sched.dma("gpsimd", ev32[0:4, 0:2048], cc_out[:, :], s_kvw,
                             deps=[(ccsem, RT["cc"])] + rev.for_write())
            rev.did_write(t_bc)
            rcc.did_read(t_bc)
            bt_toks = []
            for og in range(4):
                sh, sl = split_piece(evh[0:4, 0:512], evl[0:4, 0:512],
                                     ev32[0:4, og * 512:og * 512 + 512],
                                     deps=[t_bc] + revh.for_write())
                revh.did_write(sl)
                rev.did_read(sl)
                m1 = sched.op("tensor", lambda T: T.matmul(
                    psB[1][:, 0:512], bselr[0:4, 0:128], evh[0:4, 0:512],
                    start=True, stop=False),
                    deps=[sh, sl, t_consts] + rpsB[1].for_write(), inc=False)
                m2 = sched.op("tensor", lambda T: T.matmul(
                    psB[1][:, 0:512], bselr[0:4, 0:128], evl[0:4, 0:512],
                    start=False, stop=True), inc=True)
                rpsB[1].did_write(m2)
                revh.did_read(m2)
                dsth = pT_hi32 if og < 2 else pT_lo32
                c0 = (og % 2) * 512
                t_b = sched.op("vector", lambda V, dsth=dsth, c0=c0: V.tensor_copy(
                    out=dsth[:, c0:c0 + 512], in_=psB[1][:, 0:512]),
                    deps=[m2] + (rpT.for_write() if og == 0 else []))
                rpsB[1].did_read(t_b)
                bt_toks.append(t_b)
            rpT.did_write(bt_toks[-1])


            t_og1 = sched.dma("sync", kbuf_hi32[:, 0:1024], wi["l0f_outg"][:, 0:1024],
                              s_rope, deps=rkbuf.for_write())
            t_og2 = sched.dma("sync", kbuf_lo32[:, 0:1024], wi["l0f_outg"][:, 1024:2048],
                              s_rope)
            t_ob = sched.dma("sync", vbuf32[:, 0:2048], wi["l0f_outb"][:, :], s_rope,
                             deps=rvbuf.for_write() + rvbuf.for_read())
            rkbuf.did_write(t_og2)
            rvbuf.did_write(t_ob)
            prev_ep = prev_stat
            for i in range(NT):
                t_g = sched.dma("gpsimd", scr2[:, 0:2048], g_spill[i, :, :], s_qs,
                                deps=rgsp.for_read() + [rgsp.w]
                                + rscr2.for_write() + rscr2.for_read())
                rgsp.did_read(t_g)
                rscr2.did_write(t_g)
                t_x = sched.op("vector", lambda V, i=i: V.tensor_scalar(
                    out=ev32[:, 0:2048], in0=xres[i][:], scalar1=stats[:, i:i + 1],
                    scalar2=stats[:, 8 + i:9 + i], op0=ALU.add, op1=ALU.mult),
                    deps=rxres[i].for_read() + rstats.for_read() + rev.for_write()
                    + rev.for_read())
                rxres[i].did_read(t_x)
                rev.did_write(t_x)
                for hb, dsth in ((0, pT_hi32), (1, pT_lo32)):
                    t_m = sched.op("vector", lambda V, hb=hb, dsth=dsth:
                                   V.tensor_tensor(out=ev32b[:, 0:1024],
                                                   in0=scr2[:, hb * 1024:hb * 1024 + 1024],
                                                   in1=dsth[:, 0:1024], op=ALU.mult),
                                   deps=[t_g] + rpT.for_read() + [rpT.w]
                                   + revb.for_write() + revb.for_read())
                    rscr2.did_read(t_m)
                    rpT.did_read(t_m)
                    revb.did_write(t_m)
                    t_a = sched.op("vector", lambda V, hb=hb: V.tensor_tensor(
                        out=ev32[:, hb * 1024:hb * 1024 + 1024],
                        in0=ev32[:, hb * 1024:hb * 1024 + 1024],
                        in1=ev32b[:, 0:1024], op=ALU.add), deps=[t_m, t_x])
                    revb.did_read(t_a)
                    rev.did_write(t_a)
                sc = 48
                t1 = sched.op("vector", lambda V: V.reduce_sum(
                    out=stats[:, sc:sc + 1], in_=ev32[:, 0:2048], axis=AX.X),
                    deps=rev.for_read() + [rev.w] + prev_ep)
                t2 = sched.op("scalar", lambda Sc: Sc.activation(
                    out=stats[:, sc:sc + 1], in_=stats[:, sc:sc + 1],
                    func=AF.Copy, scale=-1.0 / D), deps=[t1] + prev_ep)
                t3 = sched.op("scalar", lambda Sc: Sc.activation(
                    out=scr2[:, 0:2048], in_=ev32[:, 0:2048], func=AF.Square,
                    bias=stats[:, sc:sc + 1], scale=1.0,
                    accum_out=stats[:, sc + 1:sc + 2]),
                    deps=[t2] + rscr2.for_write() + rscr2.for_read())
                rscr2.did_write(t3)
                t4 = sched.op("vector", lambda V: V.tensor_scalar(
                    out=stats[:, sc + 2:sc + 3], in0=stats[:, sc + 1:sc + 2],
                    scalar1=1.0 / D, scalar2=1e-6, op0=ALU.mult, op1=ALU.add),
                    deps=[t3])
                t5 = sched.op("scalar", lambda Sc: Sc.activation(
                    out=stats[:, sc + 3:sc + 4], in_=stats[:, sc + 2:sc + 3],
                    func=AF.Sqrt), deps=[t4])
                t6 = sched.op("vector", lambda V: V.reciprocal(
                    out=stats[:, sc + 4:sc + 5], in_=stats[:, sc + 3:sc + 4]),
                    deps=[t5])
                t7 = sched.op("vector", lambda V: V.tensor_tensor(
                    out=stats[:, sc + 5:sc + 6], in0=stats[:, sc + 2:sc + 3],
                    in1=stats[:, sc + 4:sc + 5], op=ALU.mult), deps=[t6, t4])
                t8 = sched.op("vector", lambda V: V.tensor_tensor(
                    out=stats[:, sc + 5:sc + 6], in0=stats[:, sc + 5:sc + 6],
                    in1=stats[:, sc + 3:sc + 4], op=ALU.add), deps=[t7, t5])
                t9 = sched.op("scalar", lambda Sc: Sc.activation(
                    out=stats[:, sc + 5:sc + 6], in_=stats[:, sc + 5:sc + 6],
                    func=AF.Copy, scale=0.5), deps=[t8])
                t10 = sched.op("vector", lambda V: V.reciprocal(
                    out=stats[:, sc + 6:sc + 7], in_=stats[:, sc + 5:sc + 6]),
                    deps=[t9])
                t_ap = sched.op("vector", lambda V: V.tensor_scalar(
                    out=ev32[:, 0:2048], in0=ev32[:, 0:2048],
                    scalar1=stats[:, sc:sc + 1], scalar2=stats[:, sc + 6:sc + 7],
                    op0=ALU.add, op1=ALU.mult), deps=[t10] + rev.for_read() + [rev.w])
                rev.did_write(t_ap)
                for hb in range(2):
                    src_g = kbuf_hi32 if hb == 0 else kbuf_lo32
                    t_ap2 = sched.op("vector", lambda V, hb=hb, src_g=src_g:
                                     V.tensor_tensor(out=ev32[:, hb * 1024:hb * 1024 + 1024],
                                                     in0=ev32[:, hb * 1024:hb * 1024 + 1024],
                                                     in1=src_g[:, 0:1024], op=ALU.mult),
                                     deps=[t_ap, t_og2])
                    rkbuf.did_read(t_ap2)
                    rev.did_write(t_ap2)
                t_ap3 = sched.op("vector", lambda V: V.tensor_tensor(
                    out=ev32[:, 0:2048], in0=ev32[:, 0:2048], in1=vbuf32[:, 0:2048],
                    op=ALU.add), deps=rev.for_read() + [rev.w, t_ob])
                rvbuf.did_read(t_ap3)
                rev.did_write(t_ap3)
                t_r = sched.op("vector", lambda V, i=i: V.tensor_tensor(
                    out=xres[i][:], in0=xres[i][:], in1=ev32[:, 0:2048], op=ALU.add),
                    deps=[t_ap3] + rxres[i].for_write())
                rev.did_read(t_r)
                rxres[i].did_write(t_r)
                prev_ep = [t9, t10, t_ap]
            rstats.did_write(prev_ep[-1])
            rstats.did_write(prev_ep[0])


        def sub_moe():
            last_feat = build_featT(1e-5)
            featdep = [last_feat]
            banks = [psA[:, 0:512], psA[:, 512:1024], psA[:, 1024:1536],
                     psA[:, 1536:2048], psB[0][:], psB[1][:], psT[0][:], psT[1][:]]
            rbanks = rpsA + rpsB + rpsT

            for dc in range(NDC):
                buf = dc % 2
                sp = stream_rhs_piece(wi["l1f_wg"], dc * 128, 0, 8, buf)
                tm = None
                for i in range(NT):
                    tm = mm3(banks[i].tensor[0:128, banks[i].offset % 2048:
                                             banks[i].offset % 2048 + 8]
                             if False else
                             (psA[:, i * 512:i * 512 + 8] if i < 4 else
                              [psB[0][:, 0:8], psB[1][:, 0:8], psT[0][:, 0:8],
                               psT[1][:, 0:8]][i - 4]),
                             rbanks[i],
                             featT_hi[:, dc * TPC + i * 128: dc * TPC + i * 128 + 128],
                             featT_lo[:, dc * TPC + i * 128: dc * TPC + i * 128 + 128],
                             whi[buf][:, 0:8], wlo[buf][:, 0:8],
                             start=(dc == 0), stop=(dc == NDC - 1),
                             deps=[sp] + (featdep if dc == 0 else []),
                             mark=(i == NT - 1))
                rwsp[buf].did_read(tm)
                rfeatT.did_read(tm)
            for i in range(NT):
                bank8 = (psA[:, i * 512:i * 512 + 8] if i < 4 else
                         [psB[0][:, 0:8], psB[1][:, 0:8], psT[0][:, 0:8],
                          psT[1][:, 0:8]][i - 4])
                t_mx = sched.op("vector", lambda V, bank8=bank8: V.reduce_max(
                    out=arow[:, 0:1], in_=bank8, axis=AX.X, negate=True),
                    deps=rbanks[i].for_read() + [rbanks[i].w] + rarow.for_write())
                rbanks[i].did_read(t_mx)
                rarow.did_write(t_mx)
                t_exp = sched.op("scalar", lambda Sc, bank8=bank8: Sc.activation(
                    out=arow[:, 4:12], in_=bank8, func=AF.Exp,
                    bias=arow[:, 0:1], scale=1.0, accum_out=arow[:, 1:2]),
                    deps=[t_mx])
                rbanks[i].did_read(t_exp)
                rarow.did_write(t_exp)
                t_rec = sched.op("vector", lambda V: V.reciprocal(
                    out=arow[:, 2:3], in_=arow[:, 1:2]), deps=[t_exp])
                rarow.did_write(t_rec)
                t_m2 = sched.op("vector", lambda V: V.reduce_max(
                    out=arow[:, 3:4], in_=arow[:, 4:12], axis=AX.X), deps=[t_exp])
                rarow.did_write(t_m2)
                t_eq = sched.op("vector", lambda V: V.tensor_single_scalar(
                    out=arow[:, 4:12], in_=arow[:, 4:12], scalar=arow[:, 3:4],
                    op=ALU.is_ge), deps=[t_m2])
                rarow.did_write(t_eq)
                t_t1 = sched.op("vector", lambda V: V.tensor_tensor(
                    out=arow[:, 4:12], in0=arow[:, 4:12], in1=iota8_sb[:, 8:16],
                    op=ALU.mult), deps=[t_eq, t_consts])
                rarow.did_write(t_t1)
                t_cand = sched.op("vector", lambda V: V.tensor_scalar(
                    out=arow[:, 4:12], in0=arow[:, 4:12], scalar1=-1.0, scalar2=9.0,
                    op0=ALU.mult, op1=ALU.add), deps=[t_t1])
                rarow.did_write(t_cand)
                t_idx = sched.op("vector", lambda V: V.tensor_reduce(
                    out=arow[:, 3:4], in_=arow[:, 4:12], op=ALU.min, axis=AX.X),
                    deps=[t_cand])
                rarow.did_write(t_idx)
                t_oh = sched.op("vector", lambda V: V.tensor_single_scalar(
                    out=arow[:, 4:12], in_=iota8_sb[:, 0:8], scalar=arow[:, 3:4],
                    op=ALU.is_equal), deps=[t_idx])
                rarow.did_write(t_oh)
                t_cb = sched.op("vector", lambda V, i=i: V.tensor_single_scalar(
                    out=moe_c[:, i * 8:i * 8 + 8], in_=arow[:, 4:12],
                    scalar=arow[:, 2:3], op=ALU.mult),
                    deps=[t_oh, t_rec] + rmoec.for_write())
                rarow.did_read(t_cb)
                rmoec.did_write(t_cb)

            for e in range(8):
                t_b = sched.dma("gpsimd", ev32[0:1, 0:2048],
                                wi["l1f_b"][e:e + 1, :], s_qs,
                                deps=rev.for_write() + rev.for_read())
                rev.did_write(t_b)
                for og in range(4):
                    sh, sl = split_piece(evh[0:1, 0:512], evl[0:1, 0:512],
                                         ev32[0:1, og * 512:og * 512 + 512],
                                         deps=[t_b] + revh.for_write())
                    revh.did_write(sl)
                    rev.did_read(sl)
                    for dc in range(NDC):
                        buf = dc % 2
                        sp = stream_rhs_piece(wi["l1f_w"], 0, 0, 512, buf,
                                              flat_base=e * D * D + dc * 128 * D + og * 512,
                                              dout=D)
                        tm = None
                        for i in range(NT):
                            tm = mm3(banks[i], rbanks[i],
                                     featT_hi[:, dc * TPC + i * 128: dc * TPC + i * 128 + 128],
                                     featT_lo[:, dc * TPC + i * 128: dc * TPC + i * 128 + 128],
                                     whi[buf][:, 0:512], wlo[buf][:, 0:512],
                                     start=(dc == 0), stop=False,
                                     deps=[sp] + (featdep if dc == 0 and e == 0 else []),
                                     mark=(i == NT - 1))
                        rwsp[buf].did_read(tm)
                        rfeatT.did_read(tm)
                    for i in range(NT):
                        sched.op("tensor", lambda T, i=i: T.matmul(
                            banks[i], onesrow[0:1, :], evh[0:1, 0:512],
                            start=False, stop=False), deps=[sh, sl, t_consts], inc=False)
                        t_f = sched.op("tensor", lambda T, i=i: T.matmul(
                            banks[i], onesrow[0:1, :], evl[0:1, 0:512],
                            start=False, stop=True), inc=True)
                        rbanks[i].did_write(t_f)
                        revh.did_read(t_f)
                    for i in range(NT):
                        t = sched.op("vector", lambda V, i=i, og=og, e=e, bank=banks[i]:
                                     V.scalar_tensor_tensor(
                                         out=xres[i][:, og * 512:og * 512 + 512],
                                         in0=bank, scalar=moe_c[:, i * 8 + e:i * 8 + e + 1],
                                         op0=ALU.mult,
                                         in1=xres[i][:, og * 512:og * 512 + 512],
                                         op1=ALU.add),
                                     deps=rbanks[i].for_read() + [rbanks[i].w]
                                     + rxres[i].for_write() + rmoec.for_read())
                        rbanks[i].did_read(t)
                        rmoec.did_read(t)
                        rxres[i].did_write(t)


        def sub_attn_fp8(layer):
            last_feat = build_featT(1e-5)
            featdep = [last_feat]
            rscl = res("scl")
            evh8 = evh[:].bitcast(FP8)
            vbufI = vbuf[:].bitcast(I32)
            RSQ = float(np.float32(1.0) / np.float32(np.sqrt(np.float32(HD))))

            t_z = sched.op("vector", lambda V: V.memset(arow[:, 8:16], 0.0),
                           deps=rarow.for_write() + rarow.for_read())
            rarow.did_write(t_z)

            def track_absmax(src_ap, col, deps):
                t1 = sched.op("vector", lambda V: V.reduce_max(
                    out=arow[:, 12:13], in_=src_ap, axis=AX.X,
                    apply_absolute_value=True),
                    deps=list(deps) + rarow.for_write())
                rarow.did_write(t1)
                t2 = sched.op("vector", lambda V, col=col: V.tensor_tensor(
                    out=arow[:, col:col + 1], in0=arow[:, col:col + 1],
                    in1=arow[:, 12:13], op=ALU.max), deps=[t1, t_z])
                rarow.did_write(t2)
                return t2

            def proj_spill(w_ap, region, col, hook=None):
                sp_next = stream_lhsT_slab(w_ap, 0, 0, extra_deps=featdep)
                for oc in range(NDC):
                    buf = oc % 2
                    sp = sp_next
                    mm_last = []
                    for dc in range(NDC):
                        for tg in range(2):
                            t = mm3(psA[:, tg * 512:tg * 512 + 512], rpsA[tg],
                                    whi[buf][:, dc * 128:dc * 128 + 128],
                                    wlo[buf][:, dc * 128:dc * 128 + 128],
                                    featT_hi[:, dc * TPC + tg * 512: dc * TPC + tg * 512 + 512],
                                    featT_lo[:, dc * TPC + tg * 512: dc * TPC + tg * 512 + 512],
                                    start=(dc == 0), stop=(dc == NDC - 1),
                                    deps=([sp] + featdep) if dc == 0 else ())
                            if dc == NDC - 1:
                                mm_last.append(t)
                    rwsp[buf].did_read(mm_last[-1])
                    rfeatT.did_read(mm_last[-1])
                    if oc + 1 < NDC:
                        sp_next = stream_lhsT_slab(w_ap, oc + 1, (oc + 1) % 2,
                                                   extra_deps=featdep)
                    t_ev = []
                    for tg in range(2):
                        t = sched.op("vector", lambda V, tg=tg: V.tensor_copy(
                            out=ev32[:, tg * 512:tg * 512 + 512],
                            in_=psA[:, tg * 512:tg * 512 + 512]),
                            deps=[mm_last[tg]] + rev.for_write() + rev.for_read())
                        rpsA[tg].did_read(t)
                        t_ev.append(t)
                    rev.did_write(t_ev[-1])
                    t_am = track_absmax(ev32[:, 0:1024], col, t_ev)
                    t_sp = sched.dma("gpsimd", qkv_f32[region, oc, :, :],
                                     ev32[:, 0:1024], s_qs, deps=t_ev)
                    rev.did_read(t_sp)
                    rqsp.did_write(t_sp)
                    if hook is not None:
                        hook(oc)

            prev_ccm = [None]

            def finalize_scale(trow, col, recip_col, scale_col):
                t_tr = sched.op("tensor", lambda T, col=col: T.transpose(
                    psT[0][0:1, 0:128], arow[:, col:col + 1], ident[:]),
                    deps=rarow.for_read() + [rarow.w, t_consts] + rpsT[0].for_write())
                rarow.did_read(t_tr)
                rpsT[0].did_write(t_tr)
                t_rm = sched.op("vector", lambda V: V.reduce_max(
                    out=arow[0:1, 13:14], in_=psT[0][0:1, 0:128], axis=AX.X),
                    deps=[t_tr] + rarow.for_write())
                rpsT[0].did_read(t_rm)
                rarow.did_write(t_rm)
                t_up = sched.dma("gpsimd", ccm_in[0:1, :], arow[0:1, 13:14], s_kvw,
                                 deps=[t_rm]
                                 + ([prev_ccm[0]] if prev_ccm[0] is not None else []))
                rarow.did_read(t_up)

                def do_ccm(G):
                    G.collective_compute(
                        "AllReduce", ALU.max, ins=[ccm_in[:, :]], outs=[ccm_out[:, :]],
                        replica_groups=[list(range(NCORE))],
                    ).then_inc(ccsem, 1)
                RT["cc"] = RT.get("cc", 0) + 1
                sched.raw("gpsimd", do_ccm, deps=[t_up])
                t_dn = sched.dma("gpsimd", arow[0:1, 13:14], ccm_out[0:1, :], s_kvw,
                                 deps=[(ccsem, RT["cc"])] + rarow.for_write())
                prev_ccm[0] = t_dn
                rarow.did_write(t_dn)
                t_s = sched.op("vector", lambda V: V.tensor_single_scalar(
                    out=arow[0:1, 14:15], in_=arow[0:1, 13:14],
                    scalar=float(np.float32(1.0) / np.float32(448.0)),
                    op=ALU.mult), deps=[t_dn])
                rarow.did_write(t_s)
                t_r = sched.op("vector", lambda V: V.reciprocal(
                    out=arow[0:1, 15:16], in_=arow[0:1, 14:15]), deps=[t_s])
                rarow.did_write(t_r)
                t_b = sched.dma("gpsimd", bsc[trow:trow + 1, 1:3], arow[0:1, 14:16],
                                s_kvw, deps=[t_r])
                rarow.did_read(t_b)
                bts = []
                for colv_, flat in ((recip_col, trow * 3 + 2),
                                    (scale_col, trow * 3 + 1)):
                    t = sched.dma("gpsimd", scl[:, colv_:colv_ + 1],
                                  mkap(bsc, flat, [[0, 128], [1, 1]]), s_kvr,
                                  deps=[t_b] + rscl.for_write())
                    bts.append(t)
                rscl.did_write(bts[-1])
                return bts[-1]

            kb8 = kbuf_hi[:].bitcast(FP8)

            def mk_quant(region, recip_col, writer, scale_tok, db=False):
                def emit_tile(tt):
                    if db and tt % 2 == 1:
                        big, rbig = ev32, rev
                        f8, rf8 = evh8, revh
                        mg, rmg = ev32b[:, 0:1024], revb
                    else:
                        big, rbig = scr2, rscr2
                        f8, rf8 = kb8[:, 0:1024], rkbuf
                        mg, rmg = kbuf_lo32[:, 0:1024], rkbuf
                    t0 = sched.dma("gpsimd", big[:, 0:1024], qkv_f32[region, tt, :, :],
                                   s_qr, deps=rqsp.for_read() + [rqsp.w]
                                   + rbig.for_write() + rbig.for_read())
                    rqsp.did_read(t0)
                    rbig.did_write(t0)
                    t1 = sched.op("vector", lambda V, big=big: V.tensor_single_scalar(
                        out=big[:, 0:1024], in_=big[:, 0:1024],
                        scalar=scl[:, recip_col:recip_col + 1], op=ALU.mult),
                        deps=[t0, scale_tok])
                    rscl.did_read(t1)
                    rbig.did_write(t1)
                    t2 = sched.op("scalar", lambda Sc, big=big, f8=f8: Sc.activation(
                        out=f8, in_=big[:, 0:1024], func=AF.Copy,
                        scale=0.5), deps=[t1] + rf8.for_write() + rf8.for_read())
                    rf8.did_write(t2)
                    t3 = sched.op("scalar", lambda Sc, big=big, f8=f8: Sc.activation(
                        out=big[:, 1024:2048], in_=f8, func=AF.Copy,
                        scale=2.0), deps=[t2])
                    rbig.did_write(t3)
                    rf8.did_read(t3)
                    t4 = sched.op("vector", lambda V, big=big, mg=mg: V.tensor_scalar(
                        out=mg, in0=big[:, 0:1024], scalar1=24576.0,
                        scalar2=-24576.0, op0=ALU.add, op1=ALU.add),
                        deps=[t1] + rmg.for_write() + rmg.for_read())
                    rmg.did_write(t4)
                    t5 = sched.op("vector", lambda V, big=big: V.tensor_single_scalar(
                        out=vbufI[:, 0:1024], in_=big[:, 0:1024].bitcast(I32),
                        scalar=0x7FFFFFFF, op=ALU.bitwise_and),
                        deps=[t1] + rvbuf.for_write() + rvbuf.for_read())
                    rvbuf.did_write(t5)
                    t6 = sched.op("vector", lambda V: V.tensor_single_scalar(
                        out=vbufI[:, 0:1024], in_=vbuf32[:, 0:1024], scalar=0.03125,
                        op=ALU.is_lt), deps=[t5])
                    rvbuf.did_write(t6)
                    t7 = sched.op("vector", lambda V, big=big, mg=mg: V.copy_predicated(
                        out=big[:, 1024:2048], mask=vbufI[:, 0:1024],
                        data=mg), deps=[t3, t4, t6])
                    rvbuf.did_read(t7)
                    rmg.did_read(t7)
                    rbig.did_write(t7)
                    t8 = sched.op("scalar", lambda Sc, big=big: Sc.activation(
                        out=qbuf_hi[:], in_=big[:, 1024:2048], func=AF.Copy),
                        deps=[t7] + rqbuf.for_write())
                    rbig.did_read(t8)
                    rqbuf.did_write(t8)
                    writer(tt, t8)
                return emit_tile

            def wq8(tt, tok):
                t = sched.dma("gpsimd", q_spill[0, tt, :, :], qbuf_hi[:], s_qs,
                              deps=[tok])
                rqbuf.did_read(t)
                rqsp.did_write(t)

            def wk8(tt, tok):
                t = sched.dma("gpsimd", kt_hi_st[tt * 128:(tt + 1) * 128, :],
                              qbuf_hi[:], s_kvw, deps=[tok])
                rqbuf.did_read(t)
                rkv.did_write(t)

            def wv8(tt, tok):
                i, half = tt // 2, tt % 2
                t = sched.dma("gpsimd",
                              vp_st[i * 128:(i + 1) * 128, half * 8:half * 8 + 8, 0:128],
                              qbuf_hi[:], s_kvw, deps=[tok])
                t2 = sched.dma("gpsimd",
                               vp_st[i * 128:(i + 1) * 128, half * 8:half * 8 + 8, 128:256],
                               qbuf_hi[:], s_kvw, deps=[tok])
                rqbuf.did_read(t)
                rqbuf.did_read(t2)
                rkv.did_write(t2)

            t_prevread = (kv_done, RT["kv_rsig"]) if RT["kv_rsig"] else None
            rkv.did_write(t_prevread) if t_prevread else None

            proj_spill(layer["wq"], 0, 8)
            tok_q = finalize_scale(0, 8, 0, 5)
            qq = mk_quant(0, 0, wq8, tok_q)
            proj_spill(layer["wk"], 1, 9,
                       hook=lambda oc: qq(oc - 3) if oc >= 3 else None)
            tok_k = finalize_scale(1, 9, 1, 6)
            t_m = sched.op("vector", lambda V: V.tensor_tensor(
                out=scl[:, 3:4], in0=scl[:, 5:6], in1=scl[:, 6:7], op=ALU.mult),
                deps=[tok_k])
            t_m = sched.op("vector", lambda V: V.tensor_single_scalar(
                out=scl[:, 3:4], in_=scl[:, 3:4], scalar=RSQ, op=ALU.mult),
                deps=[t_m])
            rscl.did_write(t_m)
            kk = mk_quant(1, 1, wk8, tok_k)
            pending_q = [lambda j=j: qq(j) for j in range(13, 16)] \
                + [lambda j=j: kk(j) for j in range(16)]

            banks = [psA[:, 0:512], psA[:, 512:1024], psA[:, 1024:1536],
                     psA[:, 1536:2048], psB[0][:], psB[1][:], psT[0][:], psT[1][:]]
            rbanks = rpsA + rpsB + rpsT
            for og in range(4):
                for dc in range(NDC):
                    buf = dc % 2
                    sp = stream_rhs_piece(layer["wv"], dc * 128, og * 512, 512, buf)
                    tm = None
                    for i in range(NT):
                        tm = mm3(banks[i], rbanks[i],
                                 featT_hi[:, dc * TPC + i * 128: dc * TPC + i * 128 + 128],
                                 featT_lo[:, dc * TPC + i * 128: dc * TPC + i * 128 + 128],
                                 whi[buf][:, 0:512], wlo[buf][:, 0:512],
                                 start=(dc == 0), stop=(dc == NDC - 1),
                                 deps=[sp], mark=(i == NT - 1))
                    rwsp[buf].did_read(tm)
                    rfeatT.did_read(tm)
                    if dc % 2 == 0 and not (og == 0 and dc < 8) and pending_q:
                        pending_q.pop(0)()
                for i in range(NT):
                    t_e = sched.op("vector", lambda V, i=i: V.tensor_copy(
                        out=ev32b[:, 0:512], in_=banks[i]),
                        deps=rbanks[i].for_read() + [rbanks[i].w]
                        + revb.for_write() + revb.for_read())
                    rbanks[i].did_read(t_e)
                    revb.did_write(t_e)
                    t_am = track_absmax(ev32b[:, 0:512], 10, [t_e])
                    t_sp = sched.dma("gpsimd",
                                     qkv_f32[2, i * 2 + og // 2, :,
                                             (og % 2) * 512:(og % 2) * 512 + 512],
                                     ev32b[:, 0:512], s_qs, deps=[t_e])
                    revb.did_read(t_sp)
                    rqsp.did_write(t_sp)

            while pending_q:
                pending_q.pop(0)()
            tok_v = finalize_scale(2, 10, 2, 4)
            vv = mk_quant(2, 2, wv8, tok_v, db=True)
            for _j in range(16):
                vv(_j)

            kvst_tok = (s_kvw, sched.dma_cnt[id(s_kvw)])
            tb1 = sched.dma("gpsimd", lambda: dyn["kt_hi"][0, :, :],
                            kt_hi_st[:, :], s_kvb, deps=[kvst_tok] + rkv.for_write())
            tb3 = sched.dma("gpsimd", lambda: dyn["vp"][0, :, :, :],
                            vp_st[:, :, :], s_kvb, deps=[kvst_tok])
            rkv.did_write(tb3)
            kvw_tok = tb3
            RT["prep"] += 1
            pv = RT["prep"]

            def signal(G, pv=pv):
                ins = G.remote_sem_update_broadcast(
                    kv_ready, lsem, rdests=[(0, 1)] + [None] * 7,
                    sim_remote_sem_inc=2)
                ins.then_inc(prep_sem, 1)
                G.wait_ge(prep_sem, pv)
                G.trigger_dma(count=1)

            sched.raw("gpsimd", signal, deps=[kvw_tok])
            RT["kv_sig"] += 2
            t_peer = (kv_ready, RT["kv_sig"])

            def emit_av8(ctx):
                rc, colp, deps0, pt_done_p = ctx
                t_av = None
                for ct in range(16):
                    d0 = (pt_done_p + deps0) if ct == 0 else ()
                    d0 = list(d0) + (rpsB[0].for_write() if ct == 0 else [])
                    sched.op("tensor", lambda T, ct=ct: T.matmul(
                        psB[0][:, 0:128],
                        pT_hi[:, ct * 128:ct * 128 + 128],
                        vbuf[:, ct * 256:ct * 256 + 128],
                        start=(ct == 0), stop=False), deps=d0, inc=False)
                    t_av = sched.op("tensor", lambda T, ct=ct: T.matmul(
                        psB[0][:, 0:128],
                        pT_lo[:, ct * 128:ct * 128 + 128],
                        vbuf[:, ct * 256:ct * 256 + 128],
                        start=False, stop=(ct == 15)), inc=(ct == 15))
                rpsB[0].did_write(t_av)
                rpT.did_read(t_av)
                rvbuf.did_read(t_av)
                t_o = sched.op("vector", lambda V, rc=rc: V.tensor_scalar(
                    out=ev32b[:, 0:128], in0=psB[0][:, 0:128],
                    scalar1=arow[:, rc:rc + 1], scalar2=scl[:, 4:5],
                    op0=ALU.mult, op1=ALU.mult),
                    deps=[t_av] + revb.for_write() + revb.for_read())
                rpsB[0].did_read(t_o)
                rarow.did_read(t_o)
                rscl.did_read(t_o)
                revb.did_write(t_o)
                bank = psT[0]; rbank = rpsT[0]
                t_tr2 = sched.op("tensor", lambda T, bank=bank: T.transpose(
                    bank[:, 0:128], ev32b[:, 0:128], ident[:]),
                    deps=[t_o, t_consts] + rbank.for_write())
                revb.did_read(t_tr2)
                rbank.did_write(t_tr2)
                t_hi = sched.op("vector", lambda V, colp=colp, bank=bank: V.tensor_copy(
                    out=featT_hi[:, colp:colp + 128], in_=bank[:, 0:128]),
                    deps=[t_tr2] + rfeatT.for_write())
                t_lo = sched.op("vector", lambda V, colp=colp, bank=bank:
                                V.tensor_tensor(out=featT_lo[:, colp:colp + 128],
                                                in0=bank[:, 0:128],
                                                in1=featT_hi[:, colp:colp + 128],
                                                op=ALU.subtract), deps=[t_hi])
                rbank.did_read(t_lo)
                rfeatT.did_write(t_lo)

            av_ctx8 = [None]
            for kvh in range(16):
                kl = []
                for slot in range(2):
                    t = sched.dma("gpsimd",
                                  kbuf_hi[:, slot * 1024:slot * 1024 + 1024],
                                  kt_hi_sh[slot, kvh * 128:(kvh + 1) * 128, :],
                                  s_kvr, deps=[t_peer, kvw_tok] + rkbuf.for_write())
                    kl.append(t)
                rkbuf.did_write(kl[-1])
                vl = []
                for ct in range(16):
                    slot = 0 if ct < 8 else 1
                    lt = ct % 8
                    t = sched.dma("gpsimd",
                                  vbuf[:, ct * 256:ct * 256 + 128],
                                  vp_sh[slot, lt * 128:(lt + 1) * 128, kvh, 0:128],
                                  s_kvv, deps=[t_peer, kvw_tok] + rvbuf.for_write())
                    vl.append(t)
                rvbuf.did_write(vl[-1])
                t_kv = [kl[-1], vl[-1]]
                tq1 = sched.dma("gpsimd", qbuf_hi[:], q_spill[0, kvh, :, :], s_qr,
                                deps=rqsp.for_read() + [rqsp.w] + rqbuf.for_write())
                rqbuf.did_write(tq1)
                for i in range(NT):
                    mm_last = []
                    for kg in range(4):
                        d0 = [tq1] + t_kv + rpsA[kg].for_write()
                        t = sched.op("tensor", lambda T, kg=kg, i=i: T.matmul(
                            psA[:, kg * 512:kg * 512 + 512],
                            qbuf_hi[:, i * 128:i * 128 + 128],
                            kbuf_hi[:, kg * 512:kg * 512 + 512],
                            start=True, stop=True), deps=d0, inc=True)
                        rpsA[kg].did_write(t)
                        mm_last.append(t)
                    rqbuf.did_read(mm_last[-1])
                    rkbuf.did_read(mm_last[-1])
                    mx_toks = []
                    for kg in range(4):
                        tmx = sched.op("vector", lambda V, kg=kg: V.reduce_max(
                            out=arow[:, 4 + kg:5 + kg],
                            in_=psA[:, kg * 512:kg * 512 + 512], axis=AX.X,
                            negate=True), deps=[mm_last[kg]] + rarow.for_write())
                        rpsA[kg].did_read(tmx)
                        rarow.did_write(tmx)
                        mx_toks.append(tmx)
                    t_mx = sched.op("vector", lambda V: V.tensor_reduce(
                        out=arow[:, 0:1], in_=arow[:, 4:8], op=ALU.min,
                        axis=AX.X), deps=mx_toks)
                    rarow.did_write(t_mx)
                    t_bias = sched.op("vector", lambda V: V.tensor_tensor(
                        out=arow[:, 1:2], in0=arow[:, 0:1], in1=scl[:, 3:4],
                        op=ALU.mult), deps=[t_mx, t_m])
                    rscl.did_read(t_bias)
                    rarow.did_write(t_bias)
                    if av_ctx8[0] is not None:
                        emit_av8(av_ctx8[0])
                        av_ctx8[0] = None
                    exp_toks = []
                    for g in range(4):
                        t_e = sched.op("scalar", lambda Sc, g=g: Sc.activation(
                            out=p32[:, g * 512:g * 512 + 512],
                            in_=psA[:, g * 512:g * 512 + 512], func=AF.Exp,
                            bias=arow[:, 1:2], scale=scl[:, 3:4],
                            accum_out=arow[:, 8 + g:9 + g]),
                            deps=[t_bias] + rev.for_write() + rev.for_read())
                        rpsA[g].did_read(t_e)
                        rscl.did_read(t_e)
                        rev.did_write(t_e)
                        rarow.did_write(t_e)
                        exp_toks.append(t_e)
                    t_exp = exp_toks[-1]
                    t_sum = sched.op("vector", lambda V: V.reduce_sum(
                        out=arow[:, 2:3], in_=arow[:, 8:12], axis=AX.X),
                        deps=exp_toks)
                    rarow.did_write(t_sum)
                    rc = 3 if i % 2 == 0 else 12
                    t_rec = sched.op("vector", lambda V, rc=rc: V.reciprocal(
                        out=arow[:, rc:rc + 1], in_=arow[:, 2:3]), deps=[t_sum])
                    rarow.did_write(t_rec)
                    pt_done = []
                    for g in range(4):
                        bank = psT[g % 2]; rbank = rpsT[g % 2]
                        mm = []
                        for jj in range(4):
                            ct = g * 4 + jj
                            t = sched.op("tensor", lambda T, ct=ct, jj=jj, bank=bank:
                                         T.transpose(bank[:, jj * 128:jj * 128 + 128],
                                                     p32[:, ct * 128:ct * 128 + 128],
                                                     ident[:]),
                                         deps=[exp_toks[g], t_consts] + rbank.for_write())
                            rev.did_read(t)
                            mm.append(t)
                        rbank.did_write(mm[-1])
                        t_hi = sched.op("vector", lambda V, g=g, bank=bank: V.tensor_copy(
                            out=pT_hi[:, g * 512:g * 512 + 512], in_=bank[:]),
                            deps=mm + rpT.for_write())
                        t_lo = sched.op("vector", lambda V, g=g, bank=bank: V.tensor_tensor(
                            out=pT_lo[:, g * 512:g * 512 + 512], in0=bank[:],
                            in1=pT_hi[:, g * 512:g * 512 + 512], op=ALU.subtract),
                            deps=[t_hi])
                        rbank.did_read(t_lo)
                        pt_done.append(t_lo)
                    rpT.did_write(pt_done[-1])
                    av_ctx8[0] = (rc, kvh * TPC + i * 128,
                                  [tq1] + list(t_kv) + [t_rec, t_m], pt_done)
                if av_ctx8[0] is not None:
                    emit_av8(av_ctx8[0])
                    av_ctx8[0] = None

            rd_toks = [(s_kvr, sched.dma_cnt.get(id(s_kvr), 0)),
                       (s_kvv, sched.dma_cnt.get(id(s_kvv), 0))]
            RT["prep"] += 1
            pv2 = RT["prep"]

            def signal_rd(G, pv=pv2):
                ins = G.remote_sem_update_broadcast(
                    kv_done, lsem, rdests=[(0, 1)] + [None] * 7,
                    sim_remote_sem_inc=2)
                ins.then_inc(prep_sem, 1)
                G.wait_ge(prep_sem, pv)
                G.trigger_dma(count=1)

            sched.raw("gpsimd", signal_rd, deps=rd_toks)
            rkv.did_read(rd_toks[0])
            rkv.did_read(rd_toks[1])
            RT["kv_rsig"] += 2

            bo = layer["bo"]
            t_bo = sched.dma("gpsimd", ev32b[0:1, 0:1024], bo[0:1, 0:1024], s_qs,
                             deps=revb.for_write() + revb.for_read())
            t_bo2 = sched.dma("gpsimd", ev32[0:1, 0:1024], bo[0:1, 1024:2048], s_qs,
                              deps=rev.for_write() + rev.for_read())
            rev.did_write(t_bo2)
            revb.did_write(t_bo)
            attn_dep = rfeatT.for_read() + [rfeatT.w]
            for og in range(4):
                bsrc = ev32b[0:1, og * 512:og * 512 + 512] if og < 2 else \
                    ev32[0:1, (og - 2) * 512:(og - 2) * 512 + 512]
                sh, sl = split_piece(evh[0:1, 0:512], evl[0:1, 0:512], bsrc,
                                     deps=[t_bo, t_bo2] + revh.for_write())
                revh.did_write(sl)
                (revb if og < 2 else rev).did_read(sl)
                for dc in range(NDC):
                    buf = dc % 2
                    sp = stream_rhs_piece(layer["wo"], dc * 128, og * 512, 512, buf)
                    tm = None
                    for i in range(NT):
                        tm = mm3(banks[i], rbanks[i],
                                 featT_hi[:, dc * TPC + i * 128: dc * TPC + i * 128 + 128],
                                 featT_lo[:, dc * TPC + i * 128: dc * TPC + i * 128 + 128],
                                 whi[buf][:, 0:512], wlo[buf][:, 0:512],
                                 start=(dc == 0), stop=False,
                                 deps=[sp] + (attn_dep if dc == 0 else []),
                                 mark=(i == NT - 1))
                    rwsp[buf].did_read(tm)
                    rfeatT.did_read(tm)
                for i in range(NT):
                    sched.op("tensor", lambda T, i=i: T.matmul(
                        banks[i], onesrow[0:1, :], evh[0:1, 0:512],
                        start=False, stop=False), deps=[sh, sl, t_consts], inc=False)
                    t_f = sched.op("tensor", lambda T, i=i: T.matmul(
                        banks[i], onesrow[0:1, :], evl[0:1, 0:512],
                        start=False, stop=True), inc=True)
                    rbanks[i].did_write(t_f)
                    revh.did_read(t_f)
                for i in range(NT):
                    t = sched.op("vector", lambda V, i=i, og=og, bank=banks[i]:
                                 V.tensor_tensor(out=xres[i][:, og * 512:og * 512 + 512],
                                                 in0=bank,
                                                 in1=xres[i][:, og * 512:og * 512 + 512],
                                                 op=ALU.add),
                                 deps=rbanks[i].for_read() + [rbanks[i].w]
                                 + rxres[i].for_write())
                    rbanks[i].did_read(t)
                    rxres[i].did_write(t)

        L0A = {"wq": wi["l0a_q"], "wk": wi["l0a_k"], "wv": wi["l0a_v"],
               "wo": wi["l0a_o"], "nkv": 4, "rope": "a"}
        L0B = {"wq": wi["l0b_q"], "wk": wi["l0b_k"], "wv": wi["l0b_v"],
               "wo": wi["l0b_o"], "nkv": 4, "rope": "b"}
        if n_sub >= 1:
            sub_attn(L0A)
        if n_sub >= 2:
            sub_ffn()
        if n_sub >= 3:
            sub_attn(L0B)
        if n_sub >= 4:
            sub_ffn()
        L1A = {"wq": wi["l1a_q"], "wk": wi["l1a_k"], "wv": wi["l1a_v"],
               "wo": wi["l1a_o"], "nkv": 16, "rope": None, "alibi": True,
               "bo": wi["l1a_bo"]}
        if n_sub >= 5:
            sub_attn(L1A)
        if n_sub >= 6:
            sub_moe()
        L1B = {"wq": wi["l1b_qkv_q"], "wk": wi["l1b_qkv_k"], "wv": wi["l1b_qkv_v"],
               "wo": wi["l1b_o"], "bo": wi["l1b_bo"]}
        if n_sub >= 7:
            sub_attn_fp8(L1B)
        if n_sub >= 8:
            sub_moe()

        bar = sched.barrier()
        st = [sched.dma("sync", out_h[i * 128:(i + 1) * 128, :], xres[i][:], s_out,
                        deps=bar) for i in range(NT)]
        sched.raw("sync", lambda E: E.wait_ge(s_out, 16 * NT), deps=st)

        with nc.Block() as block:
            sched.finalize(block)
    nc.compile()
    return nc


def _prep_inputs(inputs):
    cos, sin = rope_tables()
    sin_a = sin.copy()
    sin_a[:, 0:64] = -sin_a[:, 0:64]
    cosT = np.ascontiguousarray(cos.T)
    sinaT = np.ascontiguousarray(sin_a.T)
    sinbT = np.ascontiguousarray(sin.T)

    ident = np.eye(128, dtype=np.float32)
    iota8 = np.zeros((128, 16), dtype=np.float32)
    iota8[:, 0:8] = np.arange(8, dtype=np.float32)[None, :]
    iota8[:, 8:16] = 9.0 - np.arange(8, dtype=np.float32)[None, :]

    f32 = lambda k: np.asarray(inputs[k], dtype=np.float32)
    rep = lambda v, n: np.ascontiguousarray(np.broadcast_to(
        np.asarray(v, dtype=np.float32)[None, :], (n, len(v))))
    shared = {
        "ident": ident,
        "iota8": iota8,
        "ones4": np.ones((1, 4), dtype=BF),
        "onesrow": np.ones((1, 128), dtype=BF),
        "l0a_q": lhsT_layout(np.ascontiguousarray(f32("L0_attn_a_Wq").T)),
        "l0a_k": lhsT_layout(np.ascontiguousarray(f32("L0_attn_a_Wk").T)),
        "l0a_v": np.ascontiguousarray(f32("L0_attn_a_Wv").T),
        "l0a_o": np.ascontiguousarray(f32("L0_attn_a_Wo").T),
        "l0b_q": lhsT_layout(f32("L0_attn_b_Wq")),
        "l0b_k": lhsT_layout(f32("L0_attn_b_Wk")),
        "l0b_v": f32("L0_attn_b_Wv"),
        "l0b_o": f32("L0_attn_b_Wo"),
        "l0f_ing": rep(f32("L0_ffn_in_gamma"), 128),
        "l0f_inb": rep(f32("L0_ffn_in_beta"), 128),
        "l0f_png": rep(f32("L0_ffn_pn_gamma"), 4),
        "l0f_pnb": rep(f32("L0_ffn_pn_beta"), 4),
        "l0f_outg": rep(f32("L0_ffn_out_gamma"), 128),
        "l0f_outb": rep(f32("L0_ffn_out_beta"), 128),
        "l0f_agg": f32("L0_ffn_agg_w"),
        "l0f_d2b": rep(f32("L0_ffn_d2_b"), 4),
        "l0f_gate": f32("L0_ffn_gate_w"),
        "l0f_gateb": f32("L0_ffn_gate_b")[None, :],
        "l1a_q": lhsT_layout(f32("L1_attn_a_Wq")),
        "l1a_k": lhsT_layout(f32("L1_attn_a_Wk")),
        "l1a_v": f32("L1_attn_a_Wv"),
        "l1a_o": f32("L1_attn_a_Wo"),
        "l1a_bo": f32("L1_attn_a_bo")[None, :],
        "l1f_wg": f32("L1_ffn_wg"),
        "l1f_w": np.ascontiguousarray(f32("L1_ffn_W").transpose(0, 2, 1)),
        "l1f_b": f32("L1_ffn_b"),
        "l1b_qkv_q": lhsT_layout(np.ascontiguousarray(f32("L1_attn_b_W_qkv")[:, 0:2048])),
        "l1b_qkv_k": lhsT_layout(np.ascontiguousarray(f32("L1_attn_b_W_qkv")[:, 2048:4096])),
        "l1b_qkv_v": np.ascontiguousarray(f32("L1_attn_b_W_qkv")[:, 4096:6144]),
        "l1b_o": f32("L1_attn_b_W_out"),
        "l1b_bo": f32("L1_attn_b_b_out")[None, :],
    }

    x = f32("L0_attn_a_x")

    in_maps = []
    for c in range(NCORE):
        b, h = c // 2, c % 2
        blocks = [h + 2 * i for i in range(NT)]
        rows = np.concatenate([np.arange(w * 128, w * 128 + 128) for w in blocks])
        m = dict(shared)
        m["x"] = np.ascontiguousarray(x[b].reshape(S, D)[rows])
        m["coreinfo"] = np.array([[h, 1 - h, b, c, 0, 0, 0, 0]], dtype=np.int32)
        m["ropeC"] = np.ascontiguousarray(cosT[:, rows])
        m["ropeSa"] = np.ascontiguousarray(sinaT[:, rows])
        m["ropeSb"] = np.ascontiguousarray(sinbT[:, rows])
        m["alibi"] = alibi_core_table(h)
        m["masks"] = np.array([[1.0 if h == 1 else 0.0] + [0.0] * 7], dtype=np.float32)
        m["coreinfo"][0, 4] = c * 256
        bsel = np.zeros((4, 128), dtype=BF)
        bsel[b, :] = BF(1.0)
        m["bsel"] = bsel
        bsel1 = np.zeros((1, 4), dtype=BF)
        bsel1[0, b] = BF(1.0)
        m["bsel1"] = bsel1
        aggb_my = np.zeros((4, D), dtype=np.float32)
        if h == 1:
            aggb_my[b, :] = f32("L0_ffn_agg_b")
        m["l0f_aggb_my"] = aggb_my
        m["l0f_d1"] = np.ascontiguousarray(f32("L0_ffn_d1_w")[:, c * 1024:(c + 1) * 1024])
        m["l0f_d1b"] = rep(f32("L0_ffn_d1_b")[c * 1024:(c + 1) * 1024], 4)
        m["l0f_d2"] = np.ascontiguousarray(f32("L0_ffn_d2_w")[c * 1024:(c + 1) * 1024, :])
        m["l0f_bc"] = np.ascontiguousarray(f32("L0_ffn_bc_w")[:, c * 256:(c + 1) * 256])
        m["l0f_bcb"] = rep(f32("L0_ffn_bc_b")[c * 256:(c + 1) * 256], 4)
        for name, (shape, dt) in INPUT_DECLS.items():
            if name not in m:
                m[name] = np.zeros(shape, dtype=mybir.dt.np(dt))
        in_maps.append(m)
    return in_maps


def _scatter_out(results):
    out = np.zeros((B, S, D), dtype=np.float32)
    for c in range(NCORE):
        b, h = c // 2, c % 2
        blocks = [h + 2 * i for i in range(NT)]
        rows = np.concatenate([np.arange(w * 128, w * 128 + 128) for w in blocks])
        out[b].reshape(S, D)[rows] = results[c]["out"]
    return out


def kernel(**inputs):
    nc = build_program(N_SUB)
    in_maps = _prep_inputs(inputs)
    res = run_bass_kernel_spmd(nc, in_maps, core_ids=list(range(NCORE)))
    return _scatter_out(res.results)


# version 08 (final)
# speedup vs baseline: 1.0205x; 1.0205x over previous
import math
import numpy as np
import ml_dtypes
import bass_rust
import concourse.bass as bass
import concourse.bacc as bacc
import concourse.mybir as mybir
from concourse.bass_utils import run_bass_kernel_spmd
from concourse import library_config
from contextlib import ExitStack

F32 = mybir.dt.float32
BF16 = mybir.dt.bfloat16
I32 = mybir.dt.int32
FP8 = mybir.dt.float8e4
AF = mybir.ActivationFunctionType
ALU = mybir.AluOpType
AX = mybir.AxisListType
ds = bass.ds
BF = ml_dtypes.bfloat16

NCORE = 8
B, S, D = 4, 2048, 2048
TPC = 1024
NT = 8
NDC = D // 128
HD = 128
NEG = -1e30
N_SUB = 8

ALIBI_BASE0 = 1920
ALIBI_W = 4352


def mkap(ap, offset, pattern):
    return bass_rust.AP(tensor=ap.tensor, offset=int(offset),
                        ap=[[int(a), int(b)] for a, b in pattern])


class Res:
    __slots__ = ("w", "r")

    def __init__(self):
        self.w = None
        self.r = []

    def for_read(self):
        return [self.w] if self.w else []

    def for_write(self):
        out = list(self.r)
        if self.w:
            out.append(self.w)
        return out

    def did_read(self, tok):
        if tok:
            self.r.append(tok)

    def did_write(self, tok):
        if tok:
            self.w = tok
            self.r = []


class Sched:
    ENGINES = ("sync", "tensor", "vector", "scalar", "gpsimd")

    def __init__(self, nc, es):
        self.nc = nc
        self.q = {e: [] for e in self.ENGINES}
        self.psem = {e: [es.enter_context(nc.semaphore(f"p_{e}0")),
                         es.enter_context(nc.semaphore(f"p_{e}1"))] for e in self.ENGINES}
        self.pcnt = {e: 0 for e in self.ENGINES}
        self.bsem = {e: es.enter_context(nc.semaphore(f"b_{e}")) for e in self.ENGINES}
        self.bcnt = 0
        self.dma_cnt = {}
        self.dma_issued = {e: {} for e in self.ENGINES}
        self.last_wait = {e: {} for e in self.ENGINES}

    def _waits(self, eng, deps):
        lw = self.last_wait[eng]
        best = {}
        for tok in deps:
            if tok is None:
                continue
            semh, val = tok
            k = id(semh)
            if val > best.get(k, (None, -1))[1]:
                best[k] = (semh, val)
        out = []
        for k, (semh, val) in best.items():
            if lw.get(k, -1) >= val:
                continue
            lw[k] = val
            out.append((semh, val))
        return out

    def op(self, eng, emit, deps=(), inc=True):
        waits = self._waits(eng, deps)
        tok = None
        if inc:
            k = self.pcnt[eng]
            self.pcnt[eng] += 1
            tok = (self.psem[eng][k % 2], k // 2 + 1)
        self.q[eng].append(("op", waits, emit, tok))
        return tok

    def mark(self, eng):
        return self.op(eng, lambda E: E.nop(), inc=True)

    def dma(self, eng, out, in_, sem, deps=()):
        waits = self._waits(eng, deps)
        k = id(sem)
        self.dma_cnt[k] = self.dma_cnt.get(k, 0) + 16
        tok = (sem, self.dma_cnt[k])
        self.dma_issued[eng][k] = tok
        self.q[eng].append(("dma", waits, (out, in_), tok))
        return tok

    def raw(self, eng, emit, deps=()):
        waits = self._waits(eng, deps)
        self.q[eng].append(("raw", waits, emit, None))

    def barrier(self):
        self.bcnt += 1
        k = self.bcnt
        for e in self.ENGINES:
            waits = []
            n = self.pcnt[e]
            if (n + 1) // 2 > 0:
                waits.append((self.psem[e][0], (n + 1) // 2))
            if n // 2 > 0:
                waits.append((self.psem[e][1], n // 2))
            waits.extend(self.dma_issued[e].values())
            semb = self.bsem[e]

            def emit(E, waits=tuple(waits), semb=semb):
                for s, v in waits:
                    E.wait_ge(s, v)
                E.nop().then_inc(semb, 1)

            self.raw(e, emit)
        return [(self.bsem[e], k) for e in self.ENGINES]

    def finalize(self, block):
        def mk(eng_name):
            def emit_stream(E):
                for kind, waits, payload, tok in self.q[eng_name]:
                    for semh, val in waits:
                        E.wait_ge(semh, val)
                    if kind == "op":
                        ins = payload(E)
                    elif kind == "dma":
                        out, in_ = payload
                        if callable(out):
                            out = out()
                        if callable(in_):
                            in_ = in_()
                        ins = E.dma_start(out=out, in_=in_)
                    else:
                        payload(E)
                        continue
                    if tok is not None:
                        ins.then_inc(tok[0], 16 if kind == "dma" else 1)
            return emit_stream

        for name in self.ENGINES:
            getattr(block, name)(mk(name))


def rope_tables():
    inv_freq = (1.0 / (10000.0 ** (np.arange(0, HD, 2, dtype=np.float32)
                                   / np.float32(HD)))).astype(np.float32)
    t = np.arange(S, dtype=np.float32)
    freqs = (t[:, None] * inv_freq[None, :]).astype(np.float32)
    emb = np.concatenate([freqs, freqs], axis=-1)
    cos = np.cos(emb.astype(np.float64)).astype(np.float32)
    sin = np.sin(emb.astype(np.float64)).astype(np.float32)
    return cos, sin


def alibi_core_table(h):
    slopes = (2.0 ** (-8.0 * np.arange(1, 17) / 16)).astype(np.float32)
    r = np.arange(30)[:, None, None]
    p = np.arange(128)[None, :, None]
    f = np.arange(128)[None, None, :]
    delta = ((r - 14) * 128 + f - p - h * 128).astype(np.float32)
    tbl = slopes[:, None, None, None] * delta[None]
    tbl = np.where(delta[None] <= 0, tbl, np.float32(NEG)).astype(np.float32)
    return np.ascontiguousarray(tbl)


def lhsT_layout(w):
    din, dout = w.shape
    return np.ascontiguousarray(w.reshape(din, dout // 128, 128).transpose(1, 0, 2))


INPUT_DECLS = {}


def build_program(n_sub=N_SUB):
    INPUT_DECLS.clear()
    nc = bacc.Bacc("TRN2", target_bir_lowering=False, debug=False)

    def din(name, shape, dt=F32):
        INPUT_DECLS[name] = (tuple(shape), dt)
        return nc.dram_tensor(name, shape, dt, kind="ExternalInput").ap()

    x_in = din("x", [TPC, D])
    info_in = din("coreinfo", [1, 8], I32)
    ropeC_in = din("ropeC", [128, TPC])
    ropeSa_in = din("ropeSa", [128, TPC])
    ropeSb_in = din("ropeSb", [128, TPC])
    ident_in = din("ident", [128, 128])
    alibi_in = din("alibi", [16, 30, 128, 128])
    masks_in = din("masks", [1, 8])
    bsel_in = din("bsel", [4, 128], BF16)
    bsel1_in = din("bsel1", [1, 4], BF16)
    ones4_in = din("ones4", [1, 4], BF16)
    ones_in = din("onesrow", [1, 128], BF16)
    iota8_in = din("iota8", [128, 16])

    wi = {}

    def wdef(name, shape, dt=F32):
        wi[name] = din(name, shape, dt)

    wdef("l0a_q", [NDC, D, 128]); wdef("l0a_k", [4, D, 128])
    wdef("l0a_v", [D, 512]); wdef("l0a_o", [D, D])
    wdef("l0f_ing", [128, D]); wdef("l0f_inb", [128, D])
    wdef("l0f_png", [4, D]); wdef("l0f_pnb", [4, D])
    wdef("l0f_outg", [128, D]); wdef("l0f_outb", [128, D])
    wdef("l0f_agg", [D, D]); wdef("l0f_aggb_my", [4, D])
    wdef("l0f_d1", [D, 1024]); wdef("l0f_d1b", [4, 1024])
    wdef("l0f_d2", [1024, D]); wdef("l0f_d2b", [4, D])
    wdef("l0f_bc", [D, 256]); wdef("l0f_bcb", [4, 256])
    wdef("l0f_gate", [D, D]); wdef("l0f_gateb", [1, D])
    wdef("l0b_q", [NDC, D, 128]); wdef("l0b_k", [4, D, 128])
    wdef("l0b_v", [D, 512]); wdef("l0b_o", [D, D])
    wdef("l1a_q", [NDC, D, 128]); wdef("l1a_k", [NDC, D, 128])
    wdef("l1a_v", [D, D]); wdef("l1a_o", [D, D]); wdef("l1a_bo", [1, D])
    wdef("l1f_wg", [D, 8]); wdef("l1f_w", [8, D, D]); wdef("l1f_b", [8, D])
    wdef("l1b_qkv_q", [NDC, D, 128]); wdef("l1b_qkv_k", [NDC, D, 128])
    wdef("l1b_qkv_v", [D, D]); wdef("l1b_o", [D, D]); wdef("l1b_bo", [1, D])

    out_h = nc.dram_tensor("out", [TPC, D], F32, kind="ExternalOutput").ap()

    kt_hi_sh = nc.dram_tensor("kt_hi_sh", [2, 2048, 1024], BF16, kind="Internal",
                              addr_space="Shared").ap()
    kt_lo_sh = nc.dram_tensor("kt_lo_sh", [2, 2048, 1024], BF16, kind="Internal",
                              addr_space="Shared").ap()
    vp_sh = nc.dram_tensor("vp_sh", [2, 1024, 16, 256], BF16, kind="Internal",
                           addr_space="Shared").ap()
    q_spill = nc.dram_tensor("q_spill", [2, NDC, 128, TPC], BF16, kind="Internal").ap()
    kt_hi_st = nc.dram_tensor("kt_hi_st", [2048, 1024], BF16, kind="Internal").ap()
    kt_lo_st = nc.dram_tensor("kt_lo_st", [2048, 1024], BF16, kind="Internal").ap()
    vp_st = nc.dram_tensor("vp_st", [1024, 16, 256], BF16, kind="Internal").ap()
    g_spill = nc.dram_tensor("g_spill", [NT, 128, 2048], F32, kind="Internal").ap()
    qkv_f32 = nc.dram_tensor("qkv_f32", [3, 16, 128, 1024], F32, kind="Internal").ap()
    ccm_in = nc.dram_tensor("ccm_in", [4, 1], F32, kind="Internal").ap()
    ccm_out = nc.dram_tensor("ccm_out", [4, 1], F32, kind="Internal").ap()
    bsc = nc.dram_tensor("bsc", [4, 3], F32, kind="Internal").ap()
    cc_in = nc.dram_tensor("cc_in", [4, 2048], F32, kind="Internal").ap()
    cc_out = nc.dram_tensor("cc_out", [4, 2048], F32, kind="Internal").ap()

    es = ExitStack()
    with es:
        sbuf = lambda name, shape, dt=F32: es.enter_context(nc.sbuf_tensor(name, shape, dt))
        sem = lambda name: es.enter_context(nc.semaphore(name))

        xres = [sbuf(f"xres{i}", [128, D]) for i in range(NT)]
        featT_hi = sbuf("featT_hi", [128, NDC * TPC], BF16)
        featT_lo = sbuf("featT_lo", [128, NDC * TPC], BF16)
        ident = sbuf("ident_sb", [128, 128])
        info = sbuf("info_sb", [1, 8], I32)
        stats = sbuf("stats", [128, 64])
        w32 = [sbuf(f"w32_{i}", [128, 1024]) for i in range(2)]
        whi = [sbuf(f"whi_{i}", [128, 2048], BF16) for i in range(2)]
        wlo = [sbuf(f"wlo_{i}", [128, 2048], BF16) for i in range(2)]
        ev32 = sbuf("ev32", [128, 2048])
        ev32b = sbuf("ev32b", [128, 1024])
        evh = sbuf("evh", [128, 512], BF16)
        evl = sbuf("evl", [128, 512], BF16)
        kbuf_hi = sbuf("kbuf_hi", [128, 2048], BF16)
        kbuf_lo = sbuf("kbuf_lo", [128, 2048], BF16)
        vbuf = sbuf("vbuf", [128, 4096], BF16)
        xln = ev32
        p32 = ev32
        rope_c = kbuf_hi[:].bitcast(F32)
        rope_s = kbuf_lo[:].bitcast(F32)
        qbuf_hi = sbuf("qbuf_hi", [128, TPC], BF16)
        qbuf_lo = sbuf("qbuf_lo", [128, TPC], BF16)
        pT_hi = sbuf("pT_hi", [128, 2048], BF16)
        pT_lo = sbuf("pT_lo", [128, 2048], BF16)
        arow = sbuf("arow", [128, 16])
        scr2 = sbuf("scr2", [128, 2048])
        moe_c = sbuf("moe_c", [128, 64])
        vbuf32 = vbuf[:].bitcast(F32)
        kbuf_hi32 = kbuf_hi[:].bitcast(F32)
        kbuf_lo32 = kbuf_lo[:].bitcast(F32)
        pT_hi32 = pT_hi[:].bitcast(F32)
        pT_lo32 = pT_lo[:].bitcast(F32)

        psA = es.enter_context(nc.psum_tensor("psA", [128, 2048], F32))
        psB = [es.enter_context(nc.psum_tensor(f"psB{i}", [128, 512], F32)) for i in range(2)]
        psT = [es.enter_context(nc.psum_tensor(f"psT{i}", [128, 512], F32)) for i in range(2)]

        R = {}

        def res(name):
            if name not in R:
                R[name] = Res()
            return R[name]

        rxres = [res(f"xres{i}") for i in range(NT)]
        rfeatT = res("featT"); rstats = res("stats");         rw32 = [res(f"w32_{i}") for i in range(2)]
        rwsp = [res(f"wsp_{i}") for i in range(2)]
        rev = res("ev32"); revb = res("ev32b"); revh = res("evh")
        rkbuf = res("kbuf"); rvbuf = res("vbuf"); rqbuf = res("qbuf")
        rpT = res("pT"); rarow = res("arow")
        rpsA = [res(f"psA{i}") for i in range(4)]
        rpsB = [res(f"psB{i}") for i in range(2)]
        rpsT = [res(f"psT{i}") for i in range(2)]
        rkv = res("kv_sh"); rqsp = res("q_spill"); rgsp = res("g_spill")
        rmoec = res("moe_c")

        s_x = sem("s_x")
        s_w = [sem(f"s_w{i}") for i in range(2)]
        s_kvw = sem("s_kvw"); s_kvr = sem("s_kvr"); s_kvv = sem("s_kvv")
        s_kvb = sem("s_kvb")
        s_qs = sem("s_qs"); s_qr = sem("s_qr")
        s_out = sem("s_out"); s_rope = sem("s_rope"); s_misc = sem("s_misc")
        s_alibi = sem("s_alibi")
        s_rot = sem("s_rot")
        kv_ready = sem("kv_ready")
        kv_done = sem("kv_done")
        prep_sem = sem("prep_sem"); lsem = sem("lsem")
        ccsem = sem("ccsem")

        sched = Sched(nc, es)
        RT = {"kv_sig": 0, "prep": 0, "kv_rsig": 0}
        parv = {}

        onesrow = sbuf("ones_sb", [1, 128], BF16)
        bsel1 = sbuf("bsel1_sb", [1, 4], BF16)
        ones4 = sbuf("ones4_sb", [1, 4], BF16)
        bselr = sbuf("bselr_sb", [4, 128], BF16)
        masks_sb = sbuf("masks_sb", [1, 8])
        iota8_sb = sbuf("iota8_sb", [128, 16])
        scl = sbuf("scl_sb", [128, 8])
        t_info = sched.dma("sync", info[:], info_in[:, :], s_misc)
        t_ident = sched.dma("sync", ident[:], ident_in[:, :], s_misc)
        sched.dma("sync", onesrow[:], ones_in[:, :], s_misc)
        sched.dma("sync", bsel1[:], bsel1_in[:, :], s_misc)
        sched.dma("sync", ones4[:], ones4_in[:, :], s_misc)
        sched.dma("sync", bselr[:], bsel_in[:, :], s_misc)
        sched.dma("sync", iota8_sb[:], iota8_in[:, :], s_misc)
        t_masks = sched.dma("sync", masks_sb[:], masks_in[:, :], s_misc)
        t_consts = t_masks
        t_x = [sched.dma("sync", xres[i][:], x_in[i * 128:(i + 1) * 128, :], s_x)
               for i in range(NT)]
        for i in range(NT):
            rxres[i].did_write(t_x[i])

        colv = {}
        dyn = {}

        def load_parity(G):
            reg = G.alloc_register("par_reg")
            G.load(reg, info[0:1, 0:1])
            parv["v"] = G.snap(reg, min_val=0, max_val=1)
            reg2 = G.alloc_register("col_reg")
            G.load(reg2, info[0:1, 4:5])
            colv["v"] = G.snap(reg2, min_val=0, max_val=2048 - 256,
                               guaranteed_mod_val=256)
            dyn["kt_hi"] = kt_hi_sh[ds(parv["v"], 1)]
            dyn["kt_lo"] = kt_lo_sh[ds(parv["v"], 1)]
            dyn["vp"] = vp_sh[ds(parv["v"], 1)]

        sched.raw("gpsimd", load_parity, deps=[t_consts])

        def ln_stats(eps, srcs, rsrcs):
            deps = []
            for rs in rsrcs:
                deps += rs.for_read()
            tks = []
            for i in range(NT):
                t = sched.op("vector", lambda V, i=i: V.reduce_sum(
                    out=stats[:, i:i + 1], in_=srcs[i][:], axis=AX.X),
                    deps=deps + rstats.for_write())
                rstats.did_write(t)
                rsrcs[i].did_read(t)
                tks.append(t)
            t_negmu = sched.op("scalar", lambda Sc: Sc.activation(
                out=stats[:, 0:8], in_=stats[:, 0:8], func=AF.Copy, scale=-1.0 / D),
                deps=tks + rstats.for_write())
            rstats.did_write(t_negmu)
            vks = []
            for i in range(NT):
                t = sched.op("scalar", lambda Sc, i=i: Sc.activation(
                    out=ev32[:, 0:2048], in_=srcs[i][:], func=AF.Square,
                    bias=stats[:, i:i + 1], scale=1.0,
                    accum_out=stats[:, 16 + i:17 + i]),
                    deps=[t_negmu] + rev.for_write() + rstats.for_write())
                rstats.did_write(t)
                rsrcs[i].did_read(t)
                rev.did_write(t)
                vks.append(t)
            t_var = sched.op("vector", lambda V: V.tensor_scalar(
                out=stats[:, 24:32], in0=stats[:, 16:24], scalar1=1.0 / D,
                scalar2=float(eps), op0=ALU.mult, op1=ALU.add), deps=vks)
            rstats.did_write(t_var)
            t_s0 = sched.op("scalar", lambda Sc: Sc.activation(
                out=stats[:, 32:40], in_=stats[:, 24:32], func=AF.Sqrt), deps=[t_var])
            rstats.did_write(t_s0)
            t_r0 = sched.op("vector", lambda V: V.reciprocal(
                out=stats[:, 40:48], in_=stats[:, 32:40]), deps=[t_s0])
            rstats.did_write(t_r0)
            t_h = sched.op("vector", lambda V: V.tensor_tensor(
                out=stats[:, 48:56], in0=stats[:, 24:32], in1=stats[:, 40:48],
                op=ALU.mult), deps=[t_r0, t_var])
            t_s1 = sched.op("vector", lambda V: V.tensor_tensor(
                out=stats[:, 48:56], in0=stats[:, 48:56], in1=stats[:, 32:40],
                op=ALU.add), deps=[t_h, t_s0])
            t_s1b = sched.op("scalar", lambda Sc: Sc.activation(
                out=stats[:, 48:56], in_=stats[:, 48:56], func=AF.Copy, scale=0.5),
                deps=[t_s1])
            rstats.did_write(t_s1b)
            t_inv = sched.op("vector", lambda V: V.reciprocal(
                out=stats[:, 8:16], in_=stats[:, 48:56]), deps=[t_s1b])
            rstats.did_write(t_inv)
            return t_inv

        def ln_apply(i, srcs, rsrcs):
            t = sched.op("vector", lambda V, i=i: V.tensor_scalar(
                out=xln[:], in0=srcs[i][:], scalar1=stats[:, i:i + 1],
                scalar2=stats[:, 8 + i:9 + i], op0=ALU.add, op1=ALU.mult),
                deps=rsrcs[i].for_read() + rstats.for_read() + rev.for_write())
            rsrcs[i].did_read(t)
            rstats.did_read(t)
            rev.did_write(t)
            return t

        def build_featT(eps, srcs=None, rsrcs=None):
            srcs = srcs or xres
            rsrcs = rsrcs or rxres
            ln_stats(eps, srcs, rsrcs)
            last = None
            for i in range(NT):
                t_ln = ln_apply(i, srcs, rsrcs)
                for g in range(NDC // 4):
                    bank = psT[g % 2]
                    rbank = rpsT[g % 2]
                    mm = []
                    for jj in range(4):
                        j = g * 4 + jj
                        t = sched.op("tensor", lambda T, j=j, jj=jj, bank=bank: T.transpose(
                            bank[:, jj * 128:jj * 128 + 128],
                            xln[:, j * 128:(j + 1) * 128], ident[:]),
                            deps=[t_ln, t_consts] + rbank.for_write() + rfeatT.for_write())
                        rev.did_read(t)
                        mm.append(t)
                    rbank.did_write(mm[-1])
                    for jj in range(4):
                        j = g * 4 + jj
                        col = j * TPC + i * 128
                        t_hi = sched.op("vector", lambda V, bank=bank, jj=jj, col=col:
                                        V.tensor_copy(out=featT_hi[:, col:col + 128],
                                                      in_=bank[:, jj * 128:jj * 128 + 128]),
                                        deps=mm)
                        t_lo = sched.op("vector", lambda V, bank=bank, jj=jj, col=col:
                                        V.tensor_tensor(out=featT_lo[:, col:col + 128],
                                                        in0=bank[:, jj * 128:jj * 128 + 128],
                                                        in1=featT_hi[:, col:col + 128],
                                                        op=ALU.subtract), deps=[t_hi])
                        rbank.did_read(t_lo)
                        last = t_lo
            rfeatT.did_write(last)
            return last

        def split_piece(dst_hi_ap, dst_lo_ap, src_ap, deps):
            t_h = sched.op("scalar", lambda Sc: Sc.activation(
                out=dst_hi_ap, in_=src_ap, func=AF.Copy), deps=deps)
            t_l = sched.op("vector", lambda V: V.tensor_tensor(
                out=dst_lo_ap, in0=src_ap, in1=dst_hi_ap, op=ALU.subtract), deps=[t_h])
            return t_h, t_l

        def stream_lhsT_slab(w_ap, oc, buf, extra_deps=()):
            last = None
            for half in range(2):
                t_d = sched.dma("sync", w32[buf][:, 0:1024],
                                mkap(w_ap, oc * D * 128 + half * 8 * 128 * 128,
                                     [[128, 128], [128 * 128, 8], [1, 128]]),
                                s_w[buf],
                                deps=(list(extra_deps) if half == 0 else []) + rw32[buf].for_write())
                rw32[buf].did_write(t_d)
                t_h, t_l = split_piece(
                    whi[buf][:, half * 1024:half * 1024 + 1024],
                    wlo[buf][:, half * 1024:half * 1024 + 1024],
                    w32[buf][:, 0:1024],
                    deps=[t_d] + rwsp[buf].for_write())
                rw32[buf].did_read(t_l)
                rwsp[buf].did_write(t_l)
                last = t_l
            return last

        def stream_rhs_piece(w_ap, row0, col0, ncols, buf, extra_deps=(),
                             flat_base=None, dout=None):
            dout = dout if dout is not None else w_ap.shape[-1]
            base = flat_base if flat_base is not None else row0 * dout + col0
            t_d = sched.dma("sync", w32[buf][:, 0:ncols],
                            mkap(w_ap, base, [[dout, 128], [1, ncols]]),
                            s_w[buf],
                            deps=list(extra_deps) + rw32[buf].for_write())
            rw32[buf].did_write(t_d)
            t_h, t_l = split_piece(whi[buf][:, 0:ncols], wlo[buf][:, 0:ncols],
                                   w32[buf][:, 0:ncols],
                                   deps=[t_d] + rwsp[buf].for_write())
            rw32[buf].did_read(t_l)
            rwsp[buf].did_write(t_l)
            return t_l

        def mm3(bank_ap, rbank, lT_hi, lT_lo, r_hi, r_lo, start, stop, deps=(),
                mark=False):
            d0 = list(deps)
            if start:
                d0 += rbank.for_write()
            sched.op("tensor", lambda T: T.matmul(bank_ap, lT_hi, r_hi,
                                                  start=start, stop=False),
                     deps=d0, inc=False)
            sched.op("tensor", lambda T: T.matmul(bank_ap, lT_hi, r_lo,
                                                  start=False, stop=False), inc=False)
            t3 = sched.op("tensor", lambda T: T.matmul(bank_ap, lT_lo, r_hi,
                                                       start=False, stop=stop),
                          inc=stop or mark)
            if stop:
                rbank.did_write(t3)
            return t3

        def sub_attn(layer):
            nkv = layer["nkv"]
            last_feat = build_featT(1e-5)
            featdep = [last_feat]

            t_rc = t_rs = None
            if layer["rope"]:
                t_rc = sched.dma("sync", rope_c, ropeC_in[:, :], s_rope,
                                 deps=rkbuf.for_write())
                t_rs = sched.dma("sync", rope_s,
                                 (ropeSa_in if layer["rope"] == "a" else ropeSb_in)[:, :],
                                 s_rope, deps=rkbuf.for_write())
                rkbuf.did_write(t_rs)

            def proj_featmajor(w_ap, noc, writer):
                sp_next = stream_lhsT_slab(w_ap, 0, 0, extra_deps=featdep)
                for oc in range(noc):
                    buf = oc % 2
                    sp = sp_next
                    mm_last = []
                    for dc in range(NDC):
                        for tg in range(2):
                            t = mm3(psA[:, tg * 512:tg * 512 + 512], rpsA[tg],
                                    whi[buf][:, dc * 128:dc * 128 + 128],
                                    wlo[buf][:, dc * 128:dc * 128 + 128],
                                    featT_hi[:, dc * TPC + tg * 512: dc * TPC + tg * 512 + 512],
                                    featT_lo[:, dc * TPC + tg * 512: dc * TPC + tg * 512 + 512],
                                    start=(dc == 0), stop=(dc == NDC - 1),
                                    deps=([sp] + featdep) if dc == 0 else ())
                            if dc == NDC - 1:
                                mm_last.append(t)
                    rwsp[buf].did_read(mm_last[-1])
                    rfeatT.did_read(mm_last[-1])
                    if oc + 1 < noc:
                        sp_next = stream_lhsT_slab(w_ap, oc + 1, (oc + 1) % 2,
                                                   extra_deps=featdep)
                    t_ev = []
                    for tg in range(2):
                        t = sched.op("vector", lambda V, tg=tg: V.tensor_copy(
                            out=ev32[:, tg * 512:tg * 512 + 512],
                            in_=psA[:, tg * 512:tg * 512 + 512]),
                            deps=[mm_last[tg]] + rev.for_write())
                        rpsA[tg].did_read(t)
                        t_ev.append(t)
                    rev.did_write(t_ev[-1])
                    if layer["rope"]:
                        if layer["rope"] == "a":
                            t_r1 = sched.dma("gpsimd", ev32b[0:64, 0:1024],
                                             ev32[64:128, 0:1024], s_rot,
                                             deps=t_ev + revb.for_write())
                            t_r2 = sched.dma("gpsimd", ev32b[64:128, 0:1024],
                                             ev32[0:64, 0:1024], s_rot, deps=t_ev)
                        else:
                            t_r1 = sched.dma("gpsimd", ev32b[1:128, 0:1024],
                                             ev32[0:127, 0:1024], s_rot,
                                             deps=t_ev + revb.for_write())
                            t_r2 = sched.dma("gpsimd", ev32b[0:1, 0:1024],
                                             ev32[127:128, 0:1024], s_rot, deps=t_ev)
                        rev.did_read(t_r1); rev.did_read(t_r2)
                        revb.did_write(t_r2)
                        t_m1 = sched.op("vector", lambda V: V.tensor_tensor(
                            out=ev32[:, 0:1024], in0=ev32[:, 0:1024], in1=rope_c,
                            op=ALU.mult), deps=t_ev + [t_rs, t_r1, t_r2])
                        t_m2 = sched.op("vector", lambda V: V.tensor_tensor(
                            out=ev32b[:, 0:1024], in0=ev32b[:, 0:1024], in1=rope_s,
                            op=ALU.mult), deps=[t_r1, t_r2, t_rs])
                        rkbuf.did_read(t_m2)
                        revb.did_write(t_m2)
                        t_fin = sched.op("vector", lambda V: V.tensor_tensor(
                            out=ev32[:, 0:1024], in0=ev32[:, 0:1024], in1=ev32b[:, 0:1024],
                            op=ALU.add), deps=[t_m1, t_m2])
                        rev.did_write(t_fin)
                        revb.did_read(t_fin)
                        src_dep = [t_fin]
                    else:
                        src_dep = t_ev
                    writer(oc, src_dep)

            def q_writer(oc, src_dep):
                t_h, t_l = split_piece(qbuf_hi[:], qbuf_lo[:], ev32[:, 0:1024],
                                       deps=src_dep + rqbuf.for_write())
                rev.did_read(t_l)
                rqbuf.did_write(t_l)
                t1 = sched.dma("gpsimd", q_spill[0, oc, :, :], qbuf_hi[:], s_qs,
                               deps=[t_h])
                t2 = sched.dma("gpsimd", q_spill[1, oc, :, :], qbuf_lo[:], s_qs,
                               deps=[t_l])
                rqbuf.did_read(t1); rqbuf.did_read(t2)
                rqsp.did_write(t2)

            def k_writer(oc, src_dep):
                t_h, t_l = split_piece(qbuf_hi[:], qbuf_lo[:], ev32[:, 0:1024],
                                       deps=src_dep + rqbuf.for_write())
                rev.did_read(t_l)
                rqbuf.did_write(t_l)
                t1 = sched.dma("gpsimd", kt_hi_st[oc * 128:(oc + 1) * 128, :],
                               qbuf_hi[:], s_kvw, deps=[t_h])
                t2 = sched.dma("gpsimd", kt_lo_st[oc * 128:(oc + 1) * 128, :],
                               qbuf_lo[:], s_kvw, deps=[t_l])
                rqbuf.did_read(t1); rqbuf.did_read(t2)
                rkv.did_write(t2)

            t_prevread = (kv_done, RT["kv_rsig"]) if RT["kv_rsig"] else None
            rkv.did_write(t_prevread) if t_prevread else None

            proj_featmajor(layer["wq"], NDC, q_writer)
            proj_featmajor(layer["wk"], nkv, k_writer)

            banks = [psA[:, 0:512], psA[:, 512:1024], psA[:, 1024:1536],
                     psA[:, 1536:2048], psB[0][:], psB[1][:], psT[0][:], psT[1][:]]
            rbanks = rpsA + rpsB + rpsT
            kvd = nkv * 128
            for og in range(kvd // 512):
                for dc in range(NDC):
                    buf = dc % 2
                    sp = stream_rhs_piece(layer["wv"], dc * 128, og * 512, 512, buf)
                    tm = None
                    for i in range(NT):
                        tm = mm3(banks[i], rbanks[i],
                            featT_hi[:, dc * TPC + i * 128: dc * TPC + i * 128 + 128],
                            featT_lo[:, dc * TPC + i * 128: dc * TPC + i * 128 + 128],
                            whi[buf][:, 0:512], wlo[buf][:, 0:512],
                            start=(dc == 0), stop=(dc == NDC - 1),
                            deps=[sp] + (featdep if dc == 0 else []),
                            mark=(i == NT - 1))
                    rwsp[buf].did_read(tm)
                    rfeatT.did_read(tm)
                for i in range(NT):
                    t_h, t_l = split_piece(evh[:], evl[:], banks[i],
                                           deps=rbanks[i].for_read() + revh.for_write())
                    rbanks[i].did_read(t_l)
                    revh.did_write(t_l)
                    t1 = sched.dma("gpsimd",
                                   vp_st[i * 128:(i + 1) * 128, og * 4:og * 4 + 4, 0:128],
                                   evh[:], s_kvw, deps=[t_h])
                    t2 = sched.dma("gpsimd",
                                   vp_st[i * 128:(i + 1) * 128, og * 4:og * 4 + 4, 128:256],
                                   evl[:], s_kvw, deps=[t_l])
                    revh.did_read(t1); revh.did_read(t2)
                    rkv.did_write(t2)

            kvst_tok = (s_kvw, sched.dma_cnt[id(s_kvw)])
            tb1 = sched.dma("gpsimd", lambda: dyn["kt_hi"][0, 0:kvd, :],
                            kt_hi_st[0:kvd, :], s_kvb, deps=[kvst_tok] + rkv.for_write())
            tb2 = sched.dma("gpsimd", lambda: dyn["kt_lo"][0, 0:kvd, :],
                            kt_lo_st[0:kvd, :], s_kvb, deps=[kvst_tok])
            tb3 = sched.dma("gpsimd", lambda nh=kvd // 128: dyn["vp"][0, :, 0:nh, :],
                            vp_st[:, 0:kvd // 128, :], s_kvb, deps=[kvst_tok])
            rkv.did_write(tb3)
            kvw_tok = tb3
            RT["prep"] += 1
            pv = RT["prep"]

            def signal(G, pv=pv):
                ins = G.remote_sem_update_broadcast(
                    kv_ready, lsem, rdests=[(0, 1)] + [None] * 7,
                    sim_remote_sem_inc=2)
                ins.then_inc(prep_sem, 1)
                G.wait_ge(prep_sem, pv)
                G.trigger_dma(count=1)

            sched.raw("gpsimd", signal, deps=[kvw_tok])
            RT["kv_sig"] += 2
            t_peer = (kv_ready, RT["kv_sig"])

            if layer["rope"] == "a":
                scaling = float(np.float32(HD) ** np.float32(-0.5))
            else:
                scaling = float(np.float32(1.0) / np.float32(np.sqrt(np.float32(HD))))
            rep = 16 // nkv
            for kvh in range(nkv):
                kl = []
                for slot in range(2):
                    for src, dst in ((kt_hi_sh, kbuf_hi), (kt_lo_sh, kbuf_lo)):
                        t = sched.dma("gpsimd",
                                      dst[:, slot * 1024:slot * 1024 + 1024],
                                      src[slot, kvh * 128:(kvh + 1) * 128, :],
                                      s_kvr,
                                      deps=[t_peer, kvw_tok] + rkbuf.for_write())
                        kl.append(t)
                rkbuf.did_write(kl[-1])
                vl = []
                for ct in range(16):
                    slot = 0 if ct < 8 else 1
                    lt = ct % 8
                    t = sched.dma("gpsimd",
                                  vbuf[:, ct * 256:ct * 256 + 256],
                                  vp_sh[slot, lt * 128:(lt + 1) * 128,
                                        kvh, :],
                                  s_kvv,
                                  deps=[t_peer, kvw_tok] + rvbuf.for_write())
                    vl.append(t)
                rvbuf.did_write(vl[-1])
                t_kv = [kl[-1], vl[-1]]

                def emit_av(ctx):
                    slots_p, rc, colp, t_kv_p, pt_done_p = ctx
                    t_av = None
                    for j, ct in enumerate(slots_p):
                        t_av = mm3(psB[0][:, 0:128], rpsB[0],
                                   pT_hi[:, j * 128:j * 128 + 128],
                                   pT_lo[:, j * 128:j * 128 + 128],
                                   vbuf[:, ct * 256:ct * 256 + 128],
                                   vbuf[:, ct * 256 + 128:ct * 256 + 256],
                                   start=(j == 0), stop=(j == len(slots_p) - 1),
                                   deps=(pt_done_p + t_kv_p) if j == 0 else ())
                    rpT.did_read(t_av)
                    rvbuf.did_read(t_av)
                    t_o = sched.op("vector", lambda V, rc=rc: V.tensor_single_scalar(
                        out=ev32b[:, 0:128], in_=psB[0][:, 0:128],
                        scalar=arow[:, rc:rc + 1], op=ALU.mult),
                        deps=[t_av] + revb.for_write() + revb.for_read())
                    rpsB[0].did_read(t_o)
                    rarow.did_read(t_o)
                    revb.did_write(t_o)
                    bank = psT[0]; rbank = rpsT[0]
                    t_tr = sched.op("tensor", lambda T, bank=bank: T.transpose(
                        bank[:, 0:128], ev32b[:, 0:128], ident[:]),
                        deps=[t_o, t_consts] + rbank.for_write())
                    revb.did_read(t_tr)
                    rbank.did_write(t_tr)
                    t_hi = sched.op("vector", lambda V, colp=colp, bank=bank: V.tensor_copy(
                        out=featT_hi[:, colp:colp + 128], in_=bank[:, 0:128]),
                        deps=[t_tr] + rfeatT.for_write())
                    t_lo = sched.op("vector", lambda V, colp=colp, bank=bank:
                                    V.tensor_tensor(out=featT_lo[:, colp:colp + 128],
                                                    in0=bank[:, 0:128],
                                                    in1=featT_hi[:, colp:colp + 128],
                                                    op=ALU.subtract), deps=[t_hi])
                    rbank.did_read(t_lo)
                    rfeatT.did_write(t_lo)

                av_ctx = [None]
                for r in range(rep):
                    h = kvh * rep + r
                    tq1 = sched.dma("gpsimd", qbuf_hi[:], q_spill[0, h, :, :], s_qr,
                                    deps=rqsp.for_read() + rqbuf.for_write())
                    tq2 = sched.dma("gpsimd", qbuf_lo[:], q_spill[1, h, :, :], s_qr,
                                    deps=rqsp.for_read())
                    rqbuf.did_write(tq2)
                    t_q = [tq1, tq2]
                    for i in range(NT):
                        if layer.get("alibi"):
                            groups = [(0, False)] + ([(4, False)] if i >= 4 else []) \
                                + [(8, True)] + ([(12, True)] if i >= 4 else [])
                        else:
                            groups = [(4 * g, False) for g in range(4)]
                        ng = len(groups)
                        w = ng * 512
                        slots = [s0 + j for s0, _ in groups for j in range(4)]
                        mm_last = []
                        mx_toks = []
                        for b, (s0, _) in enumerate(groups):
                            t = mm3(psA[:, b * 512:b * 512 + 512], rpsA[b],
                                    qbuf_hi[:, i * 128:i * 128 + 128],
                                    qbuf_lo[:, i * 128:i * 128 + 128],
                                    kbuf_hi[:, s0 * 128:s0 * 128 + 512],
                                    kbuf_lo[:, s0 * 128:s0 * 128 + 512],
                                    start=True, stop=True,
                                    deps=t_q + t_kv)
                            mm_last.append(t)
                            if not layer.get("alibi"):
                                tmx = sched.op("vector", lambda V, b=b: V.reduce_max(
                                    out=arow[:, 4 + b:5 + b],
                                    in_=psA[:, b * 512:b * 512 + 512], axis=AX.X,
                                    negate=True), deps=[t] + rarow.for_write())
                                rpsA[b].did_read(tmx)
                                rarow.did_write(tmx)
                                mx_toks.append(tmx)
                        rqbuf.did_read(mm_last[-1])
                        rkbuf.did_read(mm_last[-1])
                        if layer.get("alibi"):
                            badd_toks = []
                            bt = []
                            for b, (s0, peer) in enumerate(groups):
                                bk0 = 2 * s0 if not peer else 2 * (s0 - 8) + 1
                                r0 = bk0 - 2 * i + 14
                                t = sched.dma("scalar",
                                              scr2[:, b * 512:b * 512 + 512],
                                              mkap(alibi_in,
                                                   (kvh * 30 + r0) * 16384,
                                                   [[128, 128], [32768, 4], [1, 128]]),
                                              s_alibi,
                                              deps=rscr2.for_write() + rscr2.for_read())
                                rscr2.did_write(t)
                                bt.append(t)
                            for b in range(ng):
                                t_badd = sched.op("vector", lambda V, b=b: V.tensor_tensor(
                                    out=ev32[:, b * 512:b * 512 + 512],
                                    in0=psA[:, b * 512:b * 512 + 512],
                                    in1=scr2[:, b * 512:b * 512 + 512],
                                    op=ALU.add),
                                    deps=[mm_last[b], bt[b]]
                                    + (rev.for_write() + rev.for_read() if b == 0
                                       else []))
                                rpsA[b].did_read(t_badd)
                                rscr2.did_read(t_badd)
                                rev.did_write(t_badd)
                                badd_toks.append(t_badd)
                                tmx = sched.op("vector", lambda V, b=b: V.reduce_max(
                                    out=arow[:, 4 + b:5 + b],
                                    in_=ev32[:, b * 512:b * 512 + 512], axis=AX.X,
                                    negate=True), deps=[t_badd] + rarow.for_write())
                                rev.did_read(tmx)
                                rarow.did_write(tmx)
                                mx_toks.append(tmx)
                            sc_src = ev32[:, 0:w]
                            sc_dep = badd_toks
                        else:
                            sc_src = psA[:, 0:w]
                            sc_dep = mm_last
                        if layer.get("alibi"):
                            t_mx = sched.op("vector", lambda V, ng=ng: V.tensor_reduce(
                                out=arow[:, 0:1], in_=arow[:, 4:4 + ng], op=ALU.min,
                                axis=AX.X), deps=mx_toks + rarow.for_write())
                            rarow.did_write(t_mx)
                        else:
                            t_mx = sched.op("vector", lambda V, ng=ng: V.tensor_reduce(
                                out=arow[:, 0:1], in_=arow[:, 4:4 + ng], op=ALU.min,
                                axis=AX.X), deps=mx_toks + rarow.for_write())
                            rarow.did_write(t_mx)
                        t_bias = sched.op("scalar", lambda Sc: Sc.activation(
                            out=arow[:, 1:2], in_=arow[:, 0:1], func=AF.Copy,
                            scale=scaling), deps=[t_mx])
                        rarow.did_write(t_bias)
                        if av_ctx[0] is not None:
                            emit_av(av_ctx[0])
                            av_ctx[0] = None
                        exp_toks = []
                        for g in range(ng):
                            esrc = (ev32[:, g * 512:g * 512 + 512] if layer.get("alibi")
                                    else psA[:, g * 512:g * 512 + 512])
                            t_e = sched.op("scalar", lambda Sc, g=g, esrc=esrc:
                                           Sc.activation(
                                out=p32[:, g * 512:g * 512 + 512], in_=esrc,
                                func=AF.Exp, bias=arow[:, 1:2], scale=scaling,
                                accum_out=arow[:, 8 + g:9 + g]),
                                deps=[t_bias] + sc_dep
                                + (rev.for_write() + rev.for_read()
                                   if not layer.get("alibi") else []))
                            rpsA[g].did_read(t_e)
                            rev.did_write(t_e)
                            rarow.did_write(t_e)
                            exp_toks.append(t_e)
                        t_exp = exp_toks[-1]
                        t_sum = sched.op("vector", lambda V, ng=ng: V.reduce_sum(
                            out=arow[:, 2:3], in_=arow[:, 8:8 + ng], axis=AX.X),
                            deps=exp_toks)
                        rarow.did_write(t_sum)
                        rc = 3 if (r * NT + i) % 2 == 0 else 12
                        t_rec = sched.op("vector", lambda V, rc=rc: V.reciprocal(
                            out=arow[:, rc:rc + 1], in_=arow[:, 2:3]), deps=[t_sum])
                        rarow.did_write(t_rec)
                        pt_done = []
                        for g in range(ng):
                            bank = psT[g % 2]; rbank = rpsT[g % 2]
                            mm = []
                            for jj in range(4):
                                ct = g * 4 + jj
                                t = sched.op("tensor", lambda T, ct=ct, jj=jj, bank=bank:
                                             T.transpose(bank[:, jj * 128:jj * 128 + 128],
                                                         p32[:, ct * 128:ct * 128 + 128],
                                                         ident[:]),
                                             deps=[exp_toks[g], t_consts] + rbank.for_write())
                                rev.did_read(t)
                                mm.append(t)
                            rbank.did_write(mm[-1])
                            t_hi = sched.op("vector", lambda V, g=g, bank=bank: V.tensor_copy(
                                out=pT_hi[:, g * 512:g * 512 + 512], in_=bank[:]),
                                deps=mm + rpT.for_write())
                            t_lo = sched.op("vector", lambda V, g=g, bank=bank: V.tensor_tensor(
                                out=pT_lo[:, g * 512:g * 512 + 512], in0=bank[:],
                                in1=pT_hi[:, g * 512:g * 512 + 512], op=ALU.subtract),
                                deps=[t_hi])
                            rbank.did_read(t_lo)
                            pt_done.append(t_lo)
                        rpT.did_write(pt_done[-1])
                        av_ctx[0] = (slots, rc, h * TPC + i * 128,
                                     list(t_q) + list(t_kv) + [t_rec], pt_done)
                if av_ctx[0] is not None:
                    emit_av(av_ctx[0])
                    av_ctx[0] = None

            rd_toks = [(s_kvr, sched.dma_cnt.get(id(s_kvr), 0)),
                       (s_kvv, sched.dma_cnt.get(id(s_kvv), 0))]
            RT["prep"] += 1
            pv2 = RT["prep"]

            def signal_rd(G, pv=pv2):
                ins = G.remote_sem_update_broadcast(
                    kv_done, lsem, rdests=[(0, 1)] + [None] * 7,
                    sim_remote_sem_inc=2)
                ins.then_inc(prep_sem, 1)
                G.wait_ge(prep_sem, pv)
                G.trigger_dma(count=1)

            sched.raw("gpsimd", signal_rd, deps=rd_toks)
            rkv.did_read(rd_toks[0])
            rkv.did_read(rd_toks[1])
            RT["kv_rsig"] += 2

            attn_dep = rfeatT.for_read()
            bo = layer.get("bo")
            if bo is not None:
                t_bo = sched.dma("gpsimd", ev32b[0:1, 0:1024], bo[0:1, 0:1024], s_qs,
                                 deps=revb.for_write() + revb.for_read())
                t_bo2 = sched.dma("gpsimd", ev32b[1:2, 0:1024]
                                  if False else ev32[0:1, 0:1024],
                                  bo[0:1, 1024:2048], s_qs,
                                  deps=rev.for_write() + rev.for_read())
                rev.did_write(t_bo2)
                revb.did_write(t_bo)
            for og in range(4):
                has_bias = bo is not None
                if has_bias:
                    bsrc = ev32b[0:1, og * 512:og * 512 + 512] if og < 2 else \
                        ev32[0:1, (og - 2) * 512:(og - 2) * 512 + 512]
                    bdep = [t_bo, t_bo2]
                    sh, sl = split_piece(evh[0:1, 0:512], evl[0:1, 0:512], bsrc,
                                         deps=bdep + revh.for_write())
                    revh.did_write(sl)
                    (revb if og < 2 else rev).did_read(sl)
                for dc in range(NDC):
                    buf = dc % 2
                    sp = stream_rhs_piece(layer["wo"], dc * 128, og * 512, 512, buf)
                    tm = None
                    for i in range(NT):
                        tm = mm3(banks[i], rbanks[i],
                            featT_hi[:, dc * TPC + i * 128: dc * TPC + i * 128 + 128],
                            featT_lo[:, dc * TPC + i * 128: dc * TPC + i * 128 + 128],
                            whi[buf][:, 0:512], wlo[buf][:, 0:512],
                            start=(dc == 0), stop=(dc == NDC - 1) and not has_bias,
                            deps=[sp] + (attn_dep if dc == 0 else []),
                            mark=(i == NT - 1))
                    rwsp[buf].did_read(tm)
                    rfeatT.did_read(tm)
                if has_bias:
                    for i in range(NT):
                        sched.op("tensor", lambda T, i=i: T.matmul(
                            banks[i], onesrow[0:1, :], evh[0:1, 0:512],
                            start=False, stop=False), deps=[sh, sl, t_consts], inc=False)
                        t_f = sched.op("tensor", lambda T, i=i: T.matmul(
                            banks[i], onesrow[0:1, :], evl[0:1, 0:512],
                            start=False, stop=True), inc=True)
                        rbanks[i].did_write(t_f)
                        revh.did_read(t_f)
                for i in range(NT):
                    t = sched.op("vector", lambda V, i=i, og=og, bank=banks[i]:
                                 V.tensor_tensor(out=xres[i][:, og * 512:og * 512 + 512],
                                                 in0=bank,
                                                 in1=xres[i][:, og * 512:og * 512 + 512],
                                                 op=ALU.add),
                                 deps=rbanks[i].for_read() + [rbanks[i].w] + rxres[i].for_write())
                    rbanks[i].did_read(t)
                    rxres[i].did_write(t)


        rscr2 = res("scr2"); rcc = res("cc")
        rqb2 = rqbuf

        def transpose4(src_ap_fn, nchunks, dst_hi, dst_lo, col0, deps):
            toks = []
            for g in range((nchunks + 3) // 4):
                bank = psT[g % 2]; rbank = rpsT[g % 2]
                mm = []
                for jj in range(min(4, nchunks - g * 4)):
                    j = g * 4 + jj
                    t = sched.op("tensor", lambda T, j=j, jj=jj, bank=bank: T.transpose(
                        bank[:, jj * 4:jj * 4 + 4], src_ap_fn(j), ident[0:4, 0:4]),
                        deps=list(deps) + rbank.for_write() + [t_consts])
                    mm.append(t)
                rbank.did_write(mm[-1])
                for jj in range(min(4, nchunks - g * 4)):
                    j = g * 4 + jj
                    c = col0 + j * 4
                    th = sched.op("vector", lambda V, bank=bank, jj=jj, c=c: V.tensor_copy(
                        out=dst_hi[:, c:c + 4], in_=bank[:, jj * 4:jj * 4 + 4]), deps=mm)
                    tl = sched.op("vector", lambda V, bank=bank, jj=jj, c=c: V.tensor_tensor(
                        out=dst_lo[:, c:c + 4], in0=bank[:, jj * 4:jj * 4 + 4],
                        in1=dst_hi[:, c:c + 4], op=ALU.subtract), deps=[th])
                    rbank.did_read(tl)
                    toks.append(tl)
            return toks[-1]

        def sub_ffn():
            last_feat = build_featT(1e-5)
            featdep = [last_feat]
            par = None

            t_d1b = sched.dma("gpsimd", ev32b[0:4, 0:1024], wi["l0f_d1b"][:, :], s_rot,
                              deps=revb.for_write())
            revb.did_write(t_d1b)

            t_mv1 = sched.dma("gpsimd", ev32[0:1, 0:2048], xres[7][127:128, :], s_rot,
                              deps=rxres[7].for_read() + rev.for_write())
            rxres[7].did_read(t_mv1); rev.did_write(t_mv1)
            t_mv2 = sched.dma("gpsimd", stats[0:1, 36:37], stats[127:128, 7:8], s_rot,
                              deps=rstats.for_read() + rstats.for_write())
            t_mv3 = sched.dma("gpsimd", stats[0:1, 38:39], stats[127:128, 15:16], s_rot)
            rstats.did_write(t_mv3)
            t_row = sched.op("vector", lambda V: V.tensor_scalar(
                out=ev32[0:1, 0:2048], in0=ev32[0:1, 0:2048],
                scalar1=stats[0:1, 36:37], scalar2=stats[0:1, 38:39],
                op0=ALU.add, op1=ALU.mult),
                deps=[t_mv1, t_mv2, t_mv3])
            rstats.did_read(t_row)
            rev.did_write(t_row)
            t1 = sched.op("vector", lambda V: V.reduce_sum(
                out=stats[0:1, 56:57], in_=ev32[0:1, 0:2048], axis=AX.X),
                deps=[t_row] + rstats.for_write())
            rstats.did_write(t1)
            t2 = sched.op("scalar", lambda Sc: Sc.activation(
                out=stats[0:1, 56:57], in_=stats[0:1, 56:57], func=AF.Copy,
                scale=-1.0 / D), deps=[t1])
            t3 = sched.op("scalar", lambda Sc: Sc.activation(
                out=vbuf32[0:1, 0:2048], in_=ev32[0:1, 0:2048], func=AF.Square,
                bias=stats[0:1, 56:57], scale=1.0, accum_out=stats[0:1, 57:58]),
                deps=[t2] + rvbuf.for_write())
            rvbuf.did_write(t3)
            t4 = sched.op("vector", lambda V: V.tensor_scalar(
                out=stats[0:1, 58:59], in0=stats[0:1, 57:58], scalar1=1.0 / D,
                scalar2=1e-6, op0=ALU.mult, op1=ALU.add), deps=[t3])
            t5 = sched.op("scalar", lambda Sc: Sc.activation(
                out=stats[0:1, 59:60], in_=stats[0:1, 58:59], func=AF.Sqrt), deps=[t4])
            t6 = sched.op("vector", lambda V: V.reciprocal(
                out=stats[0:1, 60:61], in_=stats[0:1, 59:60]), deps=[t5])
            t7 = sched.op("vector", lambda V: V.tensor_tensor(
                out=stats[0:1, 61:62], in0=stats[0:1, 58:59], in1=stats[0:1, 60:61],
                op=ALU.mult), deps=[t6, t4])
            t8 = sched.op("vector", lambda V: V.tensor_tensor(
                out=stats[0:1, 61:62], in0=stats[0:1, 61:62], in1=stats[0:1, 59:60],
                op=ALU.add), deps=[t7, t5])
            t9 = sched.op("scalar", lambda Sc: Sc.activation(
                out=stats[0:1, 61:62], in_=stats[0:1, 61:62], func=AF.Copy, scale=0.5),
                deps=[t8])
            t10 = sched.op("vector", lambda V: V.reciprocal(
                out=stats[0:1, 62:63], in_=stats[0:1, 61:62]), deps=[t9])
            t_ing = sched.dma("gpsimd", vbuf32[0:1, 0:1024],
                              wi["l0f_ing"][0:1, 0:1024], s_qs,
                              deps=rvbuf.for_write() + rvbuf.for_read())
            t_ing2 = sched.dma("gpsimd", kbuf_hi32[0:1, 0:1024],
                               wi["l0f_ing"][0:1, 1024:2048], s_qs,
                               deps=rkbuf.for_write())
            t_inb = sched.dma("gpsimd", vbuf32[0:1, 1024:2048],
                              wi["l0f_inb"][0:1, 0:1024], s_qs)
            t_inb2 = sched.dma("gpsimd", kbuf_lo32[0:1, 0:1024],
                               wi["l0f_inb"][0:1, 1024:2048], s_qs)
            rvbuf.did_write(t_inb2)
            rkbuf.did_write(t_inb2)
            t_xnl = sched.op("vector", lambda V: V.tensor_scalar(
                out=ev32[0:1, 0:2048], in0=ev32[0:1, 0:2048],
                scalar1=stats[0:1, 56:57], scalar2=stats[0:1, 62:63],
                op0=ALU.add, op1=ALU.mult), deps=[t10, t3])
            t_xa = sched.op("vector", lambda V: V.tensor_tensor(
                out=ev32[0:1, 0:1024], in0=ev32[0:1, 0:1024], in1=vbuf32[0:1, 0:1024],
                op=ALU.mult), deps=[t_xnl, t_inb2])
            t_xb = sched.op("vector", lambda V: V.tensor_tensor(
                out=ev32[0:1, 1024:2048], in0=ev32[0:1, 1024:2048],
                in1=kbuf_hi32[0:1, 0:1024], op=ALU.mult), deps=[t_xnl, t_inb2])
            t_xa = sched.op("vector", lambda V: V.tensor_tensor(
                out=ev32[0:1, 0:1024], in0=ev32[0:1, 0:1024], in1=vbuf32[0:1, 1024:2048],
                op=ALU.add), deps=[t_xa])
            t_xb = sched.op("vector", lambda V: V.tensor_tensor(
                out=ev32[0:1, 1024:2048], in0=ev32[0:1, 1024:2048],
                in1=kbuf_lo32[0:1, 0:1024], op=ALU.add), deps=[t_xb])
            t_xnl = sched.op("vector", lambda V: V.tensor_copy(
                out=vbuf32[0:1, 0:1], in_=ev32[0:1, 0:1]), deps=[t_xa, t_xb])
            rvbuf.did_read(t_xnl)
            rkbuf.did_read(t_xnl)
            t_xnl = sched.op("vector", lambda V: V.tensor_single_scalar(
                out=ev32[0:1, 0:2048], in_=ev32[0:1, 0:2048],
                scalar=masks_sb[0:1, 0:1], op=ALU.mult), deps=[t_xnl, t_consts])
            rev.did_write(t_xnl)

            xt_toks = []
            for g in range(4):
                bank = psT[g % 2]; rbank = rpsT[g % 2]
                mm = []
                for jj in range(4):
                    j = g * 4 + jj
                    t = sched.op("tensor", lambda T, j=j, jj=jj, bank=bank: T.transpose(
                        bank[:, jj:jj + 1], ev32[0:1, j * 128:(j + 1) * 128],
                        ident[0:1, 0:1]),
                        deps=[t_xnl, t_consts] + rbank.for_write())
                    mm.append(t)
                rbank.did_write(mm[-1])
                for jj in range(4):
                    j = g * 4 + jj
                    th = sched.op("vector", lambda V, bank=bank, jj=jj, j=j: V.tensor_copy(
                        out=qbuf_hi[:, j:j + 1], in_=bank[:, jj:jj + 1]),
                        deps=mm + (rqb2.for_write() if g == 0 and jj == 0 else []))
                    tl = sched.op("vector", lambda V, bank=bank, jj=jj, j=j: V.tensor_tensor(
                        out=qbuf_lo[:, j:j + 1], in0=bank[:, jj:jj + 1],
                        in1=qbuf_hi[:, j:j + 1], op=ALU.subtract), deps=[th])
                    rbank.did_read(tl)
                    xt_toks.append(tl)
            rqb2.did_write(xt_toks[-1])

            for og in range(4):
                for dc in range(NDC):
                    buf = dc % 2
                    sp = stream_rhs_piece(wi["l0f_agg"], dc * 128, og * 512, 512, buf)
                    t_mm = mm3(psA[0:1, og * 512:og * 512 + 512], rpsA[og],
                               qbuf_hi[:, dc:dc + 1], qbuf_lo[:, dc:dc + 1],
                               whi[buf][:, 0:512], wlo[buf][:, 0:512],
                               start=(dc == 0), stop=(dc == NDC - 1),
                               deps=[sp] + (xt_toks[-1:] if dc == 0 else []), mark=True)
                    rwsp[buf].did_read(t_mm)
                    rqb2.did_read(t_mm)
            ev_toks = []
            for og in range(4):
                t = sched.op("vector", lambda V, og=og: V.tensor_copy(
                    out=ev32[0:1, og * 512:og * 512 + 512],
                    in_=psA[0:1, og * 512:og * 512 + 512]),
                    deps=rpsA[og].for_read() + [rpsA[og].w] + rev.for_write())
                rpsA[og].did_read(t)
                ev_toks.append(t)
            rev.did_write(ev_toks[-1])
            th_toks = []
            for og in range(4):
                sh, sl = split_piece(evh[0:1, 0:512], evl[0:1, 0:512],
                                     ev32[0:1, og * 512:og * 512 + 512],
                                     deps=ev_toks + revh.for_write())
                revh.did_write(sl)
                m1 = sched.op("tensor", lambda T, og=og: T.matmul(
                    psB[0][0:4, 0:512], bsel1[0:1, 0:4], evh[0:1, 0:512],
                    start=True, stop=False), deps=[sh, sl, t_consts] + rpsB[0].for_write(),
                    inc=False)
                m2 = sched.op("tensor", lambda T, og=og: T.matmul(
                    psB[0][0:4, 0:512], bsel1[0:1, 0:4], evl[0:1, 0:512],
                    start=False, stop=True), inc=True)
                rpsB[0].did_write(m2)
                revh.did_read(m2)
                t_ab = sched.dma("gpsimd", vbuf32[0:4, 0:512],
                                 wi["l0f_aggb_my"][:, og * 512:og * 512 + 512], s_qs,
                                 deps=rvbuf.for_write())
                rvbuf.did_write(t_ab)
                t_th = sched.op("vector", lambda V, og=og: V.tensor_tensor(
                    out=scr2[0:4, og * 512:og * 512 + 512], in0=psB[0][0:4, 0:512],
                    in1=vbuf32[0:4, 0:512], op=ALU.add),
                    deps=[m2, t_ab] + rscr2.for_write())
                rpsB[0].did_read(t_th)
                rvbuf.did_read(t_th)
                th_toks.append(t_th)
            rscr2.did_write(th_toks[-1])

            def ar_thought(tag, mid=None):
                t_up = sched.dma("gpsimd", cc_in[:, :], scr2[0:4, 0:2048], s_kvw,
                                 deps=rscr2.for_read() + [rscr2.w] + rcc.for_write())
                rcc.did_write(t_up)
                ccs = {}
                def do_cc(G, t_up=t_up):
                    G.collective_compute(
                        "AllReduce", ALU.add, ins=[cc_in[:, :]], outs=[cc_out[:, :]],
                        replica_groups=[list(range(NCORE))],
                    ).then_inc(ccsem, 1)
                RT["cc"] = RT.get("cc", 0) + 1
                sched.raw("gpsimd", do_cc, deps=[t_up])
                if mid is not None:
                    mid()
                t_dn = sched.dma("gpsimd", scr2[0:4, 0:2048], cc_out[:, :], s_kvw,
                                 deps=[(ccsem, RT["cc"])] + rscr2.for_write())
                rscr2.did_write(t_dn)
                rcc.did_read(t_dn)
                return t_dn

            banks = [psA[:, 0:512], psA[:, 512:1024], psA[:, 1024:1536],
                     psA[:, 1536:2048], psB[0][:], psB[1][:], psT[0][:], psT[1][:]]
            rbanks = rpsA + rpsB + rpsT

            def gate_og_mm(og):
                t_gb = sched.dma("gpsimd", kbuf_lo32[0:1, 0:512],
                                 wi["l0f_gateb"][0:1, og * 512:og * 512 + 512],
                                 s_qs, deps=rkbuf.for_write() + rkbuf.for_read())
                rkbuf.did_write(t_gb)
                sh, sl = split_piece(evh[0:1, 0:512], evl[0:1, 0:512],
                                     kbuf_lo32[0:1, 0:512],
                                     deps=[t_gb] + revh.for_write())
                revh.did_write(sl)
                rkbuf.did_read(sl)
                tm = None
                for dc in range(NDC):
                    buf = dc % 2
                    sp = stream_rhs_piece(wi["l0f_gate"], dc * 128, og * 512, 512, buf)
                    for i in range(NT):
                        tm = mm3(banks[i], rbanks[i],
                                 featT_hi[:, dc * TPC + i * 128: dc * TPC + i * 128 + 128],
                                 featT_lo[:, dc * TPC + i * 128: dc * TPC + i * 128 + 128],
                                 whi[buf][:, 0:512], wlo[buf][:, 0:512],
                                 start=(dc == 0), stop=False,
                                 deps=[sp] + (featdep if dc == 0 else []),
                                 mark=(i == NT - 1))
                    rwsp[buf].did_read(tm)
                    rfeatT.did_read(tm)
                for i in range(NT):
                    sched.op("tensor", lambda T, i=i: T.matmul(
                        banks[i], onesrow[0:1, :], evh[0:1, 0:512],
                        start=False, stop=False), deps=[sh, sl, t_consts], inc=False)
                    t_f = sched.op("tensor", lambda T, i=i: T.matmul(
                        banks[i], onesrow[0:1, :], evl[0:1, 0:512],
                        start=False, stop=True), inc=True)
                    rbanks[i].did_write(t_f)
                    revh.did_read(t_f)

            def gate_og_evac(og):
                for i in range(NT):
                    t_s = sched.op("scalar", lambda Sc, i=i:
                                   Sc.activation(out=kbuf_hi32[:, 0:512],
                                                 in_=banks[i], func=AF.Sigmoid),
                                   deps=rbanks[i].for_read() + [rbanks[i].w]
                                   + rkbuf.for_write() + rkbuf.for_read())
                    rbanks[i].did_read(t_s)
                    rkbuf.did_write(t_s)
                    t_sp = sched.dma("gpsimd", g_spill[i, :, og * 512:og * 512 + 512],
                                     kbuf_hi32[:, 0:512], s_qs, deps=[t_s])
                    rkbuf.did_read(t_sp)
                    rgsp.did_write(t_sp)

            t_th = ar_thought("agg", mid=lambda: gate_og_mm(0))

            prev_stat = [t10]
            for step in range(4):
                sc = 40
                t1 = sched.op("vector", lambda V: V.reduce_sum(
                    out=stats[0:4, sc:sc + 1], in_=scr2[0:4, 0:2048], axis=AX.X),
                    deps=[t_th] + prev_stat)
                t2 = sched.op("scalar", lambda Sc: Sc.activation(
                    out=stats[0:4, sc:sc + 1], in_=stats[0:4, sc:sc + 1],
                    func=AF.Copy, scale=-1.0 / D), deps=[t1] + prev_stat)
                t3 = sched.op("scalar", lambda Sc: Sc.activation(
                    out=vbuf32[0:4, 0:2048], in_=scr2[0:4, 0:2048], func=AF.Square,
                    bias=stats[0:4, sc:sc + 1], scale=1.0,
                    accum_out=stats[0:4, sc + 1:sc + 2]),
                    deps=[t2] + rvbuf.for_write())
                rvbuf.did_write(t3)
                t4 = sched.op("vector", lambda V: V.tensor_scalar(
                    out=stats[0:4, sc + 2:sc + 3], in0=stats[0:4, sc + 1:sc + 2],
                    scalar1=1.0 / D, scalar2=1e-6, op0=ALU.mult, op1=ALU.add),
                    deps=[t3])
                t5 = sched.op("scalar", lambda Sc: Sc.activation(
                    out=stats[0:4, sc + 3:sc + 4], in_=stats[0:4, sc + 2:sc + 3],
                    func=AF.Sqrt), deps=[t4])
                t6 = sched.op("vector", lambda V: V.reciprocal(
                    out=stats[0:4, sc + 4:sc + 5], in_=stats[0:4, sc + 3:sc + 4]),
                    deps=[t5])
                t7 = sched.op("vector", lambda V: V.tensor_tensor(
                    out=stats[0:4, sc + 5:sc + 6], in0=stats[0:4, sc + 2:sc + 3],
                    in1=stats[0:4, sc + 4:sc + 5], op=ALU.mult), deps=[t6, t4])
                t8 = sched.op("vector", lambda V: V.tensor_tensor(
                    out=stats[0:4, sc + 5:sc + 6], in0=stats[0:4, sc + 5:sc + 6],
                    in1=stats[0:4, sc + 3:sc + 4], op=ALU.add), deps=[t7, t5])
                t9 = sched.op("scalar", lambda Sc: Sc.activation(
                    out=stats[0:4, sc + 5:sc + 6], in_=stats[0:4, sc + 5:sc + 6],
                    func=AF.Copy, scale=0.5), deps=[t8])
                t10 = sched.op("vector", lambda V: V.reciprocal(
                    out=stats[0:4, sc + 6:sc + 7], in_=stats[0:4, sc + 5:sc + 6]),
                    deps=[t9])
                t_tln = sched.op("vector", lambda V: V.tensor_scalar(
                    out=vbuf32[0:4, 0:2048], in0=scr2[0:4, 0:2048],
                    scalar1=stats[0:4, sc:sc + 1], scalar2=stats[0:4, sc + 6:sc + 7],
                    op0=ALU.add, op1=ALU.mult),
                    deps=[t10, t_th, t3] + rvbuf.for_write() + rvbuf.for_read())
                rvbuf.did_write(t_tln)
                t_g1 = sched.dma("gpsimd", ev32[0:4, 0:2048], wi["l0f_png"][:, :],
                                 s_qs, deps=rev.for_write())
                rev.did_write(t_g1)
                t_tln = sched.op("vector", lambda V: V.tensor_tensor(
                    out=vbuf32[0:4, 0:2048], in0=vbuf32[0:4, 0:2048],
                    in1=ev32[0:4, 0:2048], op=ALU.mult), deps=[t_tln, t_g1])
                rev.did_read(t_tln)
                t_g2 = sched.dma("gpsimd", ev32[0:4, 0:2048], wi["l0f_pnb"][:, :],
                                 s_qs, deps=[t_tln] + rev.for_write())
                rev.did_write(t_g2)
                t_tln = sched.op("vector", lambda V: V.tensor_tensor(
                    out=vbuf32[0:4, 0:2048], in0=vbuf32[0:4, 0:2048],
                    in1=ev32[0:4, 0:2048], op=ALU.add), deps=[t_tln, t_g2])
                rev.did_read(t_tln)
                rvbuf.did_write(t_tln)
                gate_og_evac(step)
                t_tT = transpose4(lambda j: vbuf32[0:4, j * 128:(j + 1) * 128], NDC,
                                  qbuf_hi, qbuf_lo, 16, [t_tln])
                rvbuf.did_read(t_tT)
                rqb2.did_write(t_tT)
                for ng in range(2):
                    for dc in range(NDC):
                        buf = dc % 2
                        sp = stream_rhs_piece(wi["l0f_d1"], dc * 128, ng * 512, 512, buf)
                        t_mm = mm3(psA[0:4, ng * 512:ng * 512 + 512], rpsA[ng],
                                   qbuf_hi[:, 16 + dc * 4:16 + dc * 4 + 4],
                                   qbuf_lo[:, 16 + dc * 4:16 + dc * 4 + 4],
                                   whi[buf][:, 0:512], wlo[buf][:, 0:512],
                                   start=(dc == 0), stop=(dc == NDC - 1),
                                   deps=[sp, t_tT], mark=True)
                        rwsp[buf].did_read(t_mm)
                        rqb2.did_read(t_mm)
                tg_toks = []
                for ng in range(2):
                    t_e = sched.op("vector", lambda V, ng=ng: V.tensor_tensor(
                        out=kbuf_hi32[0:4, ng * 512:ng * 512 + 512],
                        in0=psA[0:4, ng * 512:ng * 512 + 512],
                        in1=ev32b[0:4, ng * 512:ng * 512 + 512], op=ALU.add),
                        deps=rpsA[ng].for_read() + [rpsA[ng].w, t_d1b] + rkbuf.for_write())
                    rpsA[ng].did_read(t_e)
                    revb.did_read(t_e)
                    tg_toks.append(t_e)
                rkbuf.did_write(tg_toks[-1])
                t_gel = sched.op("scalar", lambda Sc: Sc.activation(
                    out=kbuf_hi32[0:4, 0:1024], in_=kbuf_hi32[0:4, 0:1024],
                    func=AF.Gelu), deps=tg_toks)
                rkbuf.did_write(t_gel)
                t_gT = transpose4(lambda j: kbuf_hi32[0:4, j * 128:(j + 1) * 128], 8,
                                  qbuf_hi, qbuf_lo, 80, [t_gel])
                rkbuf.did_read(t_gT)
                rqb2.did_write(t_gT)
                t_d2b = sched.dma("gpsimd", ev32[0:4, 0:2048], wi["l0f_d2b"][:, :],
                                  s_qs, deps=rev.for_write())
                rev.did_write(t_d2b)
                t_d2b = sched.op("vector", lambda V: V.tensor_single_scalar(
                    out=ev32[0:4, 0:2048], in_=ev32[0:4, 0:2048], scalar=0.125,
                    op=ALU.mult), deps=[t_d2b])
                rev.did_write(t_d2b)
                pc_toks = []
                for og in range(4):
                    for dc in range(8):
                        buf = dc % 2
                        sp = stream_rhs_piece(wi["l0f_d2"], dc * 128, og * 512, 512, buf)
                        t_mm = mm3(psB[og % 2][0:4, 0:512], rpsB[og % 2],
                                   qbuf_hi[:, 80 + dc * 4:80 + dc * 4 + 4],
                                   qbuf_lo[:, 80 + dc * 4:80 + dc * 4 + 4],
                                   whi[buf][:, 0:512], wlo[buf][:, 0:512],
                                   start=(dc == 0), stop=(dc == 7),
                                   deps=[sp, t_gT], mark=True)
                        rwsp[buf].did_read(t_mm)
                        rqb2.did_read(t_mm)
                    dsth = pT_hi32 if og < 2 else pT_lo32
                    c0 = (og % 2) * 512
                    t_p = sched.op("vector", lambda V, og=og, dsth=dsth, c0=c0:
                                   V.tensor_tensor(out=dsth[0:4, c0:c0 + 512],
                                                   in0=psB[og % 2][0:4, 0:512],
                                                   in1=ev32[0:4, og * 512:og * 512 + 512],
                                                   op=ALU.add),
                                   deps=rpsB[og % 2].for_read() + [rpsB[og % 2].w, t_d2b]
                                   + (rpT.for_write() if og == 0 else []))
                    rpsB[og % 2].did_read(t_p)
                    rev.did_read(t_p)
                    pc_toks.append(t_p)
                rpT.did_write(pc_toks[-1])
                t_up1 = sched.dma("gpsimd", cc_in[:, 0:1024], pT_hi32[0:4, 0:1024],
                                  s_kvw, deps=pc_toks + rcc.for_write())
                t_up2 = sched.dma("gpsimd", cc_in[:, 1024:2048], pT_lo32[0:4, 0:1024],
                                  s_kvw, deps=pc_toks)
                rcc.did_write(t_up2)
                rpT.did_read(t_up2)
                def do_cc2(G):
                    G.collective_compute(
                        "AllReduce", ALU.add, ins=[cc_in[:, :]], outs=[cc_out[:, :]],
                        replica_groups=[list(range(NCORE))],
                    ).then_inc(ccsem, 1)
                RT["cc"] = RT.get("cc", 0) + 1
                sched.raw("gpsimd", do_cc2, deps=[t_up1, t_up2])
                if step < 3:
                    gate_og_mm(step + 1)
                t_dn = sched.dma("gpsimd", vbuf32[0:4, 0:2048], cc_out[:, :], s_kvw,
                                 deps=[(ccsem, RT["cc"])] + rvbuf.for_write())
                rvbuf.did_write(t_dn)
                rcc.did_read(t_dn)
                t_th = sched.op("vector", lambda V: V.tensor_tensor(
                    out=scr2[0:4, 0:2048], in0=scr2[0:4, 0:2048],
                    in1=vbuf32[0:4, 0:2048], op=ALU.add),
                    deps=[t_dn, t_th] + rscr2.for_write() + rscr2.for_read())
                rscr2.did_write(t_th)
                rvbuf.did_read(t_th)
                prev_stat = [t9, t10, t_tln]

            t_thT = transpose4(lambda j: scr2[0:4, j * 128:(j + 1) * 128], NDC,
                               qbuf_hi, qbuf_lo, 16, [t_th])
            rscr2.did_read(t_thT)
            rqb2.did_write(t_thT)
            for dc in range(NDC):
                buf = dc % 2
                sp = stream_rhs_piece(wi["l0f_bc"], dc * 128, 0, 256, buf)
                t_mm = mm3(psB[0][0:4, 0:256], rpsB[0],
                           qbuf_hi[:, 16 + dc * 4:16 + dc * 4 + 4],
                           qbuf_lo[:, 16 + dc * 4:16 + dc * 4 + 4],
                           whi[buf][:, 0:256], wlo[buf][:, 0:256],
                           start=(dc == 0), stop=(dc == NDC - 1),
                           deps=[sp, t_thT], mark=True)
                rwsp[buf].did_read(t_mm)
                rqb2.did_read(t_mm)
            t_bcb = sched.dma("gpsimd", vbuf32[0:4, 0:256], wi["l0f_bcb"][:, :], s_qs,
                              deps=rvbuf.for_write())
            rvbuf.did_write(t_bcb)
            t_mybc = sched.op("vector", lambda V: V.tensor_tensor(
                out=vbuf32[0:4, 256:512], in0=psB[0][0:4, 0:256],
                in1=vbuf32[0:4, 0:256], op=ALU.add),
                deps=rpsB[0].for_read() + [rpsB[0].w, t_bcb])
            rpsB[0].did_read(t_mybc)
            rvbuf.did_write(t_mybc)
            t_z = sched.op("vector", lambda V: V.memset(ev32[0:4, 0:2048], 0.0),
                           deps=rev.for_write())
            rev.did_write(t_z)
            t_z = sched.dma("gpsimd", cc_in[:, :], ev32[0:4, 0:2048], s_kvw,
                            deps=[t_z] + rcc.for_write())
            t_w = sched.dma("gpsimd",
                            lambda: cc_in[:, ds(colv["v"], 256)],
                            vbuf32[0:4, 256:512], s_kvw, deps=[t_z, t_mybc])
            rcc.did_write(t_w)
            rvbuf.did_read(t_w)
            def do_cc3(G):
                G.collective_compute(
                    "AllReduce", ALU.add, ins=[cc_in[:, :]], outs=[cc_out[:, :]],
                    replica_groups=[list(range(NCORE))],
                ).then_inc(ccsem, 1)
            RT["cc"] = RT.get("cc", 0) + 1
            sched.raw("gpsimd", do_cc3, deps=[t_w])
            t_bc = sched.dma("gpsimd", ev32[0:4, 0:2048], cc_out[:, :], s_kvw,
                             deps=[(ccsem, RT["cc"])] + rev.for_write())
            rev.did_write(t_bc)
            rcc.did_read(t_bc)
            bt_toks = []
            for og in range(4):
                sh, sl = split_piece(evh[0:4, 0:512], evl[0:4, 0:512],
                                     ev32[0:4, og * 512:og * 512 + 512],
                                     deps=[t_bc] + revh.for_write())
                revh.did_write(sl)
                rev.did_read(sl)
                m1 = sched.op("tensor", lambda T: T.matmul(
                    psB[1][:, 0:512], bselr[0:4, 0:128], evh[0:4, 0:512],
                    start=True, stop=False),
                    deps=[sh, sl, t_consts] + rpsB[1].for_write(), inc=False)
                m2 = sched.op("tensor", lambda T: T.matmul(
                    psB[1][:, 0:512], bselr[0:4, 0:128], evl[0:4, 0:512],
                    start=False, stop=True), inc=True)
                rpsB[1].did_write(m2)
                revh.did_read(m2)
                dsth = pT_hi32 if og < 2 else pT_lo32
                c0 = (og % 2) * 512
                t_b = sched.op("vector", lambda V, dsth=dsth, c0=c0: V.tensor_copy(
                    out=dsth[:, c0:c0 + 512], in_=psB[1][:, 0:512]),
                    deps=[m2] + (rpT.for_write() if og == 0 else []))
                rpsB[1].did_read(t_b)
                bt_toks.append(t_b)
            rpT.did_write(bt_toks[-1])


            t_og1 = sched.dma("sync", kbuf_hi32[:, 0:1024], wi["l0f_outg"][:, 0:1024],
                              s_rope, deps=rkbuf.for_write())
            t_og2 = sched.dma("sync", kbuf_lo32[:, 0:1024], wi["l0f_outg"][:, 1024:2048],
                              s_rope)
            t_ob = sched.dma("sync", vbuf32[:, 0:2048], wi["l0f_outb"][:, :], s_rope,
                             deps=rvbuf.for_write() + rvbuf.for_read())
            rkbuf.did_write(t_og2)
            rvbuf.did_write(t_ob)
            prev_ep = prev_stat
            for i in range(NT):
                t_g = sched.dma("gpsimd", scr2[:, 0:2048], g_spill[i, :, :], s_qs,
                                deps=rgsp.for_read() + [rgsp.w]
                                + rscr2.for_write() + rscr2.for_read())
                rgsp.did_read(t_g)
                rscr2.did_write(t_g)
                t_x = sched.op("vector", lambda V, i=i: V.tensor_scalar(
                    out=ev32[:, 0:2048], in0=xres[i][:], scalar1=stats[:, i:i + 1],
                    scalar2=stats[:, 8 + i:9 + i], op0=ALU.add, op1=ALU.mult),
                    deps=rxres[i].for_read() + rstats.for_read() + rev.for_write()
                    + rev.for_read())
                rxres[i].did_read(t_x)
                rev.did_write(t_x)
                for hb, dsth in ((0, pT_hi32), (1, pT_lo32)):
                    t_m = sched.op("vector", lambda V, hb=hb, dsth=dsth:
                                   V.tensor_tensor(out=ev32b[:, 0:1024],
                                                   in0=scr2[:, hb * 1024:hb * 1024 + 1024],
                                                   in1=dsth[:, 0:1024], op=ALU.mult),
                                   deps=[t_g] + rpT.for_read() + [rpT.w]
                                   + revb.for_write() + revb.for_read())
                    rscr2.did_read(t_m)
                    rpT.did_read(t_m)
                    revb.did_write(t_m)
                    t_a = sched.op("vector", lambda V, hb=hb: V.tensor_tensor(
                        out=ev32[:, hb * 1024:hb * 1024 + 1024],
                        in0=ev32[:, hb * 1024:hb * 1024 + 1024],
                        in1=ev32b[:, 0:1024], op=ALU.add), deps=[t_m, t_x])
                    revb.did_read(t_a)
                    rev.did_write(t_a)
                sc = 48
                t1 = sched.op("vector", lambda V: V.reduce_sum(
                    out=stats[:, sc:sc + 1], in_=ev32[:, 0:2048], axis=AX.X),
                    deps=rev.for_read() + [rev.w] + prev_ep)
                t2 = sched.op("scalar", lambda Sc: Sc.activation(
                    out=stats[:, sc:sc + 1], in_=stats[:, sc:sc + 1],
                    func=AF.Copy, scale=-1.0 / D), deps=[t1] + prev_ep)
                t3 = sched.op("scalar", lambda Sc: Sc.activation(
                    out=scr2[:, 0:2048], in_=ev32[:, 0:2048], func=AF.Square,
                    bias=stats[:, sc:sc + 1], scale=1.0,
                    accum_out=stats[:, sc + 1:sc + 2]),
                    deps=[t2] + rscr2.for_write() + rscr2.for_read())
                rscr2.did_write(t3)
                t4 = sched.op("vector", lambda V: V.tensor_scalar(
                    out=stats[:, sc + 2:sc + 3], in0=stats[:, sc + 1:sc + 2],
                    scalar1=1.0 / D, scalar2=1e-6, op0=ALU.mult, op1=ALU.add),
                    deps=[t3])
                t5 = sched.op("scalar", lambda Sc: Sc.activation(
                    out=stats[:, sc + 3:sc + 4], in_=stats[:, sc + 2:sc + 3],
                    func=AF.Sqrt), deps=[t4])
                t6 = sched.op("vector", lambda V: V.reciprocal(
                    out=stats[:, sc + 4:sc + 5], in_=stats[:, sc + 3:sc + 4]),
                    deps=[t5])
                t7 = sched.op("vector", lambda V: V.tensor_tensor(
                    out=stats[:, sc + 5:sc + 6], in0=stats[:, sc + 2:sc + 3],
                    in1=stats[:, sc + 4:sc + 5], op=ALU.mult), deps=[t6, t4])
                t8 = sched.op("vector", lambda V: V.tensor_tensor(
                    out=stats[:, sc + 5:sc + 6], in0=stats[:, sc + 5:sc + 6],
                    in1=stats[:, sc + 3:sc + 4], op=ALU.add), deps=[t7, t5])
                t9 = sched.op("scalar", lambda Sc: Sc.activation(
                    out=stats[:, sc + 5:sc + 6], in_=stats[:, sc + 5:sc + 6],
                    func=AF.Copy, scale=0.5), deps=[t8])
                t10 = sched.op("vector", lambda V: V.reciprocal(
                    out=stats[:, sc + 6:sc + 7], in_=stats[:, sc + 5:sc + 6]),
                    deps=[t9])
                t_ap = sched.op("vector", lambda V: V.tensor_scalar(
                    out=ev32[:, 0:2048], in0=ev32[:, 0:2048],
                    scalar1=stats[:, sc:sc + 1], scalar2=stats[:, sc + 6:sc + 7],
                    op0=ALU.add, op1=ALU.mult), deps=[t10] + rev.for_read() + [rev.w])
                rev.did_write(t_ap)
                for hb in range(2):
                    src_g = kbuf_hi32 if hb == 0 else kbuf_lo32
                    t_ap2 = sched.op("vector", lambda V, hb=hb, src_g=src_g:
                                     V.tensor_tensor(out=ev32[:, hb * 1024:hb * 1024 + 1024],
                                                     in0=ev32[:, hb * 1024:hb * 1024 + 1024],
                                                     in1=src_g[:, 0:1024], op=ALU.mult),
                                     deps=[t_ap, t_og2])
                    rkbuf.did_read(t_ap2)
                    rev.did_write(t_ap2)
                t_ap3 = sched.op("vector", lambda V: V.tensor_tensor(
                    out=ev32[:, 0:2048], in0=ev32[:, 0:2048], in1=vbuf32[:, 0:2048],
                    op=ALU.add), deps=rev.for_read() + [rev.w, t_ob])
                rvbuf.did_read(t_ap3)
                rev.did_write(t_ap3)
                t_r = sched.op("vector", lambda V, i=i: V.tensor_tensor(
                    out=xres[i][:], in0=xres[i][:], in1=ev32[:, 0:2048], op=ALU.add),
                    deps=[t_ap3] + rxres[i].for_write())
                rev.did_read(t_r)
                rxres[i].did_write(t_r)
                prev_ep = [t9, t10, t_ap]
            rstats.did_write(prev_ep[-1])
            rstats.did_write(prev_ep[0])


        def sub_moe():
            last_feat = build_featT(1e-5)
            featdep = [last_feat]
            banks = [psA[:, 0:512], psA[:, 512:1024], psA[:, 1024:1536],
                     psA[:, 1536:2048], psB[0][:], psB[1][:], psT[0][:], psT[1][:]]
            rbanks = rpsA + rpsB + rpsT

            for dc in range(NDC):
                buf = dc % 2
                sp = stream_rhs_piece(wi["l1f_wg"], dc * 128, 0, 8, buf)
                tm = None
                for i in range(NT):
                    tm = mm3(banks[i].tensor[0:128, banks[i].offset % 2048:
                                             banks[i].offset % 2048 + 8]
                             if False else
                             (psA[:, i * 512:i * 512 + 8] if i < 4 else
                              [psB[0][:, 0:8], psB[1][:, 0:8], psT[0][:, 0:8],
                               psT[1][:, 0:8]][i - 4]),
                             rbanks[i],
                             featT_hi[:, dc * TPC + i * 128: dc * TPC + i * 128 + 128],
                             featT_lo[:, dc * TPC + i * 128: dc * TPC + i * 128 + 128],
                             whi[buf][:, 0:8], wlo[buf][:, 0:8],
                             start=(dc == 0), stop=(dc == NDC - 1),
                             deps=[sp] + (featdep if dc == 0 else []),
                             mark=(i == NT - 1))
                rwsp[buf].did_read(tm)
                rfeatT.did_read(tm)
            for i in range(NT):
                bank8 = (psA[:, i * 512:i * 512 + 8] if i < 4 else
                         [psB[0][:, 0:8], psB[1][:, 0:8], psT[0][:, 0:8],
                          psT[1][:, 0:8]][i - 4])
                t_mx = sched.op("vector", lambda V, bank8=bank8: V.reduce_max(
                    out=arow[:, 0:1], in_=bank8, axis=AX.X, negate=True),
                    deps=rbanks[i].for_read() + [rbanks[i].w] + rarow.for_write())
                rbanks[i].did_read(t_mx)
                rarow.did_write(t_mx)
                t_exp = sched.op("scalar", lambda Sc, bank8=bank8: Sc.activation(
                    out=arow[:, 4:12], in_=bank8, func=AF.Exp,
                    bias=arow[:, 0:1], scale=1.0, accum_out=arow[:, 1:2]),
                    deps=[t_mx])
                rbanks[i].did_read(t_exp)
                rarow.did_write(t_exp)
                t_rec = sched.op("vector", lambda V: V.reciprocal(
                    out=arow[:, 2:3], in_=arow[:, 1:2]), deps=[t_exp])
                rarow.did_write(t_rec)
                t_m2 = sched.op("vector", lambda V: V.reduce_max(
                    out=arow[:, 3:4], in_=arow[:, 4:12], axis=AX.X), deps=[t_exp])
                rarow.did_write(t_m2)
                t_eq = sched.op("vector", lambda V: V.tensor_single_scalar(
                    out=arow[:, 4:12], in_=arow[:, 4:12], scalar=arow[:, 3:4],
                    op=ALU.is_ge), deps=[t_m2])
                rarow.did_write(t_eq)
                t_t1 = sched.op("vector", lambda V: V.tensor_tensor(
                    out=arow[:, 4:12], in0=arow[:, 4:12], in1=iota8_sb[:, 8:16],
                    op=ALU.mult), deps=[t_eq, t_consts])
                rarow.did_write(t_t1)
                t_cand = sched.op("vector", lambda V: V.tensor_scalar(
                    out=arow[:, 4:12], in0=arow[:, 4:12], scalar1=-1.0, scalar2=9.0,
                    op0=ALU.mult, op1=ALU.add), deps=[t_t1])
                rarow.did_write(t_cand)
                t_idx = sched.op("vector", lambda V: V.tensor_reduce(
                    out=arow[:, 3:4], in_=arow[:, 4:12], op=ALU.min, axis=AX.X),
                    deps=[t_cand])
                rarow.did_write(t_idx)
                t_oh = sched.op("vector", lambda V: V.tensor_single_scalar(
                    out=arow[:, 4:12], in_=iota8_sb[:, 0:8], scalar=arow[:, 3:4],
                    op=ALU.is_equal), deps=[t_idx])
                rarow.did_write(t_oh)
                t_cb = sched.op("vector", lambda V, i=i: V.tensor_single_scalar(
                    out=moe_c[:, i * 8:i * 8 + 8], in_=arow[:, 4:12],
                    scalar=arow[:, 2:3], op=ALU.mult),
                    deps=[t_oh, t_rec] + rmoec.for_write())
                rarow.did_read(t_cb)
                rmoec.did_write(t_cb)

            for e in range(8):
                t_b = sched.dma("gpsimd", ev32[0:1, 0:2048],
                                wi["l1f_b"][e:e + 1, :], s_qs,
                                deps=rev.for_write() + rev.for_read())
                rev.did_write(t_b)
                for og in range(4):
                    sh, sl = split_piece(evh[0:1, 0:512], evl[0:1, 0:512],
                                         ev32[0:1, og * 512:og * 512 + 512],
                                         deps=[t_b] + revh.for_write())
                    revh.did_write(sl)
                    rev.did_read(sl)
                    for dc in range(NDC):
                        buf = dc % 2
                        sp = stream_rhs_piece(wi["l1f_w"], 0, 0, 512, buf,
                                              flat_base=e * D * D + dc * 128 * D + og * 512,
                                              dout=D)
                        tm = None
                        for i in range(NT):
                            tm = mm3(banks[i], rbanks[i],
                                     featT_hi[:, dc * TPC + i * 128: dc * TPC + i * 128 + 128],
                                     featT_lo[:, dc * TPC + i * 128: dc * TPC + i * 128 + 128],
                                     whi[buf][:, 0:512], wlo[buf][:, 0:512],
                                     start=(dc == 0), stop=False,
                                     deps=[sp] + (featdep if dc == 0 and e == 0 else []),
                                     mark=(i == NT - 1))
                        rwsp[buf].did_read(tm)
                        rfeatT.did_read(tm)
                    for i in range(NT):
                        sched.op("tensor", lambda T, i=i: T.matmul(
                            banks[i], onesrow[0:1, :], evh[0:1, 0:512],
                            start=False, stop=False), deps=[sh, sl, t_consts], inc=False)
                        t_f = sched.op("tensor", lambda T, i=i: T.matmul(
                            banks[i], onesrow[0:1, :], evl[0:1, 0:512],
                            start=False, stop=True), inc=True)
                        rbanks[i].did_write(t_f)
                        revh.did_read(t_f)
                    for i in range(NT):
                        t = sched.op("vector", lambda V, i=i, og=og, e=e, bank=banks[i]:
                                     V.scalar_tensor_tensor(
                                         out=xres[i][:, og * 512:og * 512 + 512],
                                         in0=bank, scalar=moe_c[:, i * 8 + e:i * 8 + e + 1],
                                         op0=ALU.mult,
                                         in1=xres[i][:, og * 512:og * 512 + 512],
                                         op1=ALU.add),
                                     deps=rbanks[i].for_read() + [rbanks[i].w]
                                     + rxres[i].for_write() + rmoec.for_read())
                        rbanks[i].did_read(t)
                        rmoec.did_read(t)
                        rxres[i].did_write(t)


        def sub_attn_fp8(layer):
            last_feat = build_featT(1e-5)
            featdep = [last_feat]
            rscl = res("scl")
            evh8 = evh[:].bitcast(FP8)
            vbufI = vbuf[:].bitcast(I32)
            RSQ = float(np.float32(1.0) / np.float32(np.sqrt(np.float32(HD))))

            t_z = sched.op("vector", lambda V: V.memset(arow[:, 8:16], 0.0),
                           deps=rarow.for_write() + rarow.for_read())
            rarow.did_write(t_z)

            def track_absmax(src_ap, col, deps):
                t1 = sched.op("vector", lambda V: V.reduce_max(
                    out=arow[:, 12:13], in_=src_ap, axis=AX.X,
                    apply_absolute_value=True),
                    deps=list(deps) + rarow.for_write())
                rarow.did_write(t1)
                t2 = sched.op("vector", lambda V, col=col: V.tensor_tensor(
                    out=arow[:, col:col + 1], in0=arow[:, col:col + 1],
                    in1=arow[:, 12:13], op=ALU.max), deps=[t1, t_z])
                rarow.did_write(t2)
                return t2

            def proj_spill(w_ap, region, col, hook=None):
                sp_next = stream_lhsT_slab(w_ap, 0, 0, extra_deps=featdep)
                for oc in range(NDC):
                    buf = oc % 2
                    sp = sp_next
                    mm_last = []
                    for dc in range(NDC):
                        for tg in range(2):
                            t = mm3(psA[:, tg * 512:tg * 512 + 512], rpsA[tg],
                                    whi[buf][:, dc * 128:dc * 128 + 128],
                                    wlo[buf][:, dc * 128:dc * 128 + 128],
                                    featT_hi[:, dc * TPC + tg * 512: dc * TPC + tg * 512 + 512],
                                    featT_lo[:, dc * TPC + tg * 512: dc * TPC + tg * 512 + 512],
                                    start=(dc == 0), stop=(dc == NDC - 1),
                                    deps=([sp] + featdep) if dc == 0 else ())
                            if dc == NDC - 1:
                                mm_last.append(t)
                    rwsp[buf].did_read(mm_last[-1])
                    rfeatT.did_read(mm_last[-1])
                    if oc + 1 < NDC:
                        sp_next = stream_lhsT_slab(w_ap, oc + 1, (oc + 1) % 2,
                                                   extra_deps=featdep)
                    t_ev = []
                    for tg in range(2):
                        t = sched.op("vector", lambda V, tg=tg: V.tensor_copy(
                            out=ev32[:, tg * 512:tg * 512 + 512],
                            in_=psA[:, tg * 512:tg * 512 + 512]),
                            deps=[mm_last[tg]] + rev.for_write() + rev.for_read())
                        rpsA[tg].did_read(t)
                        t_ev.append(t)
                    rev.did_write(t_ev[-1])
                    t_am = track_absmax(ev32[:, 0:1024], col, t_ev)
                    t_sp = sched.dma("gpsimd", qkv_f32[region, oc, :, :],
                                     ev32[:, 0:1024], s_qs, deps=t_ev)
                    rev.did_read(t_sp)
                    rqsp.did_write(t_sp)
                    if hook is not None:
                        hook(oc)

            prev_ccm = [None]

            def finalize_scale(trow, col, recip_col, scale_col):
                t_tr = sched.op("tensor", lambda T, col=col: T.transpose(
                    psT[0][0:1, 0:128], arow[:, col:col + 1], ident[:]),
                    deps=rarow.for_read() + [rarow.w, t_consts] + rpsT[0].for_write())
                rarow.did_read(t_tr)
                rpsT[0].did_write(t_tr)
                t_rm = sched.op("vector", lambda V: V.reduce_max(
                    out=arow[0:1, 13:14], in_=psT[0][0:1, 0:128], axis=AX.X),
                    deps=[t_tr] + rarow.for_write())
                rpsT[0].did_read(t_rm)
                rarow.did_write(t_rm)
                t_up = sched.dma("gpsimd", ccm_in[0:1, :], arow[0:1, 13:14], s_kvw,
                                 deps=[t_rm]
                                 + ([prev_ccm[0]] if prev_ccm[0] is not None else []))
                rarow.did_read(t_up)

                def do_ccm(G):
                    G.collective_compute(
                        "AllReduce", ALU.max, ins=[ccm_in[:, :]], outs=[ccm_out[:, :]],
                        replica_groups=[list(range(NCORE))],
                    ).then_inc(ccsem, 1)
                RT["cc"] = RT.get("cc", 0) + 1
                sched.raw("gpsimd", do_ccm, deps=[t_up])
                t_dn = sched.dma("gpsimd", arow[0:1, 13:14], ccm_out[0:1, :], s_kvw,
                                 deps=[(ccsem, RT["cc"])] + rarow.for_write())
                prev_ccm[0] = t_dn
                rarow.did_write(t_dn)
                t_s = sched.op("vector", lambda V: V.tensor_single_scalar(
                    out=arow[0:1, 14:15], in_=arow[0:1, 13:14],
                    scalar=float(np.float32(1.0) / np.float32(448.0)),
                    op=ALU.mult), deps=[t_dn])
                rarow.did_write(t_s)
                t_r = sched.op("vector", lambda V: V.reciprocal(
                    out=arow[0:1, 15:16], in_=arow[0:1, 14:15]), deps=[t_s])
                rarow.did_write(t_r)
                t_b = sched.dma("gpsimd", bsc[trow:trow + 1, 1:3], arow[0:1, 14:16],
                                s_kvw, deps=[t_r])
                rarow.did_read(t_b)
                bts = []
                for colv_, flat in ((recip_col, trow * 3 + 2),
                                    (scale_col, trow * 3 + 1)):
                    t = sched.dma("gpsimd", scl[:, colv_:colv_ + 1],
                                  mkap(bsc, flat, [[0, 128], [1, 1]]), s_kvr,
                                  deps=[t_b] + rscl.for_write())
                    bts.append(t)
                rscl.did_write(bts[-1])
                return bts[-1]

            kb8 = kbuf_hi[:].bitcast(FP8)

            def mk_quant(region, recip_col, writer, scale_tok, db=False):
                def emit_tile(tt):
                    if db and tt % 2 == 1:
                        big, rbig = ev32, rev
                        f8, rf8 = evh8, revh
                        mg, rmg = ev32b[:, 0:1024], revb
                    else:
                        big, rbig = scr2, rscr2
                        f8, rf8 = kb8[:, 0:1024], rkbuf
                        mg, rmg = kbuf_lo32[:, 0:1024], rkbuf
                    t0 = sched.dma("gpsimd", big[:, 0:1024], qkv_f32[region, tt, :, :],
                                   s_qr, deps=rqsp.for_read() + [rqsp.w]
                                   + rbig.for_write() + rbig.for_read())
                    rqsp.did_read(t0)
                    rbig.did_write(t0)
                    t1 = sched.op("vector", lambda V, big=big: V.tensor_single_scalar(
                        out=big[:, 0:1024], in_=big[:, 0:1024],
                        scalar=scl[:, recip_col:recip_col + 1], op=ALU.mult),
                        deps=[t0, scale_tok])
                    rscl.did_read(t1)
                    rbig.did_write(t1)
                    t2 = sched.op("scalar", lambda Sc, big=big, f8=f8: Sc.activation(
                        out=f8, in_=big[:, 0:1024], func=AF.Copy,
                        scale=0.5), deps=[t1] + rf8.for_write() + rf8.for_read())
                    rf8.did_write(t2)
                    t3 = sched.op("scalar", lambda Sc, big=big, f8=f8: Sc.activation(
                        out=big[:, 1024:2048], in_=f8, func=AF.Copy,
                        scale=2.0), deps=[t2])
                    rbig.did_write(t3)
                    rf8.did_read(t3)
                    t4 = sched.op("vector", lambda V, big=big, mg=mg: V.tensor_scalar(
                        out=mg, in0=big[:, 0:1024], scalar1=24576.0,
                        scalar2=-24576.0, op0=ALU.add, op1=ALU.add),
                        deps=[t1] + rmg.for_write() + rmg.for_read())
                    rmg.did_write(t4)
                    t5 = sched.op("vector", lambda V, big=big: V.tensor_single_scalar(
                        out=vbufI[:, 0:1024], in_=big[:, 0:1024].bitcast(I32),
                        scalar=0x7FFFFFFF, op=ALU.bitwise_and),
                        deps=[t1] + rvbuf.for_write() + rvbuf.for_read())
                    rvbuf.did_write(t5)
                    t6 = sched.op("vector", lambda V: V.tensor_single_scalar(
                        out=vbufI[:, 0:1024], in_=vbuf32[:, 0:1024], scalar=0.03125,
                        op=ALU.is_lt), deps=[t5])
                    rvbuf.did_write(t6)
                    t7 = sched.op("vector", lambda V, big=big, mg=mg: V.copy_predicated(
                        out=big[:, 1024:2048], mask=vbufI[:, 0:1024],
                        data=mg), deps=[t3, t4, t6])
                    rvbuf.did_read(t7)
                    rmg.did_read(t7)
                    rbig.did_write(t7)
                    t8 = sched.op("scalar", lambda Sc, big=big: Sc.activation(
                        out=qbuf_hi[:], in_=big[:, 1024:2048], func=AF.Copy),
                        deps=[t7] + rqbuf.for_write())
                    rbig.did_read(t8)
                    rqbuf.did_write(t8)
                    writer(tt, t8)
                return emit_tile

            def wq8(tt, tok):
                t = sched.dma("gpsimd", q_spill[0, tt, :, :], qbuf_hi[:], s_qs,
                              deps=[tok])
                rqbuf.did_read(t)
                rqsp.did_write(t)

            def wk8(tt, tok):
                t = sched.dma("gpsimd", kt_hi_st[tt * 128:(tt + 1) * 128, :],
                              qbuf_hi[:], s_kvw, deps=[tok])
                rqbuf.did_read(t)
                rkv.did_write(t)

            def wv8(tt, tok):
                i, half = tt // 2, tt % 2
                t = sched.dma("gpsimd",
                              vp_st[i * 128:(i + 1) * 128, half * 8:half * 8 + 8, 0:128],
                              qbuf_hi[:], s_kvw, deps=[tok])
                t2 = sched.dma("gpsimd",
                               vp_st[i * 128:(i + 1) * 128, half * 8:half * 8 + 8, 128:256],
                               qbuf_hi[:], s_kvw, deps=[tok])
                rqbuf.did_read(t)
                rqbuf.did_read(t2)
                rkv.did_write(t2)

            t_prevread = (kv_done, RT["kv_rsig"]) if RT["kv_rsig"] else None
            rkv.did_write(t_prevread) if t_prevread else None

            proj_spill(layer["wq"], 0, 8)
            tok_q = finalize_scale(0, 8, 0, 5)
            qq = mk_quant(0, 0, wq8, tok_q)
            proj_spill(layer["wk"], 1, 9,
                       hook=lambda oc: qq(oc - 3) if oc >= 3 else None)
            tok_k = finalize_scale(1, 9, 1, 6)
            t_m = sched.op("vector", lambda V: V.tensor_tensor(
                out=scl[:, 3:4], in0=scl[:, 5:6], in1=scl[:, 6:7], op=ALU.mult),
                deps=[tok_k])
            t_m = sched.op("vector", lambda V: V.tensor_single_scalar(
                out=scl[:, 3:4], in_=scl[:, 3:4], scalar=RSQ, op=ALU.mult),
                deps=[t_m])
            rscl.did_write(t_m)
            kk = mk_quant(1, 1, wk8, tok_k)
            pending_q = [lambda j=j: qq(j) for j in range(13, 16)] \
                + [lambda j=j: kk(j) for j in range(16)]

            banks = [psA[:, 0:512], psA[:, 512:1024], psA[:, 1024:1536],
                     psA[:, 1536:2048], psB[0][:], psB[1][:], psT[0][:], psT[1][:]]
            rbanks = rpsA + rpsB + rpsT
            for og in range(4):
                for dc in range(NDC):
                    buf = dc % 2
                    sp = stream_rhs_piece(layer["wv"], dc * 128, og * 512, 512, buf)
                    tm = None
                    for i in range(NT):
                        tm = mm3(banks[i], rbanks[i],
                                 featT_hi[:, dc * TPC + i * 128: dc * TPC + i * 128 + 128],
                                 featT_lo[:, dc * TPC + i * 128: dc * TPC + i * 128 + 128],
                                 whi[buf][:, 0:512], wlo[buf][:, 0:512],
                                 start=(dc == 0), stop=(dc == NDC - 1),
                                 deps=[sp], mark=(i == NT - 1))
                    rwsp[buf].did_read(tm)
                    rfeatT.did_read(tm)
                    if dc % 2 == 0 and not (og == 0 and dc < 8) and pending_q:
                        pending_q.pop(0)()
                for i in range(NT):
                    t_e = sched.op("vector", lambda V, i=i: V.tensor_copy(
                        out=ev32b[:, 0:512], in_=banks[i]),
                        deps=rbanks[i].for_read() + [rbanks[i].w]
                        + revb.for_write() + revb.for_read())
                    rbanks[i].did_read(t_e)
                    revb.did_write(t_e)
                    t_am = track_absmax(ev32b[:, 0:512], 10, [t_e])
                    t_sp = sched.dma("gpsimd",
                                     qkv_f32[2, i * 2 + og // 2, :,
                                             (og % 2) * 512:(og % 2) * 512 + 512],
                                     ev32b[:, 0:512], s_qs, deps=[t_e])
                    revb.did_read(t_sp)
                    rqsp.did_write(t_sp)

            while pending_q:
                pending_q.pop(0)()
            tok_v = finalize_scale(2, 10, 2, 4)
            vv = mk_quant(2, 2, wv8, tok_v, db=True)
            for _j in range(16):
                vv(_j)

            kvst_tok = (s_kvw, sched.dma_cnt[id(s_kvw)])
            tb1 = sched.dma("gpsimd", lambda: dyn["kt_hi"][0, :, :],
                            kt_hi_st[:, :], s_kvb, deps=[kvst_tok] + rkv.for_write())
            tb3 = sched.dma("gpsimd", lambda: dyn["vp"][0, :, :, :],
                            vp_st[:, :, :], s_kvb, deps=[kvst_tok])
            rkv.did_write(tb3)
            kvw_tok = tb3
            RT["prep"] += 1
            pv = RT["prep"]

            def signal(G, pv=pv):
                ins = G.remote_sem_update_broadcast(
                    kv_ready, lsem, rdests=[(0, 1)] + [None] * 7,
                    sim_remote_sem_inc=2)
                ins.then_inc(prep_sem, 1)
                G.wait_ge(prep_sem, pv)
                G.trigger_dma(count=1)

            sched.raw("gpsimd", signal, deps=[kvw_tok])
            RT["kv_sig"] += 2
            t_peer = (kv_ready, RT["kv_sig"])

            def emit_av8(ctx):
                rc, colp, deps0, pt_done_p = ctx
                t_av = None
                for ct in range(16):
                    d0 = (pt_done_p + deps0) if ct == 0 else ()
                    d0 = list(d0) + (rpsB[0].for_write() if ct == 0 else [])
                    sched.op("tensor", lambda T, ct=ct: T.matmul(
                        psB[0][:, 0:128],
                        pT_hi[:, ct * 128:ct * 128 + 128],
                        vbuf[:, ct * 256:ct * 256 + 128],
                        start=(ct == 0), stop=False), deps=d0, inc=False)
                    t_av = sched.op("tensor", lambda T, ct=ct: T.matmul(
                        psB[0][:, 0:128],
                        pT_lo[:, ct * 128:ct * 128 + 128],
                        vbuf[:, ct * 256:ct * 256 + 128],
                        start=False, stop=(ct == 15)), inc=(ct == 15))
                rpsB[0].did_write(t_av)
                rpT.did_read(t_av)
                rvbuf.did_read(t_av)
                t_o = sched.op("vector", lambda V, rc=rc: V.tensor_scalar(
                    out=ev32b[:, 0:128], in0=psB[0][:, 0:128],
                    scalar1=arow[:, rc:rc + 1], scalar2=scl[:, 4:5],
                    op0=ALU.mult, op1=ALU.mult),
                    deps=[t_av] + revb.for_write() + revb.for_read())
                rpsB[0].did_read(t_o)
                rarow.did_read(t_o)
                rscl.did_read(t_o)
                revb.did_write(t_o)
                bank = psT[0]; rbank = rpsT[0]
                t_tr2 = sched.op("tensor", lambda T, bank=bank: T.transpose(
                    bank[:, 0:128], ev32b[:, 0:128], ident[:]),
                    deps=[t_o, t_consts] + rbank.for_write())
                revb.did_read(t_tr2)
                rbank.did_write(t_tr2)
                t_hi = sched.op("vector", lambda V, colp=colp, bank=bank: V.tensor_copy(
                    out=featT_hi[:, colp:colp + 128], in_=bank[:, 0:128]),
                    deps=[t_tr2] + rfeatT.for_write())
                t_lo = sched.op("vector", lambda V, colp=colp, bank=bank:
                                V.tensor_tensor(out=featT_lo[:, colp:colp + 128],
                                                in0=bank[:, 0:128],
                                                in1=featT_hi[:, colp:colp + 128],
                                                op=ALU.subtract), deps=[t_hi])
                rbank.did_read(t_lo)
                rfeatT.did_write(t_lo)

            av_ctx8 = [None]
            for kvh in range(16):
                kl = []
                for slot in range(2):
                    t = sched.dma("gpsimd",
                                  kbuf_hi[:, slot * 1024:slot * 1024 + 1024],
                                  kt_hi_sh[slot, kvh * 128:(kvh + 1) * 128, :],
                                  s_kvr, deps=[t_peer, kvw_tok] + rkbuf.for_write())
                    kl.append(t)
                rkbuf.did_write(kl[-1])
                vl = []
                for ct in range(16):
                    slot = 0 if ct < 8 else 1
                    lt = ct % 8
                    t = sched.dma("gpsimd",
                                  vbuf[:, ct * 256:ct * 256 + 128],
                                  vp_sh[slot, lt * 128:(lt + 1) * 128, kvh, 0:128],
                                  s_kvv, deps=[t_peer, kvw_tok] + rvbuf.for_write())
                    vl.append(t)
                rvbuf.did_write(vl[-1])
                t_kv = [kl[-1], vl[-1]]
                tq1 = sched.dma("gpsimd", qbuf_hi[:], q_spill[0, kvh, :, :], s_qr,
                                deps=rqsp.for_read() + [rqsp.w] + rqbuf.for_write())
                rqbuf.did_write(tq1)
                for i in range(NT):
                    mm_last = []
                    for kg in range(4):
                        d0 = [tq1] + t_kv + rpsA[kg].for_write()
                        t = sched.op("tensor", lambda T, kg=kg, i=i: T.matmul(
                            psA[:, kg * 512:kg * 512 + 512],
                            qbuf_hi[:, i * 128:i * 128 + 128],
                            kbuf_hi[:, kg * 512:kg * 512 + 512],
                            start=True, stop=True), deps=d0, inc=True)
                        rpsA[kg].did_write(t)
                        mm_last.append(t)
                    rqbuf.did_read(mm_last[-1])
                    rkbuf.did_read(mm_last[-1])
                    mx_toks = []
                    for kg in range(4):
                        tmx = sched.op("vector", lambda V, kg=kg: V.reduce_max(
                            out=arow[:, 4 + kg:5 + kg],
                            in_=psA[:, kg * 512:kg * 512 + 512], axis=AX.X,
                            negate=True), deps=[mm_last[kg]] + rarow.for_write())
                        rpsA[kg].did_read(tmx)
                        rarow.did_write(tmx)
                        mx_toks.append(tmx)
                    t_mx = sched.op("vector", lambda V: V.tensor_reduce(
                        out=arow[:, 0:1], in_=arow[:, 4:8], op=ALU.min,
                        axis=AX.X), deps=mx_toks)
                    rarow.did_write(t_mx)
                    t_bias = sched.op("vector", lambda V: V.tensor_tensor(
                        out=arow[:, 1:2], in0=arow[:, 0:1], in1=scl[:, 3:4],
                        op=ALU.mult), deps=[t_mx, t_m])
                    rscl.did_read(t_bias)
                    rarow.did_write(t_bias)
                    if av_ctx8[0] is not None:
                        emit_av8(av_ctx8[0])
                        av_ctx8[0] = None
                    exp_toks = []
                    for g in range(4):
                        t_e = sched.op("scalar", lambda Sc, g=g: Sc.activation(
                            out=p32[:, g * 512:g * 512 + 512],
                            in_=psA[:, g * 512:g * 512 + 512], func=AF.Exp,
                            bias=arow[:, 1:2], scale=scl[:, 3:4],
                            accum_out=arow[:, 8 + g:9 + g]),
                            deps=[t_bias] + rev.for_write() + rev.for_read())
                        rpsA[g].did_read(t_e)
                        rscl.did_read(t_e)
                        rev.did_write(t_e)
                        rarow.did_write(t_e)
                        exp_toks.append(t_e)
                    t_exp = exp_toks[-1]
                    t_sum = sched.op("vector", lambda V: V.reduce_sum(
                        out=arow[:, 2:3], in_=arow[:, 8:12], axis=AX.X),
                        deps=exp_toks)
                    rarow.did_write(t_sum)
                    rc = 3 if i % 2 == 0 else 12
                    t_rec = sched.op("vector", lambda V, rc=rc: V.reciprocal(
                        out=arow[:, rc:rc + 1], in_=arow[:, 2:3]), deps=[t_sum])
                    rarow.did_write(t_rec)
                    pt_done = []
                    for g in range(4):
                        bank = psT[g % 2]; rbank = rpsT[g % 2]
                        mm = []
                        for jj in range(4):
                            ct = g * 4 + jj
                            t = sched.op("tensor", lambda T, ct=ct, jj=jj, bank=bank:
                                         T.transpose(bank[:, jj * 128:jj * 128 + 128],
                                                     p32[:, ct * 128:ct * 128 + 128],
                                                     ident[:]),
                                         deps=[exp_toks[g], t_consts] + rbank.for_write())
                            rev.did_read(t)
                            mm.append(t)
                        rbank.did_write(mm[-1])
                        t_hi = sched.op("vector", lambda V, g=g, bank=bank: V.tensor_copy(
                            out=pT_hi[:, g * 512:g * 512 + 512], in_=bank[:]),
                            deps=mm + rpT.for_write())
                        t_lo = sched.op("vector", lambda V, g=g, bank=bank: V.tensor_tensor(
                            out=pT_lo[:, g * 512:g * 512 + 512], in0=bank[:],
                            in1=pT_hi[:, g * 512:g * 512 + 512], op=ALU.subtract),
                            deps=[t_hi])
                        rbank.did_read(t_lo)
                        pt_done.append(t_lo)
                    rpT.did_write(pt_done[-1])
                    av_ctx8[0] = (rc, kvh * TPC + i * 128,
                                  [tq1] + list(t_kv) + [t_rec, t_m], pt_done)
                if av_ctx8[0] is not None:
                    emit_av8(av_ctx8[0])
                    av_ctx8[0] = None

            rd_toks = [(s_kvr, sched.dma_cnt.get(id(s_kvr), 0)),
                       (s_kvv, sched.dma_cnt.get(id(s_kvv), 0))]
            RT["prep"] += 1
            pv2 = RT["prep"]

            def signal_rd(G, pv=pv2):
                ins = G.remote_sem_update_broadcast(
                    kv_done, lsem, rdests=[(0, 1)] + [None] * 7,
                    sim_remote_sem_inc=2)
                ins.then_inc(prep_sem, 1)
                G.wait_ge(prep_sem, pv)
                G.trigger_dma(count=1)

            sched.raw("gpsimd", signal_rd, deps=rd_toks)
            rkv.did_read(rd_toks[0])
            rkv.did_read(rd_toks[1])
            RT["kv_rsig"] += 2

            bo = layer["bo"]
            t_bo = sched.dma("gpsimd", ev32b[0:1, 0:1024], bo[0:1, 0:1024], s_qs,
                             deps=revb.for_write() + revb.for_read())
            t_bo2 = sched.dma("gpsimd", ev32[0:1, 0:1024], bo[0:1, 1024:2048], s_qs,
                              deps=rev.for_write() + rev.for_read())
            rev.did_write(t_bo2)
            revb.did_write(t_bo)
            attn_dep = rfeatT.for_read() + [rfeatT.w]
            for og in range(4):
                bsrc = ev32b[0:1, og * 512:og * 512 + 512] if og < 2 else \
                    ev32[0:1, (og - 2) * 512:(og - 2) * 512 + 512]
                sh, sl = split_piece(evh[0:1, 0:512], evl[0:1, 0:512], bsrc,
                                     deps=[t_bo, t_bo2] + revh.for_write())
                revh.did_write(sl)
                (revb if og < 2 else rev).did_read(sl)
                for dc in range(NDC):
                    buf = dc % 2
                    sp = stream_rhs_piece(layer["wo"], dc * 128, og * 512, 512, buf)
                    tm = None
                    for i in range(NT):
                        tm = mm3(banks[i], rbanks[i],
                                 featT_hi[:, dc * TPC + i * 128: dc * TPC + i * 128 + 128],
                                 featT_lo[:, dc * TPC + i * 128: dc * TPC + i * 128 + 128],
                                 whi[buf][:, 0:512], wlo[buf][:, 0:512],
                                 start=(dc == 0), stop=False,
                                 deps=[sp] + (attn_dep if dc == 0 else []),
                                 mark=(i == NT - 1))
                    rwsp[buf].did_read(tm)
                    rfeatT.did_read(tm)
                for i in range(NT):
                    sched.op("tensor", lambda T, i=i: T.matmul(
                        banks[i], onesrow[0:1, :], evh[0:1, 0:512],
                        start=False, stop=False), deps=[sh, sl, t_consts], inc=False)
                    t_f = sched.op("tensor", lambda T, i=i: T.matmul(
                        banks[i], onesrow[0:1, :], evl[0:1, 0:512],
                        start=False, stop=True), inc=True)
                    rbanks[i].did_write(t_f)
                    revh.did_read(t_f)
                for i in range(NT):
                    t = sched.op("vector", lambda V, i=i, og=og, bank=banks[i]:
                                 V.tensor_tensor(out=xres[i][:, og * 512:og * 512 + 512],
                                                 in0=bank,
                                                 in1=xres[i][:, og * 512:og * 512 + 512],
                                                 op=ALU.add),
                                 deps=rbanks[i].for_read() + [rbanks[i].w]
                                 + rxres[i].for_write())
                    rbanks[i].did_read(t)
                    rxres[i].did_write(t)

        L0A = {"wq": wi["l0a_q"], "wk": wi["l0a_k"], "wv": wi["l0a_v"],
               "wo": wi["l0a_o"], "nkv": 4, "rope": "a"}
        L0B = {"wq": wi["l0b_q"], "wk": wi["l0b_k"], "wv": wi["l0b_v"],
               "wo": wi["l0b_o"], "nkv": 4, "rope": "b"}
        if n_sub >= 1:
            sub_attn(L0A)
        if n_sub >= 2:
            sub_ffn()
        if n_sub >= 3:
            sub_attn(L0B)
        if n_sub >= 4:
            sub_ffn()
        L1A = {"wq": wi["l1a_q"], "wk": wi["l1a_k"], "wv": wi["l1a_v"],
               "wo": wi["l1a_o"], "nkv": 16, "rope": None, "alibi": True,
               "bo": wi["l1a_bo"]}
        if n_sub >= 5:
            sub_attn(L1A)
        if n_sub >= 6:
            sub_moe()
        L1B = {"wq": wi["l1b_qkv_q"], "wk": wi["l1b_qkv_k"], "wv": wi["l1b_qkv_v"],
               "wo": wi["l1b_o"], "bo": wi["l1b_bo"]}
        if n_sub >= 7:
            sub_attn_fp8(L1B)
        if n_sub >= 8:
            sub_moe()

        bar = sched.barrier()
        st = [sched.dma("sync", out_h[i * 128:(i + 1) * 128, :], xres[i][:], s_out,
                        deps=bar) for i in range(NT)]
        sched.raw("sync", lambda E: E.wait_ge(s_out, 16 * NT), deps=st)

        with nc.Block() as block:
            sched.finalize(block)
    nc.compile()
    return nc


def _prep_inputs(inputs):
    cos, sin = rope_tables()
    sin_a = sin.copy()
    sin_a[:, 0:64] = -sin_a[:, 0:64]
    cosT = np.ascontiguousarray(cos.T)
    sinaT = np.ascontiguousarray(sin_a.T)
    sinbT = np.ascontiguousarray(sin.T)

    ident = np.eye(128, dtype=np.float32)
    iota8 = np.zeros((128, 16), dtype=np.float32)
    iota8[:, 0:8] = np.arange(8, dtype=np.float32)[None, :]
    iota8[:, 8:16] = 9.0 - np.arange(8, dtype=np.float32)[None, :]

    f32 = lambda k: np.asarray(inputs[k], dtype=np.float32)
    rep = lambda v, n: np.ascontiguousarray(np.broadcast_to(
        np.asarray(v, dtype=np.float32)[None, :], (n, len(v))))
    shared = {
        "ident": ident,
        "iota8": iota8,
        "ones4": np.ones((1, 4), dtype=BF),
        "onesrow": np.ones((1, 128), dtype=BF),
        "l0a_q": lhsT_layout(np.ascontiguousarray(f32("L0_attn_a_Wq").T)),
        "l0a_k": lhsT_layout(np.ascontiguousarray(f32("L0_attn_a_Wk").T)),
        "l0a_v": np.ascontiguousarray(f32("L0_attn_a_Wv").T),
        "l0a_o": np.ascontiguousarray(f32("L0_attn_a_Wo").T),
        "l0b_q": lhsT_layout(f32("L0_attn_b_Wq")),
        "l0b_k": lhsT_layout(f32("L0_attn_b_Wk")),
        "l0b_v": f32("L0_attn_b_Wv"),
        "l0b_o": f32("L0_attn_b_Wo"),
        "l0f_ing": rep(f32("L0_ffn_in_gamma"), 128),
        "l0f_inb": rep(f32("L0_ffn_in_beta"), 128),
        "l0f_png": rep(f32("L0_ffn_pn_gamma"), 4),
        "l0f_pnb": rep(f32("L0_ffn_pn_beta"), 4),
        "l0f_outg": rep(f32("L0_ffn_out_gamma"), 128),
        "l0f_outb": rep(f32("L0_ffn_out_beta"), 128),
        "l0f_agg": f32("L0_ffn_agg_w"),
        "l0f_d2b": rep(f32("L0_ffn_d2_b"), 4),
        "l0f_gate": f32("L0_ffn_gate_w"),
        "l0f_gateb": f32("L0_ffn_gate_b")[None, :],
        "l1a_q": lhsT_layout(f32("L1_attn_a_Wq")),
        "l1a_k": lhsT_layout(f32("L1_attn_a_Wk")),
        "l1a_v": f32("L1_attn_a_Wv"),
        "l1a_o": f32("L1_attn_a_Wo"),
        "l1a_bo": f32("L1_attn_a_bo")[None, :],
        "l1f_wg": f32("L1_ffn_wg"),
        "l1f_w": np.ascontiguousarray(f32("L1_ffn_W").transpose(0, 2, 1)),
        "l1f_b": f32("L1_ffn_b"),
        "l1b_qkv_q": lhsT_layout(np.ascontiguousarray(f32("L1_attn_b_W_qkv")[:, 0:2048])),
        "l1b_qkv_k": lhsT_layout(np.ascontiguousarray(f32("L1_attn_b_W_qkv")[:, 2048:4096])),
        "l1b_qkv_v": np.ascontiguousarray(f32("L1_attn_b_W_qkv")[:, 4096:6144]),
        "l1b_o": f32("L1_attn_b_W_out"),
        "l1b_bo": f32("L1_attn_b_b_out")[None, :],
    }

    x = f32("L0_attn_a_x")

    in_maps = []
    for c in range(NCORE):
        b, h = c // 2, c % 2
        blocks = [h + 2 * i for i in range(NT)]
        rows = np.concatenate([np.arange(w * 128, w * 128 + 128) for w in blocks])
        m = dict(shared)
        m["x"] = np.ascontiguousarray(x[b].reshape(S, D)[rows])
        m["coreinfo"] = np.array([[h, 1 - h, b, c, 0, 0, 0, 0]], dtype=np.int32)
        m["ropeC"] = np.ascontiguousarray(cosT[:, rows])
        m["ropeSa"] = np.ascontiguousarray(sinaT[:, rows])
        m["ropeSb"] = np.ascontiguousarray(sinbT[:, rows])
        m["alibi"] = alibi_core_table(h)
        m["masks"] = np.array([[1.0 if h == 1 else 0.0] + [0.0] * 7], dtype=np.float32)
        m["coreinfo"][0, 4] = c * 256
        bsel = np.zeros((4, 128), dtype=BF)
        bsel[b, :] = BF(1.0)
        m["bsel"] = bsel
        bsel1 = np.zeros((1, 4), dtype=BF)
        bsel1[0, b] = BF(1.0)
        m["bsel1"] = bsel1
        aggb_my = np.zeros((4, D), dtype=np.float32)
        if h == 1:
            aggb_my[b, :] = f32("L0_ffn_agg_b")
        m["l0f_aggb_my"] = aggb_my
        m["l0f_d1"] = np.ascontiguousarray(f32("L0_ffn_d1_w")[:, c * 1024:(c + 1) * 1024])
        m["l0f_d1b"] = rep(f32("L0_ffn_d1_b")[c * 1024:(c + 1) * 1024], 4)
        m["l0f_d2"] = np.ascontiguousarray(f32("L0_ffn_d2_w")[c * 1024:(c + 1) * 1024, :])
        m["l0f_bc"] = np.ascontiguousarray(f32("L0_ffn_bc_w")[:, c * 256:(c + 1) * 256])
        m["l0f_bcb"] = rep(f32("L0_ffn_bc_b")[c * 256:(c + 1) * 256], 4)
        for name, (shape, dt) in INPUT_DECLS.items():
            if name not in m:
                m[name] = np.zeros(shape, dtype=mybir.dt.np(dt))
        in_maps.append(m)
    return in_maps


def _scatter_out(results):
    out = np.zeros((B, S, D), dtype=np.float32)
    for c in range(NCORE):
        b, h = c // 2, c % 2
        blocks = [h + 2 * i for i in range(NT)]
        rows = np.concatenate([np.arange(w * 128, w * 128 + 128) for w in blocks])
        out[b].reshape(S, D)[rows] = results[c]["out"]
    return out


def kernel(**inputs):
    nc = build_program(N_SUB)
    in_maps = _prep_inputs(inputs)
    res = run_bass_kernel_spmd(nc, in_maps, core_ids=list(range(NCORE)))
    return _scatter_out(res.results)


# version 09
# speedup vs baseline: 1.0219x; 1.0014x over previous
import math
import numpy as np
import ml_dtypes
import bass_rust
import concourse.bass as bass
import concourse.bacc as bacc
import concourse.mybir as mybir
from concourse.bass_utils import run_bass_kernel_spmd
from concourse import library_config
from contextlib import ExitStack

F32 = mybir.dt.float32
BF16 = mybir.dt.bfloat16
I32 = mybir.dt.int32
FP8 = mybir.dt.float8e4
AF = mybir.ActivationFunctionType
ALU = mybir.AluOpType
AX = mybir.AxisListType
ds = bass.ds
BF = ml_dtypes.bfloat16

NCORE = 8
B, S, D = 4, 2048, 2048
TPC = 1024
NT = 8
NDC = D // 128
HD = 128
NEG = -1e30
N_SUB = 8

ALIBI_BASE0 = 1920
ALIBI_W = 4352


def mkap(ap, offset, pattern):
    return bass_rust.AP(tensor=ap.tensor, offset=int(offset),
                        ap=[[int(a), int(b)] for a, b in pattern])


class Res:
    __slots__ = ("w", "r")

    def __init__(self):
        self.w = None
        self.r = []

    def for_read(self):
        return [self.w] if self.w else []

    def for_write(self):
        out = list(self.r)
        if self.w:
            out.append(self.w)
        return out

    def did_read(self, tok):
        if tok:
            self.r.append(tok)

    def did_write(self, tok):
        if tok:
            self.w = tok
            self.r = []


class Sched:
    ENGINES = ("sync", "tensor", "vector", "scalar", "gpsimd")

    def __init__(self, nc, es):
        self.nc = nc
        self.q = {e: [] for e in self.ENGINES}
        self.psem = {e: [es.enter_context(nc.semaphore(f"p_{e}0")),
                         es.enter_context(nc.semaphore(f"p_{e}1"))] for e in self.ENGINES}
        self.pcnt = {e: 0 for e in self.ENGINES}
        self.bsem = {e: es.enter_context(nc.semaphore(f"b_{e}")) for e in self.ENGINES}
        self.bcnt = 0
        self.dma_cnt = {}
        self.dma_issued = {e: {} for e in self.ENGINES}
        self.last_wait = {e: {} for e in self.ENGINES}

    def _waits(self, eng, deps):
        lw = self.last_wait[eng]
        best = {}
        for tok in deps:
            if tok is None:
                continue
            semh, val = tok
            k = id(semh)
            if val > best.get(k, (None, -1))[1]:
                best[k] = (semh, val)
        out = []
        for k, (semh, val) in best.items():
            if lw.get(k, -1) >= val:
                continue
            lw[k] = val
            out.append((semh, val))
        return out

    def op(self, eng, emit, deps=(), inc=True):
        waits = self._waits(eng, deps)
        tok = None
        if inc:
            k = self.pcnt[eng]
            self.pcnt[eng] += 1
            tok = (self.psem[eng][k % 2], k // 2 + 1)
        self.q[eng].append(("op", waits, emit, tok))
        return tok

    def mark(self, eng):
        return self.op(eng, lambda E: E.nop(), inc=True)

    def dma(self, eng, out, in_, sem, deps=()):
        waits = self._waits(eng, deps)
        k = id(sem)
        self.dma_cnt[k] = self.dma_cnt.get(k, 0) + 16
        tok = (sem, self.dma_cnt[k])
        self.dma_issued[eng][k] = tok
        self.q[eng].append(("dma", waits, (out, in_), tok))
        return tok

    def raw(self, eng, emit, deps=()):
        waits = self._waits(eng, deps)
        self.q[eng].append(("raw", waits, emit, None))

    def barrier(self):
        self.bcnt += 1
        k = self.bcnt
        for e in self.ENGINES:
            waits = []
            n = self.pcnt[e]
            if (n + 1) // 2 > 0:
                waits.append((self.psem[e][0], (n + 1) // 2))
            if n // 2 > 0:
                waits.append((self.psem[e][1], n // 2))
            waits.extend(self.dma_issued[e].values())
            semb = self.bsem[e]

            def emit(E, waits=tuple(waits), semb=semb):
                for s, v in waits:
                    E.wait_ge(s, v)
                E.nop().then_inc(semb, 1)

            self.raw(e, emit)
        return [(self.bsem[e], k) for e in self.ENGINES]

    def finalize(self, block):
        def mk(eng_name):
            def emit_stream(E):
                for kind, waits, payload, tok in self.q[eng_name]:
                    for semh, val in waits:
                        E.wait_ge(semh, val)
                    if kind == "op":
                        ins = payload(E)
                    elif kind == "dma":
                        out, in_ = payload
                        if callable(out):
                            out = out()
                        if callable(in_):
                            in_ = in_()
                        ins = E.dma_start(out=out, in_=in_)
                    else:
                        payload(E)
                        continue
                    if tok is not None:
                        ins.then_inc(tok[0], 16 if kind == "dma" else 1)
            return emit_stream

        for name in self.ENGINES:
            getattr(block, name)(mk(name))


def rope_tables():
    inv_freq = (1.0 / (10000.0 ** (np.arange(0, HD, 2, dtype=np.float32)
                                   / np.float32(HD)))).astype(np.float32)
    t = np.arange(S, dtype=np.float32)
    freqs = (t[:, None] * inv_freq[None, :]).astype(np.float32)
    emb = np.concatenate([freqs, freqs], axis=-1)
    cos = np.cos(emb.astype(np.float64)).astype(np.float32)
    sin = np.sin(emb.astype(np.float64)).astype(np.float32)
    return cos, sin


def alibi_core_table(h):
    slopes = (2.0 ** (-8.0 * np.arange(1, 17) / 16)).astype(np.float32)
    r = np.arange(30)[:, None, None]
    p = np.arange(128)[None, :, None]
    f = np.arange(128)[None, None, :]
    delta = ((r - 14) * 128 + f - p - h * 128).astype(np.float32)
    tbl = slopes[:, None, None, None] * delta[None]
    tbl = np.where(delta[None] <= 0, tbl, np.float32(NEG)).astype(np.float32)
    return np.ascontiguousarray(tbl)


def lhsT_layout(w):
    din, dout = w.shape
    return np.ascontiguousarray(w.reshape(din, dout // 128, 128).transpose(1, 0, 2))


INPUT_DECLS = {}


def build_program(n_sub=N_SUB):
    INPUT_DECLS.clear()
    nc = bacc.Bacc("TRN2", target_bir_lowering=False, debug=False)

    def din(name, shape, dt=F32):
        INPUT_DECLS[name] = (tuple(shape), dt)
        return nc.dram_tensor(name, shape, dt, kind="ExternalInput").ap()

    x_in = din("x", [TPC, D])
    info_in = din("coreinfo", [1, 8], I32)
    ropeC_in = din("ropeC", [128, TPC])
    ropeSa_in = din("ropeSa", [128, TPC])
    ropeSb_in = din("ropeSb", [128, TPC])
    ident_in = din("ident", [128, 128])
    alibi_in = din("alibi", [16, 30, 128, 128])
    masks_in = din("masks", [1, 8])
    bsel_in = din("bsel", [4, 128], BF16)
    bsel1_in = din("bsel1", [1, 4], BF16)
    ones4_in = din("ones4", [1, 4], BF16)
    ones_in = din("onesrow", [1, 128], BF16)
    iota8_in = din("iota8", [128, 16])

    wi = {}

    def wdef(name, shape, dt=F32):
        wi[name] = din(name, shape, dt)

    wdef("l0a_q", [NDC, D, 128]); wdef("l0a_k", [4, D, 128])
    wdef("l0a_v", [D, 512]); wdef("l0a_o", [D, D])
    wdef("l0f_ing", [128, D]); wdef("l0f_inb", [128, D])
    wdef("l0f_png", [4, D]); wdef("l0f_pnb", [4, D])
    wdef("l0f_outg", [128, D]); wdef("l0f_outb", [128, D])
    wdef("l0f_agg", [D, D]); wdef("l0f_aggb_my", [4, D])
    wdef("l0f_d1", [D, 1024]); wdef("l0f_d1b", [4, 1024])
    wdef("l0f_d2", [1024, D]); wdef("l0f_d2b", [4, D])
    wdef("l0f_bc", [D, 256]); wdef("l0f_bcb", [4, 256])
    wdef("l0f_gate", [D, D]); wdef("l0f_gateb", [1, D])
    wdef("l0b_q", [NDC, D, 128]); wdef("l0b_k", [4, D, 128])
    wdef("l0b_v", [D, 512]); wdef("l0b_o", [D, D])
    wdef("l1a_q", [NDC, D, 128]); wdef("l1a_k", [NDC, D, 128])
    wdef("l1a_v", [D, D]); wdef("l1a_o", [D, D]); wdef("l1a_bo", [1, D])
    wdef("l1f_wg", [D, 8]); wdef("l1f_w", [8, D, D]); wdef("l1f_b", [8, D])
    wdef("l1b_qkv_q", [NDC, D, 128]); wdef("l1b_qkv_k", [NDC, D, 128])
    wdef("l1b_qkv_v", [D, D]); wdef("l1b_o", [D, D]); wdef("l1b_bo", [1, D])

    out_h = nc.dram_tensor("out", [TPC, D], F32, kind="ExternalOutput").ap()

    kt_hi_sh = nc.dram_tensor("kt_hi_sh", [2, 2048, 1024], BF16, kind="Internal",
                              addr_space="Shared").ap()
    kt_lo_sh = nc.dram_tensor("kt_lo_sh", [2, 2048, 1024], BF16, kind="Internal",
                              addr_space="Shared").ap()
    vp_sh = nc.dram_tensor("vp_sh", [2, 1024, 16, 256], BF16, kind="Internal",
                           addr_space="Shared").ap()
    q_spill = nc.dram_tensor("q_spill", [2, NDC, 128, TPC], BF16, kind="Internal").ap()
    kt_hi_st = nc.dram_tensor("kt_hi_st", [2048, 1024], BF16, kind="Internal").ap()
    kt_lo_st = nc.dram_tensor("kt_lo_st", [2048, 1024], BF16, kind="Internal").ap()
    vp_st = nc.dram_tensor("vp_st", [1024, 16, 256], BF16, kind="Internal").ap()
    g_spill = nc.dram_tensor("g_spill", [NT, 128, 2048], F32, kind="Internal").ap()
    qkv_f32 = nc.dram_tensor("qkv_f32", [3, 16, 128, 1024], F32, kind="Internal").ap()
    ccm_in = nc.dram_tensor("ccm_in", [4, 1], F32, kind="Internal").ap()
    ccm_out = nc.dram_tensor("ccm_out", [4, 1], F32, kind="Internal").ap()
    bsc = nc.dram_tensor("bsc", [4, 3], F32, kind="Internal").ap()
    cc_in = nc.dram_tensor("cc_in", [4, 2048], F32, kind="Internal").ap()
    cc_out = nc.dram_tensor("cc_out", [4, 2048], F32, kind="Internal").ap()

    es = ExitStack()
    with es:
        sbuf = lambda name, shape, dt=F32: es.enter_context(nc.sbuf_tensor(name, shape, dt))
        sem = lambda name: es.enter_context(nc.semaphore(name))

        xres = [sbuf(f"xres{i}", [128, D]) for i in range(NT)]
        featT_hi = sbuf("featT_hi", [128, NDC * TPC], BF16)
        featT_lo = sbuf("featT_lo", [128, NDC * TPC], BF16)
        ident = sbuf("ident_sb", [128, 128])
        info = sbuf("info_sb", [1, 8], I32)
        stats = sbuf("stats", [128, 64])
        w32 = [sbuf(f"w32_{i}", [128, 1024]) for i in range(2)]
        whi = [sbuf(f"whi_{i}", [128, 2048], BF16) for i in range(2)]
        wlo = [sbuf(f"wlo_{i}", [128, 2048], BF16) for i in range(2)]
        ev32 = sbuf("ev32", [128, 2048])
        ev32b = sbuf("ev32b", [128, 1024])
        evh = sbuf("evh", [128, 512], BF16)
        evl = sbuf("evl", [128, 512], BF16)
        kbuf_hi = sbuf("kbuf_hi", [128, 2048], BF16)
        kbuf_lo = sbuf("kbuf_lo", [128, 2048], BF16)
        vbuf = sbuf("vbuf", [128, 4096], BF16)
        xln = ev32
        p32 = ev32
        rope_c = kbuf_hi[:].bitcast(F32)
        rope_s = kbuf_lo[:].bitcast(F32)
        qbuf_hi = sbuf("qbuf_hi", [128, TPC], BF16)
        qbuf_lo = sbuf("qbuf_lo", [128, TPC], BF16)
        pT_hi = sbuf("pT_hi", [128, 2048], BF16)
        pT_lo = sbuf("pT_lo", [128, 2048], BF16)
        arow = sbuf("arow", [128, 16])
        scr2 = sbuf("scr2", [128, 2048])
        moe_c = sbuf("moe_c", [128, 64])
        vbuf32 = vbuf[:].bitcast(F32)
        kbuf_hi32 = kbuf_hi[:].bitcast(F32)
        kbuf_lo32 = kbuf_lo[:].bitcast(F32)
        pT_hi32 = pT_hi[:].bitcast(F32)
        pT_lo32 = pT_lo[:].bitcast(F32)

        psA = es.enter_context(nc.psum_tensor("psA", [128, 2048], F32))
        psB = [es.enter_context(nc.psum_tensor(f"psB{i}", [128, 512], F32)) for i in range(2)]
        psT = [es.enter_context(nc.psum_tensor(f"psT{i}", [128, 512], F32)) for i in range(2)]

        R = {}

        def res(name):
            if name not in R:
                R[name] = Res()
            return R[name]

        rxres = [res(f"xres{i}") for i in range(NT)]
        rfeatT = res("featT"); rstats = res("stats");         rw32 = [res(f"w32_{i}") for i in range(2)]
        rwsp = [res(f"wsp_{i}") for i in range(2)]
        rev = res("ev32"); revb = res("ev32b"); revh = res("evh")
        rkbuf = res("kbuf"); rvbuf = res("vbuf"); rqbuf = res("qbuf")
        rpT = res("pT"); rarow = res("arow")
        rpsA = [res(f"psA{i}") for i in range(4)]
        rpsB = [res(f"psB{i}") for i in range(2)]
        rpsT = [res(f"psT{i}") for i in range(2)]
        rkv = res("kv_sh"); rqsp = res("q_spill"); rgsp = res("g_spill")
        rmoec = res("moe_c")

        s_x = sem("s_x")
        s_w = [sem(f"s_w{i}") for i in range(2)]
        s_kvw = sem("s_kvw"); s_kvr = sem("s_kvr"); s_kvv = sem("s_kvv")
        s_kvb = sem("s_kvb")
        s_qs = sem("s_qs"); s_qr = sem("s_qr")
        s_out = sem("s_out"); s_rope = sem("s_rope"); s_misc = sem("s_misc")
        s_alibi = sem("s_alibi")
        s_rot = sem("s_rot")
        kv_ready = sem("kv_ready")
        kv_done = sem("kv_done")
        prep_sem = sem("prep_sem"); lsem = sem("lsem")
        ccsem = sem("ccsem")

        sched = Sched(nc, es)
        RT = {"kv_sig": 0, "prep": 0, "kv_rsig": 0}
        parv = {}

        onesrow = sbuf("ones_sb", [1, 128], BF16)
        bsel1 = sbuf("bsel1_sb", [1, 4], BF16)
        ones4 = sbuf("ones4_sb", [1, 4], BF16)
        bselr = sbuf("bselr_sb", [4, 128], BF16)
        masks_sb = sbuf("masks_sb", [1, 8])
        iota8_sb = sbuf("iota8_sb", [128, 16])
        scl = sbuf("scl_sb", [128, 8])
        t_info = sched.dma("sync", info[:], info_in[:, :], s_misc)
        t_ident = sched.dma("sync", ident[:], ident_in[:, :], s_misc)
        sched.dma("sync", onesrow[:], ones_in[:, :], s_misc)
        sched.dma("sync", bsel1[:], bsel1_in[:, :], s_misc)
        sched.dma("sync", ones4[:], ones4_in[:, :], s_misc)
        sched.dma("sync", bselr[:], bsel_in[:, :], s_misc)
        sched.dma("sync", iota8_sb[:], iota8_in[:, :], s_misc)
        t_masks = sched.dma("sync", masks_sb[:], masks_in[:, :], s_misc)
        t_consts = t_masks
        t_x = [sched.dma("sync", xres[i][:], x_in[i * 128:(i + 1) * 128, :], s_x)
               for i in range(NT)]
        for i in range(NT):
            rxres[i].did_write(t_x[i])

        colv = {}
        dyn = {}

        def load_parity(G):
            reg = G.alloc_register("par_reg")
            G.load(reg, info[0:1, 0:1])
            parv["v"] = G.snap(reg, min_val=0, max_val=1)
            reg2 = G.alloc_register("col_reg")
            G.load(reg2, info[0:1, 4:5])
            colv["v"] = G.snap(reg2, min_val=0, max_val=2048 - 256,
                               guaranteed_mod_val=256)
            dyn["kt_hi"] = kt_hi_sh[ds(parv["v"], 1)]
            dyn["kt_lo"] = kt_lo_sh[ds(parv["v"], 1)]
            dyn["vp"] = vp_sh[ds(parv["v"], 1)]

        sched.raw("gpsimd", load_parity, deps=[t_consts])

        def ln_stats(eps, srcs, rsrcs):
            deps = []
            for rs in rsrcs:
                deps += rs.for_read()
            tks = []
            for i in range(NT):
                t = sched.op("vector", lambda V, i=i: V.reduce_sum(
                    out=stats[:, i:i + 1], in_=srcs[i][:], axis=AX.X),
                    deps=deps + rstats.for_write())
                rstats.did_write(t)
                rsrcs[i].did_read(t)
                tks.append(t)
            t_negmu = sched.op("scalar", lambda Sc: Sc.activation(
                out=stats[:, 0:8], in_=stats[:, 0:8], func=AF.Copy, scale=-1.0 / D),
                deps=tks + rstats.for_write())
            rstats.did_write(t_negmu)
            vks = []
            for i in range(NT):
                t = sched.op("scalar", lambda Sc, i=i: Sc.activation(
                    out=ev32[:, 0:2048], in_=srcs[i][:], func=AF.Square,
                    bias=stats[:, i:i + 1], scale=1.0,
                    accum_out=stats[:, 16 + i:17 + i]),
                    deps=[t_negmu] + rev.for_write() + rstats.for_write())
                rstats.did_write(t)
                rsrcs[i].did_read(t)
                rev.did_write(t)
                vks.append(t)
            t_var = sched.op("vector", lambda V: V.tensor_scalar(
                out=stats[:, 24:32], in0=stats[:, 16:24], scalar1=1.0 / D,
                scalar2=float(eps), op0=ALU.mult, op1=ALU.add), deps=vks)
            rstats.did_write(t_var)
            t_s0 = sched.op("scalar", lambda Sc: Sc.activation(
                out=stats[:, 32:40], in_=stats[:, 24:32], func=AF.Sqrt), deps=[t_var])
            rstats.did_write(t_s0)
            t_r0 = sched.op("vector", lambda V: V.reciprocal(
                out=stats[:, 40:48], in_=stats[:, 32:40]), deps=[t_s0])
            rstats.did_write(t_r0)
            t_h = sched.op("vector", lambda V: V.tensor_tensor(
                out=stats[:, 48:56], in0=stats[:, 24:32], in1=stats[:, 40:48],
                op=ALU.mult), deps=[t_r0, t_var])
            t_s1 = sched.op("vector", lambda V: V.tensor_tensor(
                out=stats[:, 48:56], in0=stats[:, 48:56], in1=stats[:, 32:40],
                op=ALU.add), deps=[t_h, t_s0])
            t_s1b = sched.op("scalar", lambda Sc: Sc.activation(
                out=stats[:, 48:56], in_=stats[:, 48:56], func=AF.Copy, scale=0.5),
                deps=[t_s1])
            rstats.did_write(t_s1b)
            t_inv = sched.op("vector", lambda V: V.reciprocal(
                out=stats[:, 8:16], in_=stats[:, 48:56]), deps=[t_s1b])
            rstats.did_write(t_inv)
            return t_inv

        def ln_apply(i, srcs, rsrcs):
            t = sched.op("vector", lambda V, i=i: V.tensor_scalar(
                out=xln[:], in0=srcs[i][:], scalar1=stats[:, i:i + 1],
                scalar2=stats[:, 8 + i:9 + i], op0=ALU.add, op1=ALU.mult),
                deps=rsrcs[i].for_read() + rstats.for_read() + rev.for_write())
            rsrcs[i].did_read(t)
            rstats.did_read(t)
            rev.did_write(t)
            return t

        def build_featT(eps, srcs=None, rsrcs=None):
            srcs = srcs or xres
            rsrcs = rsrcs or rxres
            ln_stats(eps, srcs, rsrcs)
            last = None
            for i in range(NT):
                t_ln = ln_apply(i, srcs, rsrcs)
                for g in range(NDC // 4):
                    bank = psT[g % 2]
                    rbank = rpsT[g % 2]
                    mm = []
                    for jj in range(4):
                        j = g * 4 + jj
                        t = sched.op("tensor", lambda T, j=j, jj=jj, bank=bank: T.transpose(
                            bank[:, jj * 128:jj * 128 + 128],
                            xln[:, j * 128:(j + 1) * 128], ident[:]),
                            deps=[t_ln, t_consts] + rbank.for_write() + rfeatT.for_write())
                        rev.did_read(t)
                        mm.append(t)
                    rbank.did_write(mm[-1])
                    for jj in range(4):
                        j = g * 4 + jj
                        col = j * TPC + i * 128
                        t_hi = sched.op("vector", lambda V, bank=bank, jj=jj, col=col:
                                        V.tensor_copy(out=featT_hi[:, col:col + 128],
                                                      in_=bank[:, jj * 128:jj * 128 + 128]),
                                        deps=mm)
                        t_lo = sched.op("vector", lambda V, bank=bank, jj=jj, col=col:
                                        V.tensor_tensor(out=featT_lo[:, col:col + 128],
                                                        in0=bank[:, jj * 128:jj * 128 + 128],
                                                        in1=featT_hi[:, col:col + 128],
                                                        op=ALU.subtract), deps=[t_hi])
                        rbank.did_read(t_lo)
                        last = t_lo
            rfeatT.did_write(last)
            return last

        def split_piece(dst_hi_ap, dst_lo_ap, src_ap, deps):
            t_h = sched.op("scalar", lambda Sc: Sc.activation(
                out=dst_hi_ap, in_=src_ap, func=AF.Copy), deps=deps)
            t_l = sched.op("vector", lambda V: V.tensor_tensor(
                out=dst_lo_ap, in0=src_ap, in1=dst_hi_ap, op=ALU.subtract), deps=[t_h])
            return t_h, t_l

        def stream_lhsT_slab(w_ap, oc, buf, extra_deps=()):
            last = None
            for half in range(2):
                t_d = sched.dma("sync", w32[buf][:, 0:1024],
                                mkap(w_ap, oc * D * 128 + half * 8 * 128 * 128,
                                     [[128, 128], [128 * 128, 8], [1, 128]]),
                                s_w[buf],
                                deps=(list(extra_deps) if half == 0 else []) + rw32[buf].for_write())
                rw32[buf].did_write(t_d)
                t_h, t_l = split_piece(
                    whi[buf][:, half * 1024:half * 1024 + 1024],
                    wlo[buf][:, half * 1024:half * 1024 + 1024],
                    w32[buf][:, 0:1024],
                    deps=[t_d] + rwsp[buf].for_write())
                rw32[buf].did_read(t_l)
                rwsp[buf].did_write(t_l)
                last = t_l
            return last

        def stream_rhs_piece(w_ap, row0, col0, ncols, buf, extra_deps=(),
                             flat_base=None, dout=None):
            dout = dout if dout is not None else w_ap.shape[-1]
            base = flat_base if flat_base is not None else row0 * dout + col0
            t_d = sched.dma("sync", w32[buf][:, 0:ncols],
                            mkap(w_ap, base, [[dout, 128], [1, ncols]]),
                            s_w[buf],
                            deps=list(extra_deps) + rw32[buf].for_write())
            rw32[buf].did_write(t_d)
            t_h, t_l = split_piece(whi[buf][:, 0:ncols], wlo[buf][:, 0:ncols],
                                   w32[buf][:, 0:ncols],
                                   deps=[t_d] + rwsp[buf].for_write())
            rw32[buf].did_read(t_l)
            rwsp[buf].did_write(t_l)
            return t_l

        def mm3(bank_ap, rbank, lT_hi, lT_lo, r_hi, r_lo, start, stop, deps=(),
                mark=False):
            d0 = list(deps)
            if start:
                d0 += rbank.for_write()
            sched.op("tensor", lambda T: T.matmul(bank_ap, lT_hi, r_hi,
                                                  start=start, stop=False),
                     deps=d0, inc=False)
            sched.op("tensor", lambda T: T.matmul(bank_ap, lT_hi, r_lo,
                                                  start=False, stop=False), inc=False)
            t3 = sched.op("tensor", lambda T: T.matmul(bank_ap, lT_lo, r_hi,
                                                       start=False, stop=stop),
                          inc=stop or mark)
            if stop:
                rbank.did_write(t3)
            return t3

        def sub_attn(layer):
            nkv = layer["nkv"]
            last_feat = build_featT(1e-5)
            featdep = [last_feat]

            t_rc = t_rs = None
            if layer["rope"]:
                t_rc = sched.dma("sync", rope_c, ropeC_in[:, :], s_rope,
                                 deps=rkbuf.for_write())
                t_rs = sched.dma("sync", rope_s,
                                 (ropeSa_in if layer["rope"] == "a" else ropeSb_in)[:, :],
                                 s_rope, deps=rkbuf.for_write())
                rkbuf.did_write(t_rs)

            def proj_featmajor(w_ap, noc, writer):
                sp_next = stream_lhsT_slab(w_ap, 0, 0, extra_deps=featdep)
                for oc in range(noc):
                    buf = oc % 2
                    sp = sp_next
                    mm_last = []
                    for dc in range(NDC):
                        for tg in range(2):
                            t = mm3(psA[:, tg * 512:tg * 512 + 512], rpsA[tg],
                                    whi[buf][:, dc * 128:dc * 128 + 128],
                                    wlo[buf][:, dc * 128:dc * 128 + 128],
                                    featT_hi[:, dc * TPC + tg * 512: dc * TPC + tg * 512 + 512],
                                    featT_lo[:, dc * TPC + tg * 512: dc * TPC + tg * 512 + 512],
                                    start=(dc == 0), stop=(dc == NDC - 1),
                                    deps=([sp] + featdep) if dc == 0 else ())
                            if dc == NDC - 1:
                                mm_last.append(t)
                    rwsp[buf].did_read(mm_last[-1])
                    rfeatT.did_read(mm_last[-1])
                    if oc + 1 < noc:
                        sp_next = stream_lhsT_slab(w_ap, oc + 1, (oc + 1) % 2,
                                                   extra_deps=featdep)
                    t_ev = []
                    for tg in range(2):
                        t = sched.op("vector", lambda V, tg=tg: V.tensor_copy(
                            out=ev32[:, tg * 512:tg * 512 + 512],
                            in_=psA[:, tg * 512:tg * 512 + 512]),
                            deps=[mm_last[tg]] + rev.for_write())
                        rpsA[tg].did_read(t)
                        t_ev.append(t)
                    rev.did_write(t_ev[-1])
                    if layer["rope"]:
                        if layer["rope"] == "a":
                            t_r1 = sched.dma("gpsimd", ev32b[0:64, 0:1024],
                                             ev32[64:128, 0:1024], s_rot,
                                             deps=t_ev + revb.for_write())
                            t_r2 = sched.dma("gpsimd", ev32b[64:128, 0:1024],
                                             ev32[0:64, 0:1024], s_rot, deps=t_ev)
                        else:
                            t_r1 = sched.dma("gpsimd", ev32b[1:128, 0:1024],
                                             ev32[0:127, 0:1024], s_rot,
                                             deps=t_ev + revb.for_write())
                            t_r2 = sched.dma("gpsimd", ev32b[0:1, 0:1024],
                                             ev32[127:128, 0:1024], s_rot, deps=t_ev)
                        rev.did_read(t_r1); rev.did_read(t_r2)
                        revb.did_write(t_r2)
                        t_m1 = sched.op("vector", lambda V: V.tensor_tensor(
                            out=ev32[:, 0:1024], in0=ev32[:, 0:1024], in1=rope_c,
                            op=ALU.mult), deps=t_ev + [t_rs, t_r1, t_r2])
                        t_m2 = sched.op("vector", lambda V: V.tensor_tensor(
                            out=ev32b[:, 0:1024], in0=ev32b[:, 0:1024], in1=rope_s,
                            op=ALU.mult), deps=[t_r1, t_r2, t_rs])
                        rkbuf.did_read(t_m2)
                        revb.did_write(t_m2)
                        t_fin = sched.op("vector", lambda V: V.tensor_tensor(
                            out=ev32[:, 0:1024], in0=ev32[:, 0:1024], in1=ev32b[:, 0:1024],
                            op=ALU.add), deps=[t_m1, t_m2])
                        rev.did_write(t_fin)
                        revb.did_read(t_fin)
                        src_dep = [t_fin]
                    else:
                        src_dep = t_ev
                    writer(oc, src_dep)

            def q_writer(oc, src_dep):
                t_h, t_l = split_piece(qbuf_hi[:], qbuf_lo[:], ev32[:, 0:1024],
                                       deps=src_dep + rqbuf.for_write())
                rev.did_read(t_l)
                rqbuf.did_write(t_l)
                t1 = sched.dma("gpsimd", q_spill[0, oc, :, :], qbuf_hi[:], s_qs,
                               deps=[t_h])
                t2 = sched.dma("gpsimd", q_spill[1, oc, :, :], qbuf_lo[:], s_qs,
                               deps=[t_l])
                rqbuf.did_read(t1); rqbuf.did_read(t2)
                rqsp.did_write(t2)

            def k_writer(oc, src_dep):
                t_h, t_l = split_piece(qbuf_hi[:], qbuf_lo[:], ev32[:, 0:1024],
                                       deps=src_dep + rqbuf.for_write())
                rev.did_read(t_l)
                rqbuf.did_write(t_l)
                t1 = sched.dma("gpsimd", kt_hi_st[oc * 128:(oc + 1) * 128, :],
                               qbuf_hi[:], s_kvw, deps=[t_h])
                t2 = sched.dma("gpsimd", kt_lo_st[oc * 128:(oc + 1) * 128, :],
                               qbuf_lo[:], s_kvw, deps=[t_l])
                rqbuf.did_read(t1); rqbuf.did_read(t2)
                rkv.did_write(t2)

            t_prevread = (kv_done, RT["kv_rsig"]) if RT["kv_rsig"] else None
            rkv.did_write(t_prevread) if t_prevread else None

            proj_featmajor(layer["wq"], NDC, q_writer)
            proj_featmajor(layer["wk"], nkv, k_writer)

            banks = [psA[:, 0:512], psA[:, 512:1024], psA[:, 1024:1536],
                     psA[:, 1536:2048], psB[0][:], psB[1][:], psT[0][:], psT[1][:]]
            rbanks = rpsA + rpsB + rpsT
            kvd = nkv * 128
            for og in range(kvd // 512):
                for dc in range(NDC):
                    buf = dc % 2
                    sp = stream_rhs_piece(layer["wv"], dc * 128, og * 512, 512, buf)
                    tm = None
                    for i in range(NT):
                        tm = mm3(banks[i], rbanks[i],
                            featT_hi[:, dc * TPC + i * 128: dc * TPC + i * 128 + 128],
                            featT_lo[:, dc * TPC + i * 128: dc * TPC + i * 128 + 128],
                            whi[buf][:, 0:512], wlo[buf][:, 0:512],
                            start=(dc == 0), stop=(dc == NDC - 1),
                            deps=[sp] + (featdep if dc == 0 else []),
                            mark=(i == NT - 1))
                    rwsp[buf].did_read(tm)
                    rfeatT.did_read(tm)
                for i in range(NT):
                    t_h, t_l = split_piece(evh[:], evl[:], banks[i],
                                           deps=rbanks[i].for_read() + revh.for_write())
                    rbanks[i].did_read(t_l)
                    revh.did_write(t_l)
                    t1 = sched.dma("gpsimd",
                                   vp_st[i * 128:(i + 1) * 128, og * 4:og * 4 + 4, 0:128],
                                   evh[:], s_kvw, deps=[t_h])
                    t2 = sched.dma("gpsimd",
                                   vp_st[i * 128:(i + 1) * 128, og * 4:og * 4 + 4, 128:256],
                                   evl[:], s_kvw, deps=[t_l])
                    revh.did_read(t1); revh.did_read(t2)
                    rkv.did_write(t2)

            kvst_tok = (s_kvw, sched.dma_cnt[id(s_kvw)])
            tb1 = sched.dma("gpsimd", lambda: dyn["kt_hi"][0, 0:kvd, :],
                            kt_hi_st[0:kvd, :], s_kvb, deps=[kvst_tok] + rkv.for_write())
            tb2 = sched.dma("gpsimd", lambda: dyn["kt_lo"][0, 0:kvd, :],
                            kt_lo_st[0:kvd, :], s_kvb, deps=[kvst_tok])
            tb3 = sched.dma("gpsimd", lambda nh=kvd // 128: dyn["vp"][0, :, 0:nh, :],
                            vp_st[:, 0:kvd // 128, :], s_kvb, deps=[kvst_tok])
            rkv.did_write(tb3)
            kvw_tok = tb3
            RT["prep"] += 1
            pv = RT["prep"]

            def signal(G, pv=pv):
                ins = G.remote_sem_update_broadcast(
                    kv_ready, lsem, rdests=[(0, 1)] + [None] * 7,
                    sim_remote_sem_inc=2)
                ins.then_inc(prep_sem, 1)
                G.wait_ge(prep_sem, pv)
                G.trigger_dma(count=1)

            sched.raw("gpsimd", signal, deps=[kvw_tok])
            RT["kv_sig"] += 2
            t_peer = (kv_ready, RT["kv_sig"])

            if layer["rope"] == "a":
                scaling = float(np.float32(HD) ** np.float32(-0.5))
            else:
                scaling = float(np.float32(1.0) / np.float32(np.sqrt(np.float32(HD))))
            rep = 16 // nkv
            av_tail = [None]
            for kvh in range(nkv):
                kl = []
                for slot in range(2):
                    for src, dst in ((kt_hi_sh, kbuf_hi), (kt_lo_sh, kbuf_lo)):
                        t = sched.dma("gpsimd",
                                      dst[:, slot * 1024:slot * 1024 + 1024],
                                      src[slot, kvh * 128:(kvh + 1) * 128, :],
                                      s_kvr,
                                      deps=[t_peer, kvw_tok] + rkbuf.for_write())
                        kl.append(t)
                rkbuf.did_write(kl[-1])
                vl = []
                for ct in range(16):
                    slot = 0 if ct < 8 else 1
                    lt = ct % 8
                    t = sched.dma("gpsimd",
                                  vbuf[:, ct * 256:ct * 256 + 256],
                                  vp_sh[slot, lt * 128:(lt + 1) * 128,
                                        kvh, :],
                                  s_kvv,
                                  deps=[t_peer, kvw_tok] + rvbuf.for_write())
                    vl.append(t)
                rvbuf.did_write(vl[-1])
                t_kv = [kl[-1], vl[-1]]

                def emit_av(ctx):
                    slots_p, rc, colp, t_kv_p, pt_done_p = ctx
                    t_av = None
                    for j, ct in enumerate(slots_p):
                        t_av = mm3(psB[0][:, 0:128], rpsB[0],
                                   pT_hi[:, j * 128:j * 128 + 128],
                                   pT_lo[:, j * 128:j * 128 + 128],
                                   vbuf[:, ct * 256:ct * 256 + 128],
                                   vbuf[:, ct * 256 + 128:ct * 256 + 256],
                                   start=(j == 0), stop=(j == len(slots_p) - 1),
                                   deps=(pt_done_p + t_kv_p) if j == 0 else ())
                    rpT.did_read(t_av)
                    rvbuf.did_read(t_av)
                    t_o = sched.op("vector", lambda V, rc=rc: V.tensor_single_scalar(
                        out=ev32b[:, 0:128], in_=psB[0][:, 0:128],
                        scalar=arow[:, rc:rc + 1], op=ALU.mult),
                        deps=[t_av] + revb.for_write() + revb.for_read())
                    rpsB[0].did_read(t_o)
                    rarow.did_read(t_o)
                    revb.did_write(t_o)

                    def tail(t_o=t_o, colp=colp):
                        bank = psT[0]; rbank = rpsT[0]
                        t_tr = sched.op("tensor", lambda T, bank=bank: T.transpose(
                            bank[:, 0:128], ev32b[:, 0:128], ident[:]),
                            deps=[t_o, t_consts] + rbank.for_write())
                        revb.did_read(t_tr)
                        rbank.did_write(t_tr)
                        t_hi = sched.op("vector", lambda V, colp=colp, bank=bank:
                                        V.tensor_copy(
                            out=featT_hi[:, colp:colp + 128], in_=bank[:, 0:128]),
                            deps=[t_tr] + rfeatT.for_write())
                        t_lo = sched.op("vector", lambda V, colp=colp, bank=bank:
                                        V.tensor_tensor(out=featT_lo[:, colp:colp + 128],
                                                        in0=bank[:, 0:128],
                                                        in1=featT_hi[:, colp:colp + 128],
                                                        op=ALU.subtract), deps=[t_hi])
                        rbank.did_read(t_lo)
                        rfeatT.did_write(t_lo)
                    av_tail[0] = tail

                av_ctx = [None]
                for r in range(rep):
                    h = kvh * rep + r
                    tq1 = sched.dma("gpsimd", qbuf_hi[:], q_spill[0, h, :, :], s_qr,
                                    deps=rqsp.for_read() + rqbuf.for_write())
                    tq2 = sched.dma("gpsimd", qbuf_lo[:], q_spill[1, h, :, :], s_qr,
                                    deps=rqsp.for_read())
                    rqbuf.did_write(tq2)
                    t_q = [tq1, tq2]
                    for i in range(NT):
                        if layer.get("alibi"):
                            groups = [(0, False)] + ([(4, False)] if i >= 4 else []) \
                                + [(8, True)] + ([(12, True)] if i >= 4 else [])
                        else:
                            groups = [(4 * g, False) for g in range(4)]
                        ng = len(groups)
                        w = ng * 512
                        slots = [s0 + j for s0, _ in groups for j in range(4)]
                        mm_last = []
                        mx_toks = []
                        for b, (s0, _) in enumerate(groups):
                            t = mm3(psA[:, b * 512:b * 512 + 512], rpsA[b],
                                    qbuf_hi[:, i * 128:i * 128 + 128],
                                    qbuf_lo[:, i * 128:i * 128 + 128],
                                    kbuf_hi[:, s0 * 128:s0 * 128 + 512],
                                    kbuf_lo[:, s0 * 128:s0 * 128 + 512],
                                    start=True, stop=True,
                                    deps=t_q + t_kv)
                            mm_last.append(t)
                            if not layer.get("alibi"):
                                tmx = sched.op("vector", lambda V, b=b: V.reduce_max(
                                    out=arow[:, 4 + b:5 + b],
                                    in_=psA[:, b * 512:b * 512 + 512], axis=AX.X,
                                    negate=True), deps=[t] + rarow.for_write())
                                rpsA[b].did_read(tmx)
                                rarow.did_write(tmx)
                                mx_toks.append(tmx)
                        rqbuf.did_read(mm_last[-1])
                        rkbuf.did_read(mm_last[-1])
                        if layer.get("alibi"):
                            badd_toks = []
                            bt = []
                            for b, (s0, peer) in enumerate(groups):
                                bk0 = 2 * s0 if not peer else 2 * (s0 - 8) + 1
                                r0 = bk0 - 2 * i + 14
                                t = sched.dma("scalar",
                                              scr2[:, b * 512:b * 512 + 512],
                                              mkap(alibi_in,
                                                   (kvh * 30 + r0) * 16384,
                                                   [[128, 128], [32768, 4], [1, 128]]),
                                              s_alibi,
                                              deps=rscr2.for_write() + rscr2.for_read())
                                rscr2.did_write(t)
                                bt.append(t)
                            for b in range(ng):
                                t_badd = sched.op("vector", lambda V, b=b: V.tensor_tensor(
                                    out=ev32[:, b * 512:b * 512 + 512],
                                    in0=psA[:, b * 512:b * 512 + 512],
                                    in1=scr2[:, b * 512:b * 512 + 512],
                                    op=ALU.add),
                                    deps=[mm_last[b], bt[b]]
                                    + (rev.for_write() + rev.for_read() if b == 0
                                       else []))
                                rpsA[b].did_read(t_badd)
                                rscr2.did_read(t_badd)
                                rev.did_write(t_badd)
                                badd_toks.append(t_badd)
                                tmx = sched.op("vector", lambda V, b=b: V.reduce_max(
                                    out=arow[:, 4 + b:5 + b],
                                    in_=ev32[:, b * 512:b * 512 + 512], axis=AX.X,
                                    negate=True), deps=[t_badd] + rarow.for_write())
                                rev.did_read(tmx)
                                rarow.did_write(tmx)
                                mx_toks.append(tmx)
                            sc_src = ev32[:, 0:w]
                            sc_dep = badd_toks
                        else:
                            sc_src = psA[:, 0:w]
                            sc_dep = mm_last
                        if layer.get("alibi"):
                            t_mx = sched.op("vector", lambda V, ng=ng: V.tensor_reduce(
                                out=arow[:, 0:1], in_=arow[:, 4:4 + ng], op=ALU.min,
                                axis=AX.X), deps=mx_toks + rarow.for_write())
                            rarow.did_write(t_mx)
                        else:
                            t_mx = sched.op("vector", lambda V, ng=ng: V.tensor_reduce(
                                out=arow[:, 0:1], in_=arow[:, 4:4 + ng], op=ALU.min,
                                axis=AX.X), deps=mx_toks + rarow.for_write())
                            rarow.did_write(t_mx)
                        t_bias = sched.op("scalar", lambda Sc: Sc.activation(
                            out=arow[:, 1:2], in_=arow[:, 0:1], func=AF.Copy,
                            scale=scaling), deps=[t_mx])
                        rarow.did_write(t_bias)
                        if av_ctx[0] is not None:
                            emit_av(av_ctx[0])
                            av_ctx[0] = None
                        exp_toks = []
                        for g in range(ng):
                            esrc = (ev32[:, g * 512:g * 512 + 512] if layer.get("alibi")
                                    else psA[:, g * 512:g * 512 + 512])
                            t_e = sched.op("scalar", lambda Sc, g=g, esrc=esrc:
                                           Sc.activation(
                                out=p32[:, g * 512:g * 512 + 512], in_=esrc,
                                func=AF.Exp, bias=arow[:, 1:2], scale=scaling,
                                accum_out=arow[:, 8 + g:9 + g]),
                                deps=[t_bias] + sc_dep
                                + (rev.for_write() + rev.for_read()
                                   if not layer.get("alibi") else []))
                            rpsA[g].did_read(t_e)
                            rev.did_write(t_e)
                            rarow.did_write(t_e)
                            exp_toks.append(t_e)
                        t_exp = exp_toks[-1]
                        t_sum = sched.op("vector", lambda V, ng=ng: V.reduce_sum(
                            out=arow[:, 2:3], in_=arow[:, 8:8 + ng], axis=AX.X),
                            deps=exp_toks)
                        rarow.did_write(t_sum)
                        rc = 3 if (r * NT + i) % 2 == 0 else 12
                        t_rec = sched.op("vector", lambda V, rc=rc: V.reciprocal(
                            out=arow[:, rc:rc + 1], in_=arow[:, 2:3]), deps=[t_sum])
                        rarow.did_write(t_rec)
                        pt_done = []
                        for g in range(ng):
                            bank = psT[g % 2]; rbank = rpsT[g % 2]
                            mm = []
                            for jj in range(4):
                                ct = g * 4 + jj
                                t = sched.op("tensor", lambda T, ct=ct, jj=jj, bank=bank:
                                             T.transpose(bank[:, jj * 128:jj * 128 + 128],
                                                         p32[:, ct * 128:ct * 128 + 128],
                                                         ident[:]),
                                             deps=[exp_toks[g], t_consts] + rbank.for_write())
                                rev.did_read(t)
                                mm.append(t)
                            rbank.did_write(mm[-1])
                            t_hi = sched.op("vector", lambda V, g=g, bank=bank: V.tensor_copy(
                                out=pT_hi[:, g * 512:g * 512 + 512], in_=bank[:]),
                                deps=mm + rpT.for_write())
                            t_lo = sched.op("vector", lambda V, g=g, bank=bank: V.tensor_tensor(
                                out=pT_lo[:, g * 512:g * 512 + 512], in0=bank[:],
                                in1=pT_hi[:, g * 512:g * 512 + 512], op=ALU.subtract),
                                deps=[t_hi])
                            rbank.did_read(t_lo)
                            pt_done.append(t_lo)
                        rpT.did_write(pt_done[-1])
                        if av_tail[0] is not None:
                            av_tail[0]()
                            av_tail[0] = None
                        av_ctx[0] = (slots, rc, h * TPC + i * 128,
                                     list(t_q) + list(t_kv) + [t_rec], pt_done)
                if av_ctx[0] is not None:
                    emit_av(av_ctx[0])
                    av_ctx[0] = None

            if av_tail[0] is not None:
                av_tail[0]()
                av_tail[0] = None

            rd_toks = [(s_kvr, sched.dma_cnt.get(id(s_kvr), 0)),
                       (s_kvv, sched.dma_cnt.get(id(s_kvv), 0))]
            RT["prep"] += 1
            pv2 = RT["prep"]

            def signal_rd(G, pv=pv2):
                ins = G.remote_sem_update_broadcast(
                    kv_done, lsem, rdests=[(0, 1)] + [None] * 7,
                    sim_remote_sem_inc=2)
                ins.then_inc(prep_sem, 1)
                G.wait_ge(prep_sem, pv)
                G.trigger_dma(count=1)

            sched.raw("gpsimd", signal_rd, deps=rd_toks)
            rkv.did_read(rd_toks[0])
            rkv.did_read(rd_toks[1])
            RT["kv_rsig"] += 2

            attn_dep = rfeatT.for_read()
            bo = layer.get("bo")
            if bo is not None:
                t_bo = sched.dma("gpsimd", ev32b[0:1, 0:1024], bo[0:1, 0:1024], s_qs,
                                 deps=revb.for_write() + revb.for_read())
                t_bo2 = sched.dma("gpsimd", ev32b[1:2, 0:1024]
                                  if False else ev32[0:1, 0:1024],
                                  bo[0:1, 1024:2048], s_qs,
                                  deps=rev.for_write() + rev.for_read())
                rev.did_write(t_bo2)
                revb.did_write(t_bo)
            for og in range(4):
                has_bias = bo is not None
                if has_bias:
                    bsrc = ev32b[0:1, og * 512:og * 512 + 512] if og < 2 else \
                        ev32[0:1, (og - 2) * 512:(og - 2) * 512 + 512]
                    bdep = [t_bo, t_bo2]
                    sh, sl = split_piece(evh[0:1, 0:512], evl[0:1, 0:512], bsrc,
                                         deps=bdep + revh.for_write())
                    revh.did_write(sl)
                    (revb if og < 2 else rev).did_read(sl)
                for dc in range(NDC):
                    buf = dc % 2
                    sp = stream_rhs_piece(layer["wo"], dc * 128, og * 512, 512, buf)
                    tm = None
                    for i in range(NT):
                        tm = mm3(banks[i], rbanks[i],
                            featT_hi[:, dc * TPC + i * 128: dc * TPC + i * 128 + 128],
                            featT_lo[:, dc * TPC + i * 128: dc * TPC + i * 128 + 128],
                            whi[buf][:, 0:512], wlo[buf][:, 0:512],
                            start=(dc == 0), stop=(dc == NDC - 1) and not has_bias,
                            deps=[sp] + (attn_dep if dc == 0 else []),
                            mark=(i == NT - 1))
                    rwsp[buf].did_read(tm)
                    rfeatT.did_read(tm)
                if has_bias:
                    for i in range(NT):
                        sched.op("tensor", lambda T, i=i: T.matmul(
                            banks[i], onesrow[0:1, :], evh[0:1, 0:512],
                            start=False, stop=False), deps=[sh, sl, t_consts], inc=False)
                        t_f = sched.op("tensor", lambda T, i=i: T.matmul(
                            banks[i], onesrow[0:1, :], evl[0:1, 0:512],
                            start=False, stop=True), inc=True)
                        rbanks[i].did_write(t_f)
                        revh.did_read(t_f)
                for i in range(NT):
                    t = sched.op("vector", lambda V, i=i, og=og, bank=banks[i]:
                                 V.tensor_tensor(out=xres[i][:, og * 512:og * 512 + 512],
                                                 in0=bank,
                                                 in1=xres[i][:, og * 512:og * 512 + 512],
                                                 op=ALU.add),
                                 deps=rbanks[i].for_read() + [rbanks[i].w] + rxres[i].for_write())
                    rbanks[i].did_read(t)
                    rxres[i].did_write(t)


        rscr2 = res("scr2"); rcc = res("cc")
        rqb2 = rqbuf

        def transpose4(src_ap_fn, nchunks, dst_hi, dst_lo, col0, deps):
            toks = []
            for g in range((nchunks + 3) // 4):
                bank = psT[g % 2]; rbank = rpsT[g % 2]
                mm = []
                for jj in range(min(4, nchunks - g * 4)):
                    j = g * 4 + jj
                    t = sched.op("tensor", lambda T, j=j, jj=jj, bank=bank: T.transpose(
                        bank[:, jj * 4:jj * 4 + 4], src_ap_fn(j), ident[0:4, 0:4]),
                        deps=list(deps) + rbank.for_write() + [t_consts])
                    mm.append(t)
                rbank.did_write(mm[-1])
                for jj in range(min(4, nchunks - g * 4)):
                    j = g * 4 + jj
                    c = col0 + j * 4
                    th = sched.op("vector", lambda V, bank=bank, jj=jj, c=c: V.tensor_copy(
                        out=dst_hi[:, c:c + 4], in_=bank[:, jj * 4:jj * 4 + 4]), deps=mm)
                    tl = sched.op("vector", lambda V, bank=bank, jj=jj, c=c: V.tensor_tensor(
                        out=dst_lo[:, c:c + 4], in0=bank[:, jj * 4:jj * 4 + 4],
                        in1=dst_hi[:, c:c + 4], op=ALU.subtract), deps=[th])
                    rbank.did_read(tl)
                    toks.append(tl)
            return toks[-1]

        def sub_ffn():
            last_feat = build_featT(1e-5)
            featdep = [last_feat]
            par = None

            t_d1b = sched.dma("gpsimd", ev32b[0:4, 0:1024], wi["l0f_d1b"][:, :], s_rot,
                              deps=revb.for_write())
            revb.did_write(t_d1b)

            t_mv1 = sched.dma("gpsimd", ev32[0:1, 0:2048], xres[7][127:128, :], s_rot,
                              deps=rxres[7].for_read() + rev.for_write())
            rxres[7].did_read(t_mv1); rev.did_write(t_mv1)
            t_mv2 = sched.dma("gpsimd", stats[0:1, 36:37], stats[127:128, 7:8], s_rot,
                              deps=rstats.for_read() + rstats.for_write())
            t_mv3 = sched.dma("gpsimd", stats[0:1, 38:39], stats[127:128, 15:16], s_rot)
            rstats.did_write(t_mv3)
            t_row = sched.op("vector", lambda V: V.tensor_scalar(
                out=ev32[0:1, 0:2048], in0=ev32[0:1, 0:2048],
                scalar1=stats[0:1, 36:37], scalar2=stats[0:1, 38:39],
                op0=ALU.add, op1=ALU.mult),
                deps=[t_mv1, t_mv2, t_mv3])
            rstats.did_read(t_row)
            rev.did_write(t_row)
            t1 = sched.op("vector", lambda V: V.reduce_sum(
                out=stats[0:1, 56:57], in_=ev32[0:1, 0:2048], axis=AX.X),
                deps=[t_row] + rstats.for_write())
            rstats.did_write(t1)
            t2 = sched.op("scalar", lambda Sc: Sc.activation(
                out=stats[0:1, 56:57], in_=stats[0:1, 56:57], func=AF.Copy,
                scale=-1.0 / D), deps=[t1])
            t3 = sched.op("scalar", lambda Sc: Sc.activation(
                out=vbuf32[0:1, 0:2048], in_=ev32[0:1, 0:2048], func=AF.Square,
                bias=stats[0:1, 56:57], scale=1.0, accum_out=stats[0:1, 57:58]),
                deps=[t2] + rvbuf.for_write())
            rvbuf.did_write(t3)
            t4 = sched.op("vector", lambda V: V.tensor_scalar(
                out=stats[0:1, 58:59], in0=stats[0:1, 57:58], scalar1=1.0 / D,
                scalar2=1e-6, op0=ALU.mult, op1=ALU.add), deps=[t3])
            t5 = sched.op("scalar", lambda Sc: Sc.activation(
                out=stats[0:1, 59:60], in_=stats[0:1, 58:59], func=AF.Sqrt), deps=[t4])
            t6 = sched.op("vector", lambda V: V.reciprocal(
                out=stats[0:1, 60:61], in_=stats[0:1, 59:60]), deps=[t5])
            t7 = sched.op("vector", lambda V: V.tensor_tensor(
                out=stats[0:1, 61:62], in0=stats[0:1, 58:59], in1=stats[0:1, 60:61],
                op=ALU.mult), deps=[t6, t4])
            t8 = sched.op("vector", lambda V: V.tensor_tensor(
                out=stats[0:1, 61:62], in0=stats[0:1, 61:62], in1=stats[0:1, 59:60],
                op=ALU.add), deps=[t7, t5])
            t9 = sched.op("scalar", lambda Sc: Sc.activation(
                out=stats[0:1, 61:62], in_=stats[0:1, 61:62], func=AF.Copy, scale=0.5),
                deps=[t8])
            t10 = sched.op("vector", lambda V: V.reciprocal(
                out=stats[0:1, 62:63], in_=stats[0:1, 61:62]), deps=[t9])
            t_ing = sched.dma("gpsimd", vbuf32[0:1, 0:1024],
                              wi["l0f_ing"][0:1, 0:1024], s_qs,
                              deps=rvbuf.for_write() + rvbuf.for_read())
            t_ing2 = sched.dma("gpsimd", kbuf_hi32[0:1, 0:1024],
                               wi["l0f_ing"][0:1, 1024:2048], s_qs,
                               deps=rkbuf.for_write())
            t_inb = sched.dma("gpsimd", vbuf32[0:1, 1024:2048],
                              wi["l0f_inb"][0:1, 0:1024], s_qs)
            t_inb2 = sched.dma("gpsimd", kbuf_lo32[0:1, 0:1024],
                               wi["l0f_inb"][0:1, 1024:2048], s_qs)
            rvbuf.did_write(t_inb2)
            rkbuf.did_write(t_inb2)
            t_xnl = sched.op("vector", lambda V: V.tensor_scalar(
                out=ev32[0:1, 0:2048], in0=ev32[0:1, 0:2048],
                scalar1=stats[0:1, 56:57], scalar2=stats[0:1, 62:63],
                op0=ALU.add, op1=ALU.mult), deps=[t10, t3])
            t_xa = sched.op("vector", lambda V: V.tensor_tensor(
                out=ev32[0:1, 0:1024], in0=ev32[0:1, 0:1024], in1=vbuf32[0:1, 0:1024],
                op=ALU.mult), deps=[t_xnl, t_inb2])
            t_xb = sched.op("vector", lambda V: V.tensor_tensor(
                out=ev32[0:1, 1024:2048], in0=ev32[0:1, 1024:2048],
                in1=kbuf_hi32[0:1, 0:1024], op=ALU.mult), deps=[t_xnl, t_inb2])
            t_xa = sched.op("vector", lambda V: V.tensor_tensor(
                out=ev32[0:1, 0:1024], in0=ev32[0:1, 0:1024], in1=vbuf32[0:1, 1024:2048],
                op=ALU.add), deps=[t_xa])
            t_xb = sched.op("vector", lambda V: V.tensor_tensor(
                out=ev32[0:1, 1024:2048], in0=ev32[0:1, 1024:2048],
                in1=kbuf_lo32[0:1, 0:1024], op=ALU.add), deps=[t_xb])
            t_xnl = sched.op("vector", lambda V: V.tensor_copy(
                out=vbuf32[0:1, 0:1], in_=ev32[0:1, 0:1]), deps=[t_xa, t_xb])
            rvbuf.did_read(t_xnl)
            rkbuf.did_read(t_xnl)
            t_xnl = sched.op("vector", lambda V: V.tensor_single_scalar(
                out=ev32[0:1, 0:2048], in_=ev32[0:1, 0:2048],
                scalar=masks_sb[0:1, 0:1], op=ALU.mult), deps=[t_xnl, t_consts])
            rev.did_write(t_xnl)

            xt_toks = []
            for g in range(4):
                bank = psT[g % 2]; rbank = rpsT[g % 2]
                mm = []
                for jj in range(4):
                    j = g * 4 + jj
                    t = sched.op("tensor", lambda T, j=j, jj=jj, bank=bank: T.transpose(
                        bank[:, jj:jj + 1], ev32[0:1, j * 128:(j + 1) * 128],
                        ident[0:1, 0:1]),
                        deps=[t_xnl, t_consts] + rbank.for_write())
                    mm.append(t)
                rbank.did_write(mm[-1])
                for jj in range(4):
                    j = g * 4 + jj
                    th = sched.op("vector", lambda V, bank=bank, jj=jj, j=j: V.tensor_copy(
                        out=qbuf_hi[:, j:j + 1], in_=bank[:, jj:jj + 1]),
                        deps=mm + (rqb2.for_write() if g == 0 and jj == 0 else []))
                    tl = sched.op("vector", lambda V, bank=bank, jj=jj, j=j: V.tensor_tensor(
                        out=qbuf_lo[:, j:j + 1], in0=bank[:, jj:jj + 1],
                        in1=qbuf_hi[:, j:j + 1], op=ALU.subtract), deps=[th])
                    rbank.did_read(tl)
                    xt_toks.append(tl)
            rqb2.did_write(xt_toks[-1])

            for og in range(4):
                for dc in range(NDC):
                    buf = dc % 2
                    sp = stream_rhs_piece(wi["l0f_agg"], dc * 128, og * 512, 512, buf)
                    t_mm = mm3(psA[0:1, og * 512:og * 512 + 512], rpsA[og],
                               qbuf_hi[:, dc:dc + 1], qbuf_lo[:, dc:dc + 1],
                               whi[buf][:, 0:512], wlo[buf][:, 0:512],
                               start=(dc == 0), stop=(dc == NDC - 1),
                               deps=[sp] + (xt_toks[-1:] if dc == 0 else []), mark=True)
                    rwsp[buf].did_read(t_mm)
                    rqb2.did_read(t_mm)
            ev_toks = []
            for og in range(4):
                t = sched.op("vector", lambda V, og=og: V.tensor_copy(
                    out=ev32[0:1, og * 512:og * 512 + 512],
                    in_=psA[0:1, og * 512:og * 512 + 512]),
                    deps=rpsA[og].for_read() + [rpsA[og].w] + rev.for_write())
                rpsA[og].did_read(t)
                ev_toks.append(t)
            rev.did_write(ev_toks[-1])
            th_toks = []
            for og in range(4):
                sh, sl = split_piece(evh[0:1, 0:512], evl[0:1, 0:512],
                                     ev32[0:1, og * 512:og * 512 + 512],
                                     deps=ev_toks + revh.for_write())
                revh.did_write(sl)
                m1 = sched.op("tensor", lambda T, og=og: T.matmul(
                    psB[0][0:4, 0:512], bsel1[0:1, 0:4], evh[0:1, 0:512],
                    start=True, stop=False), deps=[sh, sl, t_consts] + rpsB[0].for_write(),
                    inc=False)
                m2 = sched.op("tensor", lambda T, og=og: T.matmul(
                    psB[0][0:4, 0:512], bsel1[0:1, 0:4], evl[0:1, 0:512],
                    start=False, stop=True), inc=True)
                rpsB[0].did_write(m2)
                revh.did_read(m2)
                t_ab = sched.dma("gpsimd", vbuf32[0:4, 0:512],
                                 wi["l0f_aggb_my"][:, og * 512:og * 512 + 512], s_qs,
                                 deps=rvbuf.for_write())
                rvbuf.did_write(t_ab)
                t_th = sched.op("vector", lambda V, og=og: V.tensor_tensor(
                    out=scr2[0:4, og * 512:og * 512 + 512], in0=psB[0][0:4, 0:512],
                    in1=vbuf32[0:4, 0:512], op=ALU.add),
                    deps=[m2, t_ab] + rscr2.for_write())
                rpsB[0].did_read(t_th)
                rvbuf.did_read(t_th)
                th_toks.append(t_th)
            rscr2.did_write(th_toks[-1])

            def ar_thought(tag, mid=None):
                t_up = sched.dma("gpsimd", cc_in[:, :], scr2[0:4, 0:2048], s_kvw,
                                 deps=rscr2.for_read() + [rscr2.w] + rcc.for_write())
                rcc.did_write(t_up)
                ccs = {}
                def do_cc(G, t_up=t_up):
                    G.collective_compute(
                        "AllReduce", ALU.add, ins=[cc_in[:, :]], outs=[cc_out[:, :]],
                        replica_groups=[list(range(NCORE))],
                    ).then_inc(ccsem, 1)
                RT["cc"] = RT.get("cc", 0) + 1
                sched.raw("gpsimd", do_cc, deps=[t_up])
                if mid is not None:
                    mid()
                t_dn = sched.dma("gpsimd", scr2[0:4, 0:2048], cc_out[:, :], s_kvw,
                                 deps=[(ccsem, RT["cc"])] + rscr2.for_write())
                rscr2.did_write(t_dn)
                rcc.did_read(t_dn)
                return t_dn

            banks = [psA[:, 0:512], psA[:, 512:1024], psA[:, 1024:1536],
                     psA[:, 1536:2048], psB[0][:], psB[1][:], psT[0][:], psT[1][:]]
            rbanks = rpsA + rpsB + rpsT

            def gate_og_mm(og):
                t_gb = sched.dma("gpsimd", kbuf_lo32[0:1, 0:512],
                                 wi["l0f_gateb"][0:1, og * 512:og * 512 + 512],
                                 s_qs, deps=rkbuf.for_write() + rkbuf.for_read())
                rkbuf.did_write(t_gb)
                sh, sl = split_piece(evh[0:1, 0:512], evl[0:1, 0:512],
                                     kbuf_lo32[0:1, 0:512],
                                     deps=[t_gb] + revh.for_write())
                revh.did_write(sl)
                rkbuf.did_read(sl)
                tm = None
                for dc in range(NDC):
                    buf = dc % 2
                    sp = stream_rhs_piece(wi["l0f_gate"], dc * 128, og * 512, 512, buf)
                    for i in range(NT):
                        tm = mm3(banks[i], rbanks[i],
                                 featT_hi[:, dc * TPC + i * 128: dc * TPC + i * 128 + 128],
                                 featT_lo[:, dc * TPC + i * 128: dc * TPC + i * 128 + 128],
                                 whi[buf][:, 0:512], wlo[buf][:, 0:512],
                                 start=(dc == 0), stop=False,
                                 deps=[sp] + (featdep if dc == 0 else []),
                                 mark=(i == NT - 1))
                    rwsp[buf].did_read(tm)
                    rfeatT.did_read(tm)
                for i in range(NT):
                    sched.op("tensor", lambda T, i=i: T.matmul(
                        banks[i], onesrow[0:1, :], evh[0:1, 0:512],
                        start=False, stop=False), deps=[sh, sl, t_consts], inc=False)
                    t_f = sched.op("tensor", lambda T, i=i: T.matmul(
                        banks[i], onesrow[0:1, :], evl[0:1, 0:512],
                        start=False, stop=True), inc=True)
                    rbanks[i].did_write(t_f)
                    revh.did_read(t_f)

            def gate_og_evac(og):
                for i in range(NT):
                    t_s = sched.op("scalar", lambda Sc, i=i:
                                   Sc.activation(out=kbuf_hi32[:, 0:512],
                                                 in_=banks[i], func=AF.Sigmoid),
                                   deps=rbanks[i].for_read() + [rbanks[i].w]
                                   + rkbuf.for_write() + rkbuf.for_read())
                    rbanks[i].did_read(t_s)
                    rkbuf.did_write(t_s)
                    t_sp = sched.dma("gpsimd", g_spill[i, :, og * 512:og * 512 + 512],
                                     kbuf_hi32[:, 0:512], s_qs, deps=[t_s])
                    rkbuf.did_read(t_sp)
                    rgsp.did_write(t_sp)

            t_th = ar_thought("agg", mid=lambda: gate_og_mm(0))

            prev_stat = [t10]
            for step in range(4):
                sc = 40
                t1 = sched.op("vector", lambda V: V.reduce_sum(
                    out=stats[0:4, sc:sc + 1], in_=scr2[0:4, 0:2048], axis=AX.X),
                    deps=[t_th] + prev_stat)
                t2 = sched.op("scalar", lambda Sc: Sc.activation(
                    out=stats[0:4, sc:sc + 1], in_=stats[0:4, sc:sc + 1],
                    func=AF.Copy, scale=-1.0 / D), deps=[t1] + prev_stat)
                t3 = sched.op("scalar", lambda Sc: Sc.activation(
                    out=vbuf32[0:4, 0:2048], in_=scr2[0:4, 0:2048], func=AF.Square,
                    bias=stats[0:4, sc:sc + 1], scale=1.0,
                    accum_out=stats[0:4, sc + 1:sc + 2]),
                    deps=[t2] + rvbuf.for_write())
                rvbuf.did_write(t3)
                t4 = sched.op("vector", lambda V: V.tensor_scalar(
                    out=stats[0:4, sc + 2:sc + 3], in0=stats[0:4, sc + 1:sc + 2],
                    scalar1=1.0 / D, scalar2=1e-6, op0=ALU.mult, op1=ALU.add),
                    deps=[t3])
                t5 = sched.op("scalar", lambda Sc: Sc.activation(
                    out=stats[0:4, sc + 3:sc + 4], in_=stats[0:4, sc + 2:sc + 3],
                    func=AF.Sqrt), deps=[t4])
                t6 = sched.op("vector", lambda V: V.reciprocal(
                    out=stats[0:4, sc + 4:sc + 5], in_=stats[0:4, sc + 3:sc + 4]),
                    deps=[t5])
                t7 = sched.op("vector", lambda V: V.tensor_tensor(
                    out=stats[0:4, sc + 5:sc + 6], in0=stats[0:4, sc + 2:sc + 3],
                    in1=stats[0:4, sc + 4:sc + 5], op=ALU.mult), deps=[t6, t4])
                t8 = sched.op("vector", lambda V: V.tensor_tensor(
                    out=stats[0:4, sc + 5:sc + 6], in0=stats[0:4, sc + 5:sc + 6],
                    in1=stats[0:4, sc + 3:sc + 4], op=ALU.add), deps=[t7, t5])
                t9 = sched.op("scalar", lambda Sc: Sc.activation(
                    out=stats[0:4, sc + 5:sc + 6], in_=stats[0:4, sc + 5:sc + 6],
                    func=AF.Copy, scale=0.5), deps=[t8])
                t10 = sched.op("vector", lambda V: V.reciprocal(
                    out=stats[0:4, sc + 6:sc + 7], in_=stats[0:4, sc + 5:sc + 6]),
                    deps=[t9])
                t_tln = sched.op("vector", lambda V: V.tensor_scalar(
                    out=vbuf32[0:4, 0:2048], in0=scr2[0:4, 0:2048],
                    scalar1=stats[0:4, sc:sc + 1], scalar2=stats[0:4, sc + 6:sc + 7],
                    op0=ALU.add, op1=ALU.mult),
                    deps=[t10, t_th, t3] + rvbuf.for_write() + rvbuf.for_read())
                rvbuf.did_write(t_tln)
                t_g1 = sched.dma("gpsimd", ev32[0:4, 0:2048], wi["l0f_png"][:, :],
                                 s_qs, deps=rev.for_write())
                rev.did_write(t_g1)
                t_tln = sched.op("vector", lambda V: V.tensor_tensor(
                    out=vbuf32[0:4, 0:2048], in0=vbuf32[0:4, 0:2048],
                    in1=ev32[0:4, 0:2048], op=ALU.mult), deps=[t_tln, t_g1])
                rev.did_read(t_tln)
                t_g2 = sched.dma("gpsimd", ev32[0:4, 0:2048], wi["l0f_pnb"][:, :],
                                 s_qs, deps=[t_tln] + rev.for_write())
                rev.did_write(t_g2)
                t_tln = sched.op("vector", lambda V: V.tensor_tensor(
                    out=vbuf32[0:4, 0:2048], in0=vbuf32[0:4, 0:2048],
                    in1=ev32[0:4, 0:2048], op=ALU.add), deps=[t_tln, t_g2])
                rev.did_read(t_tln)
                rvbuf.did_write(t_tln)
                gate_og_evac(step)
                t_tT = transpose4(lambda j: vbuf32[0:4, j * 128:(j + 1) * 128], NDC,
                                  qbuf_hi, qbuf_lo, 16, [t_tln])
                rvbuf.did_read(t_tT)
                rqb2.did_write(t_tT)
                for ng in range(2):
                    for dc in range(NDC):
                        buf = dc % 2
                        sp = stream_rhs_piece(wi["l0f_d1"], dc * 128, ng * 512, 512, buf)
                        t_mm = mm3(psA[0:4, ng * 512:ng * 512 + 512], rpsA[ng],
                                   qbuf_hi[:, 16 + dc * 4:16 + dc * 4 + 4],
                                   qbuf_lo[:, 16 + dc * 4:16 + dc * 4 + 4],
                                   whi[buf][:, 0:512], wlo[buf][:, 0:512],
                                   start=(dc == 0), stop=(dc == NDC - 1),
                                   deps=[sp, t_tT], mark=True)
                        rwsp[buf].did_read(t_mm)
                        rqb2.did_read(t_mm)
                tg_toks = []
                for ng in range(2):
                    t_e = sched.op("vector", lambda V, ng=ng: V.tensor_tensor(
                        out=kbuf_hi32[0:4, ng * 512:ng * 512 + 512],
                        in0=psA[0:4, ng * 512:ng * 512 + 512],
                        in1=ev32b[0:4, ng * 512:ng * 512 + 512], op=ALU.add),
                        deps=rpsA[ng].for_read() + [rpsA[ng].w, t_d1b] + rkbuf.for_write())
                    rpsA[ng].did_read(t_e)
                    revb.did_read(t_e)
                    tg_toks.append(t_e)
                rkbuf.did_write(tg_toks[-1])
                t_gel = sched.op("scalar", lambda Sc: Sc.activation(
                    out=kbuf_hi32[0:4, 0:1024], in_=kbuf_hi32[0:4, 0:1024],
                    func=AF.Gelu), deps=tg_toks)
                rkbuf.did_write(t_gel)
                t_gT = transpose4(lambda j: kbuf_hi32[0:4, j * 128:(j + 1) * 128], 8,
                                  qbuf_hi, qbuf_lo, 80, [t_gel])
                rkbuf.did_read(t_gT)
                rqb2.did_write(t_gT)
                t_d2b = sched.dma("gpsimd", ev32[0:4, 0:2048], wi["l0f_d2b"][:, :],
                                  s_qs, deps=rev.for_write())
                rev.did_write(t_d2b)
                t_d2b = sched.op("vector", lambda V: V.tensor_single_scalar(
                    out=ev32[0:4, 0:2048], in_=ev32[0:4, 0:2048], scalar=0.125,
                    op=ALU.mult), deps=[t_d2b])
                rev.did_write(t_d2b)
                pc_toks = []
                for og in range(4):
                    for dc in range(8):
                        buf = dc % 2
                        sp = stream_rhs_piece(wi["l0f_d2"], dc * 128, og * 512, 512, buf)
                        t_mm = mm3(psB[og % 2][0:4, 0:512], rpsB[og % 2],
                                   qbuf_hi[:, 80 + dc * 4:80 + dc * 4 + 4],
                                   qbuf_lo[:, 80 + dc * 4:80 + dc * 4 + 4],
                                   whi[buf][:, 0:512], wlo[buf][:, 0:512],
                                   start=(dc == 0), stop=(dc == 7),
                                   deps=[sp, t_gT], mark=True)
                        rwsp[buf].did_read(t_mm)
                        rqb2.did_read(t_mm)
                    dsth = pT_hi32 if og < 2 else pT_lo32
                    c0 = (og % 2) * 512
                    t_p = sched.op("vector", lambda V, og=og, dsth=dsth, c0=c0:
                                   V.tensor_tensor(out=dsth[0:4, c0:c0 + 512],
                                                   in0=psB[og % 2][0:4, 0:512],
                                                   in1=ev32[0:4, og * 512:og * 512 + 512],
                                                   op=ALU.add),
                                   deps=rpsB[og % 2].for_read() + [rpsB[og % 2].w, t_d2b]
                                   + (rpT.for_write() if og == 0 else []))
                    rpsB[og % 2].did_read(t_p)
                    rev.did_read(t_p)
                    pc_toks.append(t_p)
                rpT.did_write(pc_toks[-1])
                t_up1 = sched.dma("gpsimd", cc_in[:, 0:1024], pT_hi32[0:4, 0:1024],
                                  s_kvw, deps=pc_toks + rcc.for_write())
                t_up2 = sched.dma("gpsimd", cc_in[:, 1024:2048], pT_lo32[0:4, 0:1024],
                                  s_kvw, deps=pc_toks)
                rcc.did_write(t_up2)
                rpT.did_read(t_up2)
                def do_cc2(G):
                    G.collective_compute(
                        "AllReduce", ALU.add, ins=[cc_in[:, :]], outs=[cc_out[:, :]],
                        replica_groups=[list(range(NCORE))],
                    ).then_inc(ccsem, 1)
                RT["cc"] = RT.get("cc", 0) + 1
                sched.raw("gpsimd", do_cc2, deps=[t_up1, t_up2])
                if step < 3:
                    gate_og_mm(step + 1)
                t_dn = sched.dma("gpsimd", vbuf32[0:4, 0:2048], cc_out[:, :], s_kvw,
                                 deps=[(ccsem, RT["cc"])] + rvbuf.for_write())
                rvbuf.did_write(t_dn)
                rcc.did_read(t_dn)
                t_th = sched.op("vector", lambda V: V.tensor_tensor(
                    out=scr2[0:4, 0:2048], in0=scr2[0:4, 0:2048],
                    in1=vbuf32[0:4, 0:2048], op=ALU.add),
                    deps=[t_dn, t_th] + rscr2.for_write() + rscr2.for_read())
                rscr2.did_write(t_th)
                rvbuf.did_read(t_th)
                prev_stat = [t9, t10, t_tln]

            t_thT = transpose4(lambda j: scr2[0:4, j * 128:(j + 1) * 128], NDC,
                               qbuf_hi, qbuf_lo, 16, [t_th])
            rscr2.did_read(t_thT)
            rqb2.did_write(t_thT)
            for dc in range(NDC):
                buf = dc % 2
                sp = stream_rhs_piece(wi["l0f_bc"], dc * 128, 0, 256, buf)
                t_mm = mm3(psB[0][0:4, 0:256], rpsB[0],
                           qbuf_hi[:, 16 + dc * 4:16 + dc * 4 + 4],
                           qbuf_lo[:, 16 + dc * 4:16 + dc * 4 + 4],
                           whi[buf][:, 0:256], wlo[buf][:, 0:256],
                           start=(dc == 0), stop=(dc == NDC - 1),
                           deps=[sp, t_thT], mark=True)
                rwsp[buf].did_read(t_mm)
                rqb2.did_read(t_mm)
            t_bcb = sched.dma("gpsimd", vbuf32[0:4, 0:256], wi["l0f_bcb"][:, :], s_qs,
                              deps=rvbuf.for_write())
            rvbuf.did_write(t_bcb)
            t_mybc = sched.op("vector", lambda V: V.tensor_tensor(
                out=vbuf32[0:4, 256:512], in0=psB[0][0:4, 0:256],
                in1=vbuf32[0:4, 0:256], op=ALU.add),
                deps=rpsB[0].for_read() + [rpsB[0].w, t_bcb])
            rpsB[0].did_read(t_mybc)
            rvbuf.did_write(t_mybc)
            t_z = sched.op("vector", lambda V: V.memset(ev32[0:4, 0:2048], 0.0),
                           deps=rev.for_write())
            rev.did_write(t_z)
            t_z = sched.dma("gpsimd", cc_in[:, :], ev32[0:4, 0:2048], s_kvw,
                            deps=[t_z] + rcc.for_write())
            t_w = sched.dma("gpsimd",
                            lambda: cc_in[:, ds(colv["v"], 256)],
                            vbuf32[0:4, 256:512], s_kvw, deps=[t_z, t_mybc])
            rcc.did_write(t_w)
            rvbuf.did_read(t_w)
            def do_cc3(G):
                G.collective_compute(
                    "AllReduce", ALU.add, ins=[cc_in[:, :]], outs=[cc_out[:, :]],
                    replica_groups=[list(range(NCORE))],
                ).then_inc(ccsem, 1)
            RT["cc"] = RT.get("cc", 0) + 1
            sched.raw("gpsimd", do_cc3, deps=[t_w])
            t_bc = sched.dma("gpsimd", ev32[0:4, 0:2048], cc_out[:, :], s_kvw,
                             deps=[(ccsem, RT["cc"])] + rev.for_write())
            rev.did_write(t_bc)
            rcc.did_read(t_bc)
            bt_toks = []
            for og in range(4):
                sh, sl = split_piece(evh[0:4, 0:512], evl[0:4, 0:512],
                                     ev32[0:4, og * 512:og * 512 + 512],
                                     deps=[t_bc] + revh.for_write())
                revh.did_write(sl)
                rev.did_read(sl)
                m1 = sched.op("tensor", lambda T: T.matmul(
                    psB[1][:, 0:512], bselr[0:4, 0:128], evh[0:4, 0:512],
                    start=True, stop=False),
                    deps=[sh, sl, t_consts] + rpsB[1].for_write(), inc=False)
                m2 = sched.op("tensor", lambda T: T.matmul(
                    psB[1][:, 0:512], bselr[0:4, 0:128], evl[0:4, 0:512],
                    start=False, stop=True), inc=True)
                rpsB[1].did_write(m2)
                revh.did_read(m2)
                dsth = pT_hi32 if og < 2 else pT_lo32
                c0 = (og % 2) * 512
                t_b = sched.op("vector", lambda V, dsth=dsth, c0=c0: V.tensor_copy(
                    out=dsth[:, c0:c0 + 512], in_=psB[1][:, 0:512]),
                    deps=[m2] + (rpT.for_write() if og == 0 else []))
                rpsB[1].did_read(t_b)
                bt_toks.append(t_b)
            rpT.did_write(bt_toks[-1])


            t_og1 = sched.dma("sync", kbuf_hi32[:, 0:1024], wi["l0f_outg"][:, 0:1024],
                              s_rope, deps=rkbuf.for_write())
            t_og2 = sched.dma("sync", kbuf_lo32[:, 0:1024], wi["l0f_outg"][:, 1024:2048],
                              s_rope)
            t_ob = sched.dma("sync", vbuf32[:, 0:2048], wi["l0f_outb"][:, :], s_rope,
                             deps=rvbuf.for_write() + rvbuf.for_read())
            rkbuf.did_write(t_og2)
            rvbuf.did_write(t_ob)
            prev_ep = prev_stat
            for i in range(NT):
                t_g = sched.dma("gpsimd", scr2[:, 0:2048], g_spill[i, :, :], s_qs,
                                deps=rgsp.for_read() + [rgsp.w]
                                + rscr2.for_write() + rscr2.for_read())
                rgsp.did_read(t_g)
                rscr2.did_write(t_g)
                t_x = sched.op("vector", lambda V, i=i: V.tensor_scalar(
                    out=ev32[:, 0:2048], in0=xres[i][:], scalar1=stats[:, i:i + 1],
                    scalar2=stats[:, 8 + i:9 + i], op0=ALU.add, op1=ALU.mult),
                    deps=rxres[i].for_read() + rstats.for_read() + rev.for_write()
                    + rev.for_read())
                rxres[i].did_read(t_x)
                rev.did_write(t_x)
                for hb, dsth in ((0, pT_hi32), (1, pT_lo32)):
                    t_m = sched.op("vector", lambda V, hb=hb, dsth=dsth:
                                   V.tensor_tensor(out=ev32b[:, 0:1024],
                                                   in0=scr2[:, hb * 1024:hb * 1024 + 1024],
                                                   in1=dsth[:, 0:1024], op=ALU.mult),
                                   deps=[t_g] + rpT.for_read() + [rpT.w]
                                   + revb.for_write() + revb.for_read())
                    rscr2.did_read(t_m)
                    rpT.did_read(t_m)
                    revb.did_write(t_m)
                    t_a = sched.op("vector", lambda V, hb=hb: V.tensor_tensor(
                        out=ev32[:, hb * 1024:hb * 1024 + 1024],
                        in0=ev32[:, hb * 1024:hb * 1024 + 1024],
                        in1=ev32b[:, 0:1024], op=ALU.add), deps=[t_m, t_x])
                    revb.did_read(t_a)
                    rev.did_write(t_a)
                sc = 48
                t1 = sched.op("vector", lambda V: V.reduce_sum(
                    out=stats[:, sc:sc + 1], in_=ev32[:, 0:2048], axis=AX.X),
                    deps=rev.for_read() + [rev.w] + prev_ep)
                t2 = sched.op("scalar", lambda Sc: Sc.activation(
                    out=stats[:, sc:sc + 1], in_=stats[:, sc:sc + 1],
                    func=AF.Copy, scale=-1.0 / D), deps=[t1] + prev_ep)
                t3 = sched.op("scalar", lambda Sc: Sc.activation(
                    out=scr2[:, 0:2048], in_=ev32[:, 0:2048], func=AF.Square,
                    bias=stats[:, sc:sc + 1], scale=1.0,
                    accum_out=stats[:, sc + 1:sc + 2]),
                    deps=[t2] + rscr2.for_write() + rscr2.for_read())
                rscr2.did_write(t3)
                t4 = sched.op("vector", lambda V: V.tensor_scalar(
                    out=stats[:, sc + 2:sc + 3], in0=stats[:, sc + 1:sc + 2],
                    scalar1=1.0 / D, scalar2=1e-6, op0=ALU.mult, op1=ALU.add),
                    deps=[t3])
                t5 = sched.op("scalar", lambda Sc: Sc.activation(
                    out=stats[:, sc + 3:sc + 4], in_=stats[:, sc + 2:sc + 3],
                    func=AF.Sqrt), deps=[t4])
                t6 = sched.op("vector", lambda V: V.reciprocal(
                    out=stats[:, sc + 4:sc + 5], in_=stats[:, sc + 3:sc + 4]),
                    deps=[t5])
                t7 = sched.op("vector", lambda V: V.tensor_tensor(
                    out=stats[:, sc + 5:sc + 6], in0=stats[:, sc + 2:sc + 3],
                    in1=stats[:, sc + 4:sc + 5], op=ALU.mult), deps=[t6, t4])
                t8 = sched.op("vector", lambda V: V.tensor_tensor(
                    out=stats[:, sc + 5:sc + 6], in0=stats[:, sc + 5:sc + 6],
                    in1=stats[:, sc + 3:sc + 4], op=ALU.add), deps=[t7, t5])
                t9 = sched.op("scalar", lambda Sc: Sc.activation(
                    out=stats[:, sc + 5:sc + 6], in_=stats[:, sc + 5:sc + 6],
                    func=AF.Copy, scale=0.5), deps=[t8])
                t10 = sched.op("vector", lambda V: V.reciprocal(
                    out=stats[:, sc + 6:sc + 7], in_=stats[:, sc + 5:sc + 6]),
                    deps=[t9])
                t_ap = sched.op("vector", lambda V: V.tensor_scalar(
                    out=ev32[:, 0:2048], in0=ev32[:, 0:2048],
                    scalar1=stats[:, sc:sc + 1], scalar2=stats[:, sc + 6:sc + 7],
                    op0=ALU.add, op1=ALU.mult), deps=[t10] + rev.for_read() + [rev.w])
                rev.did_write(t_ap)
                for hb in range(2):
                    src_g = kbuf_hi32 if hb == 0 else kbuf_lo32
                    t_ap2 = sched.op("vector", lambda V, hb=hb, src_g=src_g:
                                     V.tensor_tensor(out=ev32[:, hb * 1024:hb * 1024 + 1024],
                                                     in0=ev32[:, hb * 1024:hb * 1024 + 1024],
                                                     in1=src_g[:, 0:1024], op=ALU.mult),
                                     deps=[t_ap, t_og2])
                    rkbuf.did_read(t_ap2)
                    rev.did_write(t_ap2)
                t_ap3 = sched.op("vector", lambda V: V.tensor_tensor(
                    out=ev32[:, 0:2048], in0=ev32[:, 0:2048], in1=vbuf32[:, 0:2048],
                    op=ALU.add), deps=rev.for_read() + [rev.w, t_ob])
                rvbuf.did_read(t_ap3)
                rev.did_write(t_ap3)
                t_r = sched.op("vector", lambda V, i=i: V.tensor_tensor(
                    out=xres[i][:], in0=xres[i][:], in1=ev32[:, 0:2048], op=ALU.add),
                    deps=[t_ap3] + rxres[i].for_write())
                rev.did_read(t_r)
                rxres[i].did_write(t_r)
                prev_ep = [t9, t10, t_ap]
            rstats.did_write(prev_ep[-1])
            rstats.did_write(prev_ep[0])


        def sub_moe():
            last_feat = build_featT(1e-5)
            featdep = [last_feat]
            banks = [psA[:, 0:512], psA[:, 512:1024], psA[:, 1024:1536],
                     psA[:, 1536:2048], psB[0][:], psB[1][:], psT[0][:], psT[1][:]]
            rbanks = rpsA + rpsB + rpsT

            for dc in range(NDC):
                buf = dc % 2
                sp = stream_rhs_piece(wi["l1f_wg"], dc * 128, 0, 8, buf)
                tm = None
                for i in range(NT):
                    tm = mm3(banks[i].tensor[0:128, banks[i].offset % 2048:
                                             banks[i].offset % 2048 + 8]
                             if False else
                             (psA[:, i * 512:i * 512 + 8] if i < 4 else
                              [psB[0][:, 0:8], psB[1][:, 0:8], psT[0][:, 0:8],
                               psT[1][:, 0:8]][i - 4]),
                             rbanks[i],
                             featT_hi[:, dc * TPC + i * 128: dc * TPC + i * 128 + 128],
                             featT_lo[:, dc * TPC + i * 128: dc * TPC + i * 128 + 128],
                             whi[buf][:, 0:8], wlo[buf][:, 0:8],
                             start=(dc == 0), stop=(dc == NDC - 1),
                             deps=[sp] + (featdep if dc == 0 else []),
                             mark=(i == NT - 1))
                rwsp[buf].did_read(tm)
                rfeatT.did_read(tm)
            for i in range(NT):
                bank8 = (psA[:, i * 512:i * 512 + 8] if i < 4 else
                         [psB[0][:, 0:8], psB[1][:, 0:8], psT[0][:, 0:8],
                          psT[1][:, 0:8]][i - 4])
                t_mx = sched.op("vector", lambda V, bank8=bank8: V.reduce_max(
                    out=arow[:, 0:1], in_=bank8, axis=AX.X, negate=True),
                    deps=rbanks[i].for_read() + [rbanks[i].w] + rarow.for_write())
                rbanks[i].did_read(t_mx)
                rarow.did_write(t_mx)
                t_exp = sched.op("scalar", lambda Sc, bank8=bank8: Sc.activation(
                    out=arow[:, 4:12], in_=bank8, func=AF.Exp,
                    bias=arow[:, 0:1], scale=1.0, accum_out=arow[:, 1:2]),
                    deps=[t_mx])
                rbanks[i].did_read(t_exp)
                rarow.did_write(t_exp)
                t_rec = sched.op("vector", lambda V: V.reciprocal(
                    out=arow[:, 2:3], in_=arow[:, 1:2]), deps=[t_exp])
                rarow.did_write(t_rec)
                t_m2 = sched.op("vector", lambda V: V.reduce_max(
                    out=arow[:, 3:4], in_=arow[:, 4:12], axis=AX.X), deps=[t_exp])
                rarow.did_write(t_m2)
                t_eq = sched.op("vector", lambda V: V.tensor_single_scalar(
                    out=arow[:, 4:12], in_=arow[:, 4:12], scalar=arow[:, 3:4],
                    op=ALU.is_ge), deps=[t_m2])
                rarow.did_write(t_eq)
                t_t1 = sched.op("vector", lambda V: V.tensor_tensor(
                    out=arow[:, 4:12], in0=arow[:, 4:12], in1=iota8_sb[:, 8:16],
                    op=ALU.mult), deps=[t_eq, t_consts])
                rarow.did_write(t_t1)
                t_cand = sched.op("vector", lambda V: V.tensor_scalar(
                    out=arow[:, 4:12], in0=arow[:, 4:12], scalar1=-1.0, scalar2=9.0,
                    op0=ALU.mult, op1=ALU.add), deps=[t_t1])
                rarow.did_write(t_cand)
                t_idx = sched.op("vector", lambda V: V.tensor_reduce(
                    out=arow[:, 3:4], in_=arow[:, 4:12], op=ALU.min, axis=AX.X),
                    deps=[t_cand])
                rarow.did_write(t_idx)
                t_oh = sched.op("vector", lambda V: V.tensor_single_scalar(
                    out=arow[:, 4:12], in_=iota8_sb[:, 0:8], scalar=arow[:, 3:4],
                    op=ALU.is_equal), deps=[t_idx])
                rarow.did_write(t_oh)
                t_cb = sched.op("vector", lambda V, i=i: V.tensor_single_scalar(
                    out=moe_c[:, i * 8:i * 8 + 8], in_=arow[:, 4:12],
                    scalar=arow[:, 2:3], op=ALU.mult),
                    deps=[t_oh, t_rec] + rmoec.for_write())
                rarow.did_read(t_cb)
                rmoec.did_write(t_cb)

            for e in range(8):
                t_b = sched.dma("gpsimd", ev32[0:1, 0:2048],
                                wi["l1f_b"][e:e + 1, :], s_qs,
                                deps=rev.for_write() + rev.for_read())
                rev.did_write(t_b)
                for og in range(4):
                    sh, sl = split_piece(evh[0:1, 0:512], evl[0:1, 0:512],
                                         ev32[0:1, og * 512:og * 512 + 512],
                                         deps=[t_b] + revh.for_write())
                    revh.did_write(sl)
                    rev.did_read(sl)
                    for dc in range(NDC):
                        buf = dc % 2
                        sp = stream_rhs_piece(wi["l1f_w"], 0, 0, 512, buf,
                                              flat_base=e * D * D + dc * 128 * D + og * 512,
                                              dout=D)
                        tm = None
                        for i in range(NT):
                            tm = mm3(banks[i], rbanks[i],
                                     featT_hi[:, dc * TPC + i * 128: dc * TPC + i * 128 + 128],
                                     featT_lo[:, dc * TPC + i * 128: dc * TPC + i * 128 + 128],
                                     whi[buf][:, 0:512], wlo[buf][:, 0:512],
                                     start=(dc == 0), stop=False,
                                     deps=[sp] + (featdep if dc == 0 and e == 0 else []),
                                     mark=(i == NT - 1))
                        rwsp[buf].did_read(tm)
                        rfeatT.did_read(tm)
                    for i in range(NT):
                        sched.op("tensor", lambda T, i=i: T.matmul(
                            banks[i], onesrow[0:1, :], evh[0:1, 0:512],
                            start=False, stop=False), deps=[sh, sl, t_consts], inc=False)
                        t_f = sched.op("tensor", lambda T, i=i: T.matmul(
                            banks[i], onesrow[0:1, :], evl[0:1, 0:512],
                            start=False, stop=True), inc=True)
                        rbanks[i].did_write(t_f)
                        revh.did_read(t_f)
                    for i in range(NT):
                        t = sched.op("vector", lambda V, i=i, og=og, e=e, bank=banks[i]:
                                     V.scalar_tensor_tensor(
                                         out=xres[i][:, og * 512:og * 512 + 512],
                                         in0=bank, scalar=moe_c[:, i * 8 + e:i * 8 + e + 1],
                                         op0=ALU.mult,
                                         in1=xres[i][:, og * 512:og * 512 + 512],
                                         op1=ALU.add),
                                     deps=rbanks[i].for_read() + [rbanks[i].w]
                                     + rxres[i].for_write() + rmoec.for_read())
                        rbanks[i].did_read(t)
                        rmoec.did_read(t)
                        rxres[i].did_write(t)


        def sub_attn_fp8(layer):
            last_feat = build_featT(1e-5)
            featdep = [last_feat]
            rscl = res("scl")
            evh8 = evh[:].bitcast(FP8)
            vbufI = vbuf[:].bitcast(I32)
            RSQ = float(np.float32(1.0) / np.float32(np.sqrt(np.float32(HD))))

            t_z = sched.op("vector", lambda V: V.memset(arow[:, 8:16], 0.0),
                           deps=rarow.for_write() + rarow.for_read())
            rarow.did_write(t_z)

            def track_absmax(src_ap, col, deps):
                t1 = sched.op("vector", lambda V: V.reduce_max(
                    out=arow[:, 12:13], in_=src_ap, axis=AX.X,
                    apply_absolute_value=True),
                    deps=list(deps) + rarow.for_write())
                rarow.did_write(t1)
                t2 = sched.op("vector", lambda V, col=col: V.tensor_tensor(
                    out=arow[:, col:col + 1], in0=arow[:, col:col + 1],
                    in1=arow[:, 12:13], op=ALU.max), deps=[t1, t_z])
                rarow.did_write(t2)
                return t2

            def proj_spill(w_ap, region, col, hook=None):
                sp_next = stream_lhsT_slab(w_ap, 0, 0, extra_deps=featdep)
                for oc in range(NDC):
                    buf = oc % 2
                    sp = sp_next
                    mm_last = []
                    for dc in range(NDC):
                        for tg in range(2):
                            t = mm3(psA[:, tg * 512:tg * 512 + 512], rpsA[tg],
                                    whi[buf][:, dc * 128:dc * 128 + 128],
                                    wlo[buf][:, dc * 128:dc * 128 + 128],
                                    featT_hi[:, dc * TPC + tg * 512: dc * TPC + tg * 512 + 512],
                                    featT_lo[:, dc * TPC + tg * 512: dc * TPC + tg * 512 + 512],
                                    start=(dc == 0), stop=(dc == NDC - 1),
                                    deps=([sp] + featdep) if dc == 0 else ())
                            if dc == NDC - 1:
                                mm_last.append(t)
                    rwsp[buf].did_read(mm_last[-1])
                    rfeatT.did_read(mm_last[-1])
                    if oc + 1 < NDC:
                        sp_next = stream_lhsT_slab(w_ap, oc + 1, (oc + 1) % 2,
                                                   extra_deps=featdep)
                    t_ev = []
                    for tg in range(2):
                        t = sched.op("vector", lambda V, tg=tg: V.tensor_copy(
                            out=ev32[:, tg * 512:tg * 512 + 512],
                            in_=psA[:, tg * 512:tg * 512 + 512]),
                            deps=[mm_last[tg]] + rev.for_write() + rev.for_read())
                        rpsA[tg].did_read(t)
                        t_ev.append(t)
                    rev.did_write(t_ev[-1])
                    t_am = track_absmax(ev32[:, 0:1024], col, t_ev)
                    t_sp = sched.dma("gpsimd", qkv_f32[region, oc, :, :],
                                     ev32[:, 0:1024], s_qs, deps=t_ev)
                    rev.did_read(t_sp)
                    rqsp.did_write(t_sp)
                    if hook is not None:
                        hook(oc)

            prev_ccm = [None]

            def finalize_scale(trow, col, recip_col, scale_col):
                t_tr = sched.op("tensor", lambda T, col=col: T.transpose(
                    psT[0][0:1, 0:128], arow[:, col:col + 1], ident[:]),
                    deps=rarow.for_read() + [rarow.w, t_consts] + rpsT[0].for_write())
                rarow.did_read(t_tr)
                rpsT[0].did_write(t_tr)
                t_rm = sched.op("vector", lambda V: V.reduce_max(
                    out=arow[0:1, 13:14], in_=psT[0][0:1, 0:128], axis=AX.X),
                    deps=[t_tr] + rarow.for_write())
                rpsT[0].did_read(t_rm)
                rarow.did_write(t_rm)
                t_up = sched.dma("gpsimd", ccm_in[0:1, :], arow[0:1, 13:14], s_kvw,
                                 deps=[t_rm]
                                 + ([prev_ccm[0]] if prev_ccm[0] is not None else []))
                rarow.did_read(t_up)

                def do_ccm(G):
                    G.collective_compute(
                        "AllReduce", ALU.max, ins=[ccm_in[:, :]], outs=[ccm_out[:, :]],
                        replica_groups=[list(range(NCORE))],
                    ).then_inc(ccsem, 1)
                RT["cc"] = RT.get("cc", 0) + 1
                sched.raw("gpsimd", do_ccm, deps=[t_up])
                t_dn = sched.dma("gpsimd", arow[0:1, 13:14], ccm_out[0:1, :], s_kvw,
                                 deps=[(ccsem, RT["cc"])] + rarow.for_write())
                prev_ccm[0] = t_dn
                rarow.did_write(t_dn)
                t_s = sched.op("vector", lambda V: V.tensor_single_scalar(
                    out=arow[0:1, 14:15], in_=arow[0:1, 13:14],
                    scalar=float(np.float32(1.0) / np.float32(448.0)),
                    op=ALU.mult), deps=[t_dn])
                rarow.did_write(t_s)
                t_r = sched.op("vector", lambda V: V.reciprocal(
                    out=arow[0:1, 15:16], in_=arow[0:1, 14:15]), deps=[t_s])
                rarow.did_write(t_r)
                t_b = sched.dma("gpsimd", bsc[trow:trow + 1, 1:3], arow[0:1, 14:16],
                                s_kvw, deps=[t_r])
                rarow.did_read(t_b)
                bts = []
                for colv_, flat in ((recip_col, trow * 3 + 2),
                                    (scale_col, trow * 3 + 1)):
                    t = sched.dma("gpsimd", scl[:, colv_:colv_ + 1],
                                  mkap(bsc, flat, [[0, 128], [1, 1]]), s_kvr,
                                  deps=[t_b] + rscl.for_write())
                    bts.append(t)
                rscl.did_write(bts[-1])
                return bts[-1]

            kb8 = kbuf_hi[:].bitcast(FP8)

            def mk_quant(region, recip_col, writer, scale_tok, db=False):
                def emit_tile(tt):
                    if db and tt % 2 == 1:
                        big, rbig = ev32, rev
                        f8, rf8 = evh8, revh
                        mg, rmg = ev32b[:, 0:1024], revb
                    else:
                        big, rbig = scr2, rscr2
                        f8, rf8 = kb8[:, 0:1024], rkbuf
                        mg, rmg = kbuf_lo32[:, 0:1024], rkbuf
                    t0 = sched.dma("gpsimd", big[:, 0:1024], qkv_f32[region, tt, :, :],
                                   s_qr, deps=rqsp.for_read() + [rqsp.w]
                                   + rbig.for_write() + rbig.for_read())
                    rqsp.did_read(t0)
                    rbig.did_write(t0)
                    t1 = sched.op("vector", lambda V, big=big: V.tensor_single_scalar(
                        out=big[:, 0:1024], in_=big[:, 0:1024],
                        scalar=scl[:, recip_col:recip_col + 1], op=ALU.mult),
                        deps=[t0, scale_tok])
                    rscl.did_read(t1)
                    rbig.did_write(t1)
                    t2 = sched.op("scalar", lambda Sc, big=big, f8=f8: Sc.activation(
                        out=f8, in_=big[:, 0:1024], func=AF.Copy,
                        scale=0.5), deps=[t1] + rf8.for_write() + rf8.for_read())
                    rf8.did_write(t2)
                    t3 = sched.op("scalar", lambda Sc, big=big, f8=f8: Sc.activation(
                        out=big[:, 1024:2048], in_=f8, func=AF.Copy,
                        scale=2.0), deps=[t2])
                    rbig.did_write(t3)
                    rf8.did_read(t3)
                    t4 = sched.op("vector", lambda V, big=big, mg=mg: V.tensor_scalar(
                        out=mg, in0=big[:, 0:1024], scalar1=24576.0,
                        scalar2=-24576.0, op0=ALU.add, op1=ALU.add),
                        deps=[t1] + rmg.for_write() + rmg.for_read())
                    rmg.did_write(t4)
                    t5 = sched.op("vector", lambda V, big=big: V.tensor_single_scalar(
                        out=vbufI[:, 0:1024], in_=big[:, 0:1024].bitcast(I32),
                        scalar=0x7FFFFFFF, op=ALU.bitwise_and),
                        deps=[t1] + rvbuf.for_write() + rvbuf.for_read())
                    rvbuf.did_write(t5)
                    t6 = sched.op("vector", lambda V: V.tensor_single_scalar(
                        out=vbufI[:, 0:1024], in_=vbuf32[:, 0:1024], scalar=0.03125,
                        op=ALU.is_lt), deps=[t5])
                    rvbuf.did_write(t6)
                    t7 = sched.op("vector", lambda V, big=big, mg=mg: V.copy_predicated(
                        out=big[:, 1024:2048], mask=vbufI[:, 0:1024],
                        data=mg), deps=[t3, t4, t6])
                    rvbuf.did_read(t7)
                    rmg.did_read(t7)
                    rbig.did_write(t7)
                    t8 = sched.op("scalar", lambda Sc, big=big: Sc.activation(
                        out=qbuf_hi[:], in_=big[:, 1024:2048], func=AF.Copy),
                        deps=[t7] + rqbuf.for_write())
                    rbig.did_read(t8)
                    rqbuf.did_write(t8)
                    writer(tt, t8)
                return emit_tile

            def wq8(tt, tok):
                t = sched.dma("gpsimd", q_spill[0, tt, :, :], qbuf_hi[:], s_qs,
                              deps=[tok])
                rqbuf.did_read(t)
                rqsp.did_write(t)

            def wk8(tt, tok):
                t = sched.dma("gpsimd", kt_hi_st[tt * 128:(tt + 1) * 128, :],
                              qbuf_hi[:], s_kvw, deps=[tok])
                rqbuf.did_read(t)
                rkv.did_write(t)

            def wv8(tt, tok):
                i, half = tt // 2, tt % 2
                t = sched.dma("gpsimd",
                              vp_st[i * 128:(i + 1) * 128, half * 8:half * 8 + 8, 0:128],
                              qbuf_hi[:], s_kvw, deps=[tok])
                t2 = sched.dma("gpsimd",
                               vp_st[i * 128:(i + 1) * 128, half * 8:half * 8 + 8, 128:256],
                               qbuf_hi[:], s_kvw, deps=[tok])
                rqbuf.did_read(t)
                rqbuf.did_read(t2)
                rkv.did_write(t2)

            t_prevread = (kv_done, RT["kv_rsig"]) if RT["kv_rsig"] else None
            rkv.did_write(t_prevread) if t_prevread else None

            proj_spill(layer["wq"], 0, 8)
            tok_q = finalize_scale(0, 8, 0, 5)
            qq = mk_quant(0, 0, wq8, tok_q)
            proj_spill(layer["wk"], 1, 9,
                       hook=lambda oc: qq(oc - 3) if oc >= 3 else None)
            tok_k = finalize_scale(1, 9, 1, 6)
            t_m = sched.op("vector", lambda V: V.tensor_tensor(
                out=scl[:, 3:4], in0=scl[:, 5:6], in1=scl[:, 6:7], op=ALU.mult),
                deps=[tok_k])
            t_m = sched.op("vector", lambda V: V.tensor_single_scalar(
                out=scl[:, 3:4], in_=scl[:, 3:4], scalar=RSQ, op=ALU.mult),
                deps=[t_m])
            rscl.did_write(t_m)
            kk = mk_quant(1, 1, wk8, tok_k)
            pending_q = [lambda j=j: qq(j) for j in range(13, 16)] \
                + [lambda j=j: kk(j) for j in range(16)]

            banks = [psA[:, 0:512], psA[:, 512:1024], psA[:, 1024:1536],
                     psA[:, 1536:2048], psB[0][:], psB[1][:], psT[0][:], psT[1][:]]
            rbanks = rpsA + rpsB + rpsT
            for og in range(4):
                for dc in range(NDC):
                    buf = dc % 2
                    sp = stream_rhs_piece(layer["wv"], dc * 128, og * 512, 512, buf)
                    tm = None
                    for i in range(NT):
                        tm = mm3(banks[i], rbanks[i],
                                 featT_hi[:, dc * TPC + i * 128: dc * TPC + i * 128 + 128],
                                 featT_lo[:, dc * TPC + i * 128: dc * TPC + i * 128 + 128],
                                 whi[buf][:, 0:512], wlo[buf][:, 0:512],
                                 start=(dc == 0), stop=(dc == NDC - 1),
                                 deps=[sp], mark=(i == NT - 1))
                    rwsp[buf].did_read(tm)
                    rfeatT.did_read(tm)
                    if dc % 2 == 0 and not (og == 0 and dc < 8) and pending_q:
                        pending_q.pop(0)()
                for i in range(NT):
                    t_e = sched.op("vector", lambda V, i=i: V.tensor_copy(
                        out=ev32b[:, 0:512], in_=banks[i]),
                        deps=rbanks[i].for_read() + [rbanks[i].w]
                        + revb.for_write() + revb.for_read())
                    rbanks[i].did_read(t_e)
                    revb.did_write(t_e)
                    t_am = track_absmax(ev32b[:, 0:512], 10, [t_e])
                    t_sp = sched.dma("gpsimd",
                                     qkv_f32[2, i * 2 + og // 2, :,
                                             (og % 2) * 512:(og % 2) * 512 + 512],
                                     ev32b[:, 0:512], s_qs, deps=[t_e])
                    revb.did_read(t_sp)
                    rqsp.did_write(t_sp)

            while pending_q:
                pending_q.pop(0)()
            tok_v = finalize_scale(2, 10, 2, 4)
            vv = mk_quant(2, 2, wv8, tok_v, db=True)
            for _j in range(16):
                vv(_j)

            kvst_tok = (s_kvw, sched.dma_cnt[id(s_kvw)])
            tb1 = sched.dma("gpsimd", lambda: dyn["kt_hi"][0, :, :],
                            kt_hi_st[:, :], s_kvb, deps=[kvst_tok] + rkv.for_write())
            tb3 = sched.dma("gpsimd", lambda: dyn["vp"][0, :, :, :],
                            vp_st[:, :, :], s_kvb, deps=[kvst_tok])
            rkv.did_write(tb3)
            kvw_tok = tb3
            RT["prep"] += 1
            pv = RT["prep"]

            def signal(G, pv=pv):
                ins = G.remote_sem_update_broadcast(
                    kv_ready, lsem, rdests=[(0, 1)] + [None] * 7,
                    sim_remote_sem_inc=2)
                ins.then_inc(prep_sem, 1)
                G.wait_ge(prep_sem, pv)
                G.trigger_dma(count=1)

            sched.raw("gpsimd", signal, deps=[kvw_tok])
            RT["kv_sig"] += 2
            t_peer = (kv_ready, RT["kv_sig"])

            def emit_av8(ctx):
                rc, colp, deps0, pt_done_p = ctx
                t_av = None
                for ct in range(16):
                    d0 = (pt_done_p + deps0) if ct == 0 else ()
                    d0 = list(d0) + (rpsB[0].for_write() if ct == 0 else [])
                    sched.op("tensor", lambda T, ct=ct: T.matmul(
                        psB[0][:, 0:128],
                        pT_hi[:, ct * 128:ct * 128 + 128],
                        vbuf[:, ct * 256:ct * 256 + 128],
                        start=(ct == 0), stop=False), deps=d0, inc=False)
                    t_av = sched.op("tensor", lambda T, ct=ct: T.matmul(
                        psB[0][:, 0:128],
                        pT_lo[:, ct * 128:ct * 128 + 128],
                        vbuf[:, ct * 256:ct * 256 + 128],
                        start=False, stop=(ct == 15)), inc=(ct == 15))
                rpsB[0].did_write(t_av)
                rpT.did_read(t_av)
                rvbuf.did_read(t_av)
                t_o = sched.op("vector", lambda V, rc=rc: V.tensor_scalar(
                    out=ev32b[:, 0:128], in0=psB[0][:, 0:128],
                    scalar1=arow[:, rc:rc + 1], scalar2=scl[:, 4:5],
                    op0=ALU.mult, op1=ALU.mult),
                    deps=[t_av] + revb.for_write() + revb.for_read())
                rpsB[0].did_read(t_o)
                rarow.did_read(t_o)
                rscl.did_read(t_o)
                revb.did_write(t_o)
                bank = psT[0]; rbank = rpsT[0]
                t_tr2 = sched.op("tensor", lambda T, bank=bank: T.transpose(
                    bank[:, 0:128], ev32b[:, 0:128], ident[:]),
                    deps=[t_o, t_consts] + rbank.for_write())
                revb.did_read(t_tr2)
                rbank.did_write(t_tr2)
                t_hi = sched.op("vector", lambda V, colp=colp, bank=bank: V.tensor_copy(
                    out=featT_hi[:, colp:colp + 128], in_=bank[:, 0:128]),
                    deps=[t_tr2] + rfeatT.for_write())
                t_lo = sched.op("vector", lambda V, colp=colp, bank=bank:
                                V.tensor_tensor(out=featT_lo[:, colp:colp + 128],
                                                in0=bank[:, 0:128],
                                                in1=featT_hi[:, colp:colp + 128],
                                                op=ALU.subtract), deps=[t_hi])
                rbank.did_read(t_lo)
                rfeatT.did_write(t_lo)

            av_ctx8 = [None]
            for kvh in range(16):
                kl = []
                for slot in range(2):
                    t = sched.dma("gpsimd",
                                  kbuf_hi[:, slot * 1024:slot * 1024 + 1024],
                                  kt_hi_sh[slot, kvh * 128:(kvh + 1) * 128, :],
                                  s_kvr, deps=[t_peer, kvw_tok] + rkbuf.for_write())
                    kl.append(t)
                rkbuf.did_write(kl[-1])
                vl = []
                for ct in range(16):
                    slot = 0 if ct < 8 else 1
                    lt = ct % 8
                    t = sched.dma("gpsimd",
                                  vbuf[:, ct * 256:ct * 256 + 128],
                                  vp_sh[slot, lt * 128:(lt + 1) * 128, kvh, 0:128],
                                  s_kvv, deps=[t_peer, kvw_tok] + rvbuf.for_write())
                    vl.append(t)
                rvbuf.did_write(vl[-1])
                t_kv = [kl[-1], vl[-1]]
                tq1 = sched.dma("gpsimd", qbuf_hi[:], q_spill[0, kvh, :, :], s_qr,
                                deps=rqsp.for_read() + [rqsp.w] + rqbuf.for_write())
                rqbuf.did_write(tq1)
                for i in range(NT):
                    mm_last = []
                    for kg in range(4):
                        d0 = [tq1] + t_kv + rpsA[kg].for_write()
                        t = sched.op("tensor", lambda T, kg=kg, i=i: T.matmul(
                            psA[:, kg * 512:kg * 512 + 512],
                            qbuf_hi[:, i * 128:i * 128 + 128],
                            kbuf_hi[:, kg * 512:kg * 512 + 512],
                            start=True, stop=True), deps=d0, inc=True)
                        rpsA[kg].did_write(t)
                        mm_last.append(t)
                    rqbuf.did_read(mm_last[-1])
                    rkbuf.did_read(mm_last[-1])
                    mx_toks = []
                    for kg in range(4):
                        tmx = sched.op("vector", lambda V, kg=kg: V.reduce_max(
                            out=arow[:, 4 + kg:5 + kg],
                            in_=psA[:, kg * 512:kg * 512 + 512], axis=AX.X,
                            negate=True), deps=[mm_last[kg]] + rarow.for_write())
                        rpsA[kg].did_read(tmx)
                        rarow.did_write(tmx)
                        mx_toks.append(tmx)
                    t_mx = sched.op("vector", lambda V: V.tensor_reduce(
                        out=arow[:, 0:1], in_=arow[:, 4:8], op=ALU.min,
                        axis=AX.X), deps=mx_toks)
                    rarow.did_write(t_mx)
                    t_bias = sched.op("vector", lambda V: V.tensor_tensor(
                        out=arow[:, 1:2], in0=arow[:, 0:1], in1=scl[:, 3:4],
                        op=ALU.mult), deps=[t_mx, t_m])
                    rscl.did_read(t_bias)
                    rarow.did_write(t_bias)
                    if av_ctx8[0] is not None:
                        emit_av8(av_ctx8[0])
                        av_ctx8[0] = None
                    exp_toks = []
                    for g in range(4):
                        t_e = sched.op("scalar", lambda Sc, g=g: Sc.activation(
                            out=p32[:, g * 512:g * 512 + 512],
                            in_=psA[:, g * 512:g * 512 + 512], func=AF.Exp,
                            bias=arow[:, 1:2], scale=scl[:, 3:4],
                            accum_out=arow[:, 8 + g:9 + g]),
                            deps=[t_bias] + rev.for_write() + rev.for_read())
                        rpsA[g].did_read(t_e)
                        rscl.did_read(t_e)
                        rev.did_write(t_e)
                        rarow.did_write(t_e)
                        exp_toks.append(t_e)
                    t_exp = exp_toks[-1]
                    t_sum = sched.op("vector", lambda V: V.reduce_sum(
                        out=arow[:, 2:3], in_=arow[:, 8:12], axis=AX.X),
                        deps=exp_toks)
                    rarow.did_write(t_sum)
                    rc = 3 if i % 2 == 0 else 12
                    t_rec = sched.op("vector", lambda V, rc=rc: V.reciprocal(
                        out=arow[:, rc:rc + 1], in_=arow[:, 2:3]), deps=[t_sum])
                    rarow.did_write(t_rec)
                    pt_done = []
                    for g in range(4):
                        bank = psT[g % 2]; rbank = rpsT[g % 2]
                        mm = []
                        for jj in range(4):
                            ct = g * 4 + jj
                            t = sched.op("tensor", lambda T, ct=ct, jj=jj, bank=bank:
                                         T.transpose(bank[:, jj * 128:jj * 128 + 128],
                                                     p32[:, ct * 128:ct * 128 + 128],
                                                     ident[:]),
                                         deps=[exp_toks[g], t_consts] + rbank.for_write())
                            rev.did_read(t)
                            mm.append(t)
                        rbank.did_write(mm[-1])
                        t_hi = sched.op("vector", lambda V, g=g, bank=bank: V.tensor_copy(
                            out=pT_hi[:, g * 512:g * 512 + 512], in_=bank[:]),
                            deps=mm + rpT.for_write())
                        t_lo = sched.op("vector", lambda V, g=g, bank=bank: V.tensor_tensor(
                            out=pT_lo[:, g * 512:g * 512 + 512], in0=bank[:],
                            in1=pT_hi[:, g * 512:g * 512 + 512], op=ALU.subtract),
                            deps=[t_hi])
                        rbank.did_read(t_lo)
                        pt_done.append(t_lo)
                    rpT.did_write(pt_done[-1])
                    av_ctx8[0] = (rc, kvh * TPC + i * 128,
                                  [tq1] + list(t_kv) + [t_rec, t_m], pt_done)
                if av_ctx8[0] is not None:
                    emit_av8(av_ctx8[0])
                    av_ctx8[0] = None

            rd_toks = [(s_kvr, sched.dma_cnt.get(id(s_kvr), 0)),
                       (s_kvv, sched.dma_cnt.get(id(s_kvv), 0))]
            RT["prep"] += 1
            pv2 = RT["prep"]

            def signal_rd(G, pv=pv2):
                ins = G.remote_sem_update_broadcast(
                    kv_done, lsem, rdests=[(0, 1)] + [None] * 7,
                    sim_remote_sem_inc=2)
                ins.then_inc(prep_sem, 1)
                G.wait_ge(prep_sem, pv)
                G.trigger_dma(count=1)

            sched.raw("gpsimd", signal_rd, deps=rd_toks)
            rkv.did_read(rd_toks[0])
            rkv.did_read(rd_toks[1])
            RT["kv_rsig"] += 2

            bo = layer["bo"]
            t_bo = sched.dma("gpsimd", ev32b[0:1, 0:1024], bo[0:1, 0:1024], s_qs,
                             deps=revb.for_write() + revb.for_read())
            t_bo2 = sched.dma("gpsimd", ev32[0:1, 0:1024], bo[0:1, 1024:2048], s_qs,
                              deps=rev.for_write() + rev.for_read())
            rev.did_write(t_bo2)
            revb.did_write(t_bo)
            attn_dep = rfeatT.for_read() + [rfeatT.w]
            for og in range(4):
                bsrc = ev32b[0:1, og * 512:og * 512 + 512] if og < 2 else \
                    ev32[0:1, (og - 2) * 512:(og - 2) * 512 + 512]
                sh, sl = split_piece(evh[0:1, 0:512], evl[0:1, 0:512], bsrc,
                                     deps=[t_bo, t_bo2] + revh.for_write())
                revh.did_write(sl)
                (revb if og < 2 else rev).did_read(sl)
                for dc in range(NDC):
                    buf = dc % 2
                    sp = stream_rhs_piece(layer["wo"], dc * 128, og * 512, 512, buf)
                    tm = None
                    for i in range(NT):
                        tm = mm3(banks[i], rbanks[i],
                                 featT_hi[:, dc * TPC + i * 128: dc * TPC + i * 128 + 128],
                                 featT_lo[:, dc * TPC + i * 128: dc * TPC + i * 128 + 128],
                                 whi[buf][:, 0:512], wlo[buf][:, 0:512],
                                 start=(dc == 0), stop=False,
                                 deps=[sp] + (attn_dep if dc == 0 else []),
                                 mark=(i == NT - 1))
                    rwsp[buf].did_read(tm)
                    rfeatT.did_read(tm)
                for i in range(NT):
                    sched.op("tensor", lambda T, i=i: T.matmul(
                        banks[i], onesrow[0:1, :], evh[0:1, 0:512],
                        start=False, stop=False), deps=[sh, sl, t_consts], inc=False)
                    t_f = sched.op("tensor", lambda T, i=i: T.matmul(
                        banks[i], onesrow[0:1, :], evl[0:1, 0:512],
                        start=False, stop=True), inc=True)
                    rbanks[i].did_write(t_f)
                    revh.did_read(t_f)
                for i in range(NT):
                    t = sched.op("vector", lambda V, i=i, og=og, bank=banks[i]:
                                 V.tensor_tensor(out=xres[i][:, og * 512:og * 512 + 512],
                                                 in0=bank,
                                                 in1=xres[i][:, og * 512:og * 512 + 512],
                                                 op=ALU.add),
                                 deps=rbanks[i].for_read() + [rbanks[i].w]
                                 + rxres[i].for_write())
                    rbanks[i].did_read(t)
                    rxres[i].did_write(t)

        L0A = {"wq": wi["l0a_q"], "wk": wi["l0a_k"], "wv": wi["l0a_v"],
               "wo": wi["l0a_o"], "nkv": 4, "rope": "a"}
        L0B = {"wq": wi["l0b_q"], "wk": wi["l0b_k"], "wv": wi["l0b_v"],
               "wo": wi["l0b_o"], "nkv": 4, "rope": "b"}
        if n_sub >= 1:
            sub_attn(L0A)
        if n_sub >= 2:
            sub_ffn()
        if n_sub >= 3:
            sub_attn(L0B)
        if n_sub >= 4:
            sub_ffn()
        L1A = {"wq": wi["l1a_q"], "wk": wi["l1a_k"], "wv": wi["l1a_v"],
               "wo": wi["l1a_o"], "nkv": 16, "rope": None, "alibi": True,
               "bo": wi["l1a_bo"]}
        if n_sub >= 5:
            sub_attn(L1A)
        if n_sub >= 6:
            sub_moe()
        L1B = {"wq": wi["l1b_qkv_q"], "wk": wi["l1b_qkv_k"], "wv": wi["l1b_qkv_v"],
               "wo": wi["l1b_o"], "bo": wi["l1b_bo"]}
        if n_sub >= 7:
            sub_attn_fp8(L1B)
        if n_sub >= 8:
            sub_moe()

        bar = sched.barrier()
        st = [sched.dma("sync", out_h[i * 128:(i + 1) * 128, :], xres[i][:], s_out,
                        deps=bar) for i in range(NT)]
        sched.raw("sync", lambda E: E.wait_ge(s_out, 16 * NT), deps=st)

        with nc.Block() as block:
            sched.finalize(block)
    nc.compile()
    return nc


def _prep_inputs(inputs):
    cos, sin = rope_tables()
    sin_a = sin.copy()
    sin_a[:, 0:64] = -sin_a[:, 0:64]
    cosT = np.ascontiguousarray(cos.T)
    sinaT = np.ascontiguousarray(sin_a.T)
    sinbT = np.ascontiguousarray(sin.T)

    ident = np.eye(128, dtype=np.float32)
    iota8 = np.zeros((128, 16), dtype=np.float32)
    iota8[:, 0:8] = np.arange(8, dtype=np.float32)[None, :]
    iota8[:, 8:16] = 9.0 - np.arange(8, dtype=np.float32)[None, :]

    f32 = lambda k: np.asarray(inputs[k], dtype=np.float32)
    rep = lambda v, n: np.ascontiguousarray(np.broadcast_to(
        np.asarray(v, dtype=np.float32)[None, :], (n, len(v))))
    shared = {
        "ident": ident,
        "iota8": iota8,
        "ones4": np.ones((1, 4), dtype=BF),
        "onesrow": np.ones((1, 128), dtype=BF),
        "l0a_q": lhsT_layout(np.ascontiguousarray(f32("L0_attn_a_Wq").T)),
        "l0a_k": lhsT_layout(np.ascontiguousarray(f32("L0_attn_a_Wk").T)),
        "l0a_v": np.ascontiguousarray(f32("L0_attn_a_Wv").T),
        "l0a_o": np.ascontiguousarray(f32("L0_attn_a_Wo").T),
        "l0b_q": lhsT_layout(f32("L0_attn_b_Wq")),
        "l0b_k": lhsT_layout(f32("L0_attn_b_Wk")),
        "l0b_v": f32("L0_attn_b_Wv"),
        "l0b_o": f32("L0_attn_b_Wo"),
        "l0f_ing": rep(f32("L0_ffn_in_gamma"), 128),
        "l0f_inb": rep(f32("L0_ffn_in_beta"), 128),
        "l0f_png": rep(f32("L0_ffn_pn_gamma"), 4),
        "l0f_pnb": rep(f32("L0_ffn_pn_beta"), 4),
        "l0f_outg": rep(f32("L0_ffn_out_gamma"), 128),
        "l0f_outb": rep(f32("L0_ffn_out_beta"), 128),
        "l0f_agg": f32("L0_ffn_agg_w"),
        "l0f_d2b": rep(f32("L0_ffn_d2_b"), 4),
        "l0f_gate": f32("L0_ffn_gate_w"),
        "l0f_gateb": f32("L0_ffn_gate_b")[None, :],
        "l1a_q": lhsT_layout(f32("L1_attn_a_Wq")),
        "l1a_k": lhsT_layout(f32("L1_attn_a_Wk")),
        "l1a_v": f32("L1_attn_a_Wv"),
        "l1a_o": f32("L1_attn_a_Wo"),
        "l1a_bo": f32("L1_attn_a_bo")[None, :],
        "l1f_wg": f32("L1_ffn_wg"),
        "l1f_w": np.ascontiguousarray(f32("L1_ffn_W").transpose(0, 2, 1)),
        "l1f_b": f32("L1_ffn_b"),
        "l1b_qkv_q": lhsT_layout(np.ascontiguousarray(f32("L1_attn_b_W_qkv")[:, 0:2048])),
        "l1b_qkv_k": lhsT_layout(np.ascontiguousarray(f32("L1_attn_b_W_qkv")[:, 2048:4096])),
        "l1b_qkv_v": np.ascontiguousarray(f32("L1_attn_b_W_qkv")[:, 4096:6144]),
        "l1b_o": f32("L1_attn_b_W_out"),
        "l1b_bo": f32("L1_attn_b_b_out")[None, :],
    }

    x = f32("L0_attn_a_x")

    in_maps = []
    for c in range(NCORE):
        b, h = c // 2, c % 2
        blocks = [h + 2 * i for i in range(NT)]
        rows = np.concatenate([np.arange(w * 128, w * 128 + 128) for w in blocks])
        m = dict(shared)
        m["x"] = np.ascontiguousarray(x[b].reshape(S, D)[rows])
        m["coreinfo"] = np.array([[h, 1 - h, b, c, 0, 0, 0, 0]], dtype=np.int32)
        m["ropeC"] = np.ascontiguousarray(cosT[:, rows])
        m["ropeSa"] = np.ascontiguousarray(sinaT[:, rows])
        m["ropeSb"] = np.ascontiguousarray(sinbT[:, rows])
        m["alibi"] = alibi_core_table(h)
        m["masks"] = np.array([[1.0 if h == 1 else 0.0] + [0.0] * 7], dtype=np.float32)
        m["coreinfo"][0, 4] = c * 256
        bsel = np.zeros((4, 128), dtype=BF)
        bsel[b, :] = BF(1.0)
        m["bsel"] = bsel
        bsel1 = np.zeros((1, 4), dtype=BF)
        bsel1[0, b] = BF(1.0)
        m["bsel1"] = bsel1
        aggb_my = np.zeros((4, D), dtype=np.float32)
        if h == 1:
            aggb_my[b, :] = f32("L0_ffn_agg_b")
        m["l0f_aggb_my"] = aggb_my
        m["l0f_d1"] = np.ascontiguousarray(f32("L0_ffn_d1_w")[:, c * 1024:(c + 1) * 1024])
        m["l0f_d1b"] = rep(f32("L0_ffn_d1_b")[c * 1024:(c + 1) * 1024], 4)
        m["l0f_d2"] = np.ascontiguousarray(f32("L0_ffn_d2_w")[c * 1024:(c + 1) * 1024, :])
        m["l0f_bc"] = np.ascontiguousarray(f32("L0_ffn_bc_w")[:, c * 256:(c + 1) * 256])
        m["l0f_bcb"] = rep(f32("L0_ffn_bc_b")[c * 256:(c + 1) * 256], 4)
        for name, (shape, dt) in INPUT_DECLS.items():
            if name not in m:
                m[name] = np.zeros(shape, dtype=mybir.dt.np(dt))
        in_maps.append(m)
    return in_maps


def _scatter_out(results):
    out = np.zeros((B, S, D), dtype=np.float32)
    for c in range(NCORE):
        b, h = c // 2, c % 2
        blocks = [h + 2 * i for i in range(NT)]
        rows = np.concatenate([np.arange(w * 128, w * 128 + 128) for w in blocks])
        out[b].reshape(S, D)[rows] = results[c]["out"]
    return out


def kernel(**inputs):
    nc = build_program(N_SUB)
    in_maps = _prep_inputs(inputs)
    res = run_bass_kernel_spmd(nc, in_maps, core_ids=list(range(NCORE)))
    return _scatter_out(res.results)
